# version 04
# baseline (speedup 1.0000x reference)
import contextlib
import numpy as np
import concourse.bass as bass
import concourse.mybir as mybir
from concourse.bass_utils import run_bass_kernel_spmd

F32 = mybir.dt.float32
BF16 = mybir.dt.bfloat16
U8 = mybir.dt.uint8
Alu = mybir.AluOpType
Act = mybir.ActivationFunctionType
AX = mybir.AxisListType.X

MG = 12582912.0
D = 1024
FC = 8
TOK = 4096
T = 512
NT = TOK // T
NG = T // 32
ENGS = ("sync", "scalar", "vector", "tensor", "gpsimd")

C_MG0, C_MU0, C_MWD0 = 0, 1, 2
C_IMG0, C_IMU0, C_IMWD0 = 3, 4, 5
C_G1, C_G2, C_G3, C_G4, C_G5 = 6, 7, 8, 9, 10
C_S1, C_S2, C_S3, C_S4, C_S5 = 11, 12, 13, 14, 15
C_K1, C_K2, C_K3, C_K4, C_K5 = 16, 17, 18, 19, 20
C_NS3MG = 21
C_T0, C_T1 = 24, 25
C_EPS = 26
C_SUM = 32


class B:
    def __init__(self, nc, es):
        self.nc, self.es = nc, es
        self.ops = {e: [] for e in ENGS}
        self.sem, self.cnt = {}, {}

    def mksem(self, name):
        self.sem[name] = self.es.enter_context(self.nc.semaphore(name))
        self.cnt[name] = 0

    def wait(self, eng, semname, val):
        s = self.sem[semname]
        self.ops[eng].append(lambda e, s=s, v=val: e.wait_ge(s, v))

    def put(self, eng, make, sem=None, n=1):
        if sem is None:
            self.ops[eng].append(make)
            return None
        self.cnt[sem] += n
        s = self.sem[sem]
        self.ops[eng].append(lambda e, make=make, s=s, n=n: make(e).then_inc(s, n))
        return self.cnt[sem]

    def val(self, s):
        return self.cnt[s]


def build_program(n_cores=8, phases=('prep', 'l0a', 'l0f', 'l1a', 'l1f'), dbg=False):
    nc = bass.Bass("TRN2", target_bir_lowering=False)

    din = lambda n, s: nc.dram_tensor(n, s, F32, kind="ExternalInput").ap()
    xT = din("xT", [D, TOK])
    yT = nc.dram_tensor("yT", [D, TOK], F32, kind="ExternalOutput").ap()
    qkvaT = din("qkvaT", [D, 3 * D])
    gamma = din("gamma", [1, D])
    l0g, l0u, l0d = din("l0g", [D, 4096]), din("l0u", [D, 4096]), din("l0d", [4096, D])
    wzT, bzd = din("wzT", [D, D]), din("bz", [1, D])
    whT, bhd = din("whT", [D, D]), din("bh", [1, D])
    wgn, bgd = din("wgn", [D, D]), din("bg", [1, D])
    l1g, l1u, l1d = din("l1g", [3072, D]), din("l1u", [3072, D]), din("l1d", [D, 3072])

    scr = lambda n, s, dt=BF16: nc.dram_tensor(n, s, dt).ap()
    qkva_b = scr("qkva_b", [D, 3 * D])
    wz_b, wh_b, wg_b = scr("wz_b", [D, D]), scr("wh_b", [D, D]), scr("wg_b", [D, D])
    l0g_q, l0u_q, l0d_q = scr("l0g_q", [D, 4096]), scr("l0u_q", [D, 4096]), scr("l0d_q", [4096, D])
    l1g_b, l1u_b, l1d_b = scr("l1g_b", [3072, D]), scr("l1u_b", [3072, D]), scr("l1d_b", [D, 3072])
    Vbuf = (nc.dram_tensor("Vbuf", [TOK, 4096], BF16, kind="ExternalOutput").ap()
            if dbg else scr("Vbuf", [TOK, 4096]))
    scrd = ((lambda n, s: nc.dram_tensor(n, s, BF16, kind="ExternalOutput").ap())
            if dbg else scr)
    Ubuf, Gbuf = scrd("Ubuf", [3072, TOK]), scrd("Gbuf", [3072, TOK])
    GSbuf, Pbuf = scrd("GSbuf", [3072, TOK]), scrd("Pbuf", [3072, TOK])
    dbgf = nc.dram_tensor("dbgf", [128, 8, 512], F32, kind="ExternalOutput").ap() if dbg else None
    dbgb = nc.dram_tensor("dbgb", [128, 8, 512], BF16, kind="ExternalOutput").ap() if dbg else None
    x3T = nc.dram_tensor("x3T", [D, TOK], F32, kind="ExternalOutput").ap() if dbg else None
    cc_i = [scr(f"cc_i{i}", [1, 2], F32) for i in range(4)]
    cc_o = [scr(f"cc_o{i}", [1, 2], F32) for i in range(4)]

    es = contextlib.ExitStack()
    with es:
        sbt = lambda name, shape, dt: es.enter_context(nc.sbuf_tensor(name, shape, dt))
        xres = sbt("xres", [128, FC, TOK], F32)
        arena = sbt("arena", [128, 54272], U8)
        f0 = sbt("f0", [128, T], F32)
        f1 = sbt("f1", [128, T], F32)
        f2 = sbt("f2", [128, T], F32)
        bc0 = sbt("bc0", [128, T], F32)
        bc1 = sbt("bc1", [128, T], F32)
        sp0 = sbt("sp0", [128, T], BF16)
        sp1 = sbt("sp1", [128, T], BF16)
        s32 = sbt("s32", [32, 64], F32)
        tin = sbt("tin", [128, 32], F32)
        carry = sbt("carry", [128, FC], F32)
        cols = sbt("cols", [128, 32], F32)
        cols2 = sbt("cols2", [128, 36], F32)
        cols3 = sbt("cols3", [128, 32], F32)
        cols4 = sbt("cols4", [128, 32], F32)
        cells = sbt("cells", [1, 64], F32)
        ccol = sbt("ccol", [128, 8], F32)
        gcol = sbt("gcol", [128, FC], F32)
        bzc = sbt("bzc", [128, FC], F32)
        bhc = sbt("bhc", [128, FC], F32)
        bgc = sbt("bgc", [128, FC], F32)
        onesb = sbt("onesb", [128, 1], BF16)
        onef = sbt("onef", [1, 1], F32)
        onecol = sbt("onecol", [128, 1], F32)
        eyef = sbt("eyef", [128, 128], F32)
        zcol = sbt("zcol", [128, 1], F32)
        mgcol = sbt("mgcol", [128, 1], F32)
        rowsA = sbt("rowsA", [1, 6, T], F32)
        PS = es.enter_context(nc.psum_tensor("psum", [128, 8, 512], F32))

        def AR(off, shape, dt):
            n = int(np.prod(shape)) * (2 if dt == BF16 else 4)
            v = arena[:, off:off + n].bitcast(dt)
            if len(shape) == 1:
                return v
            pat = " ".join(f"d{i}" for i in range(len(shape)))
            return v.rearrange(f"p ({pat}) -> p {pat}",
                               **{f"d{i}": s for i, s in enumerate(shape)})

        b = B(nc, es)
        for s in ("dS", "dA", "dG", "mm", "vv", "aa", "gg", "bar"):
            b.mksem(s)

        def dma(q, out, in_, **kw):
            sem = {"sync": "dS", "scalar": "dA", "gpsimd": "dG"}[q]
            return b.put(q, lambda e, o=out, i=in_, kw=kw: e.dma_start(out=o, in_=i, **kw),
                         sem, 16)

        def dma_tr(q, out, in_):
            sem = {"sync": "dS", "scalar": "dA", "gpsimd": "dG"}[q]
            return b.put(q, lambda e, o=out, i=in_: e.dma_start_transpose(o, i), sem, 16)

        V = lambda make, sem="vv": b.put("vector", make, sem)
        A = lambda make, sem="aa": b.put("scalar", make, sem)
        Tn = lambda make, sem=None: b.put("tensor", make, sem)
        VW = lambda: b.wait("vector", "vv", b.val("vv"))

        def barrier():
            tgt = b.val("bar") + 4
            b.put("vector", lambda e: e.memset(f2[0:1, 0:1], 0.0), "bar")
            b.put("scalar", lambda e: e.copy(f2[0:1, 1:2], f2[0:1, 1:2]), "bar")
            b.put("tensor", lambda e: e.matmul(PS[0:1, 7, 0:1], onesb[0:1, 0:1],
                                               onesb[0:1, 0:1], start=True, stop=True), "bar")
            b.put("gpsimd", lambda e: e.memset(f2[0:1, 2:3], 0.0), "bar")
            marks = {q: b.val(q) for q in ("dS", "dA", "dG")}
            for e in ENGS:
                b.wait(e, "bar", tgt)
                for q, v in marks.items():
                    b.wait(e, q, v)

        def fold128_to_cell(col, cell_idx, op):
            b.wait("gpsimd", "vv", b.val("vv"))
            t = dma("gpsimd", f2[0:64, 0:1], col[64:128, 0:1])
            b.wait("vector", "dG", t)
            V(lambda e: e.tensor_tensor(f2[0:64, 1:2], col[0:64, 0:1], f2[0:64, 0:1], op))
            b.wait("gpsimd", "vv", b.val("vv"))
            t = dma("gpsimd", s32[0:32, 0:1], f2[32:64, 1:2])
            b.wait("vector", "dG", t)
            V(lambda e: e.tensor_tensor(s32[0:32, 32:33], f2[0:32, 1:2], s32[0:32, 0:1], op))
            VW()
            V(lambda e: e.transpose(s32[0:32, 0:32], s32[0:32, 32:64]))
            VW()
            V(lambda e: e.tensor_reduce(cells[0:1, cell_idx:cell_idx + 1],
                                        s32[0:1, 0:32], AX, op))
            VW()

        def cell_to_col(cell_idx, col_out):
            return dma("gpsimd", col_out,
                       cells[0:1, cell_idx:cell_idx + 1].unsqueeze(1).broadcast_to([1, 128, 1]),
                       allow_slow_non_contiguous=True)

        def bcast_row(row, tile_out, n=T):
            return dma("gpsimd", tile_out, row.unsqueeze(1).broadcast_to([1, 128, n]))

        def row_pieces_to_tin(row, base):
            last = None
            for tg in range(4):
                for i in range(4):
                    p = 32 * i + base + tg
                    last = dma("gpsimd", tin[p:p + 1, 0:32],
                               row[0:1, tg * 128 + 32 * i:tg * 128 + 32 * i + 32])
            return last

        def tin_rowslices_from_tr(tr, col_base, row_out):
            last = None
            for tg in range(4):
                for i in range(4):
                    p = 32 * i + col_base + tg
                    last = dma("gpsimd", row_out[0:1, tg * 128 + 32 * i:tg * 128 + 32 * i + 32],
                               tr[p:p + 1, 0:32])
            return last

        R_S1, R_S2, R_M, R_R, R_C, R_T = range(6)
        R_MX, R_MN = R_S1, R_S2
        rw = lambda i: rowsA[0:1, i, :]

        for c4 in range(4):
            dma("sync", xres[:, 2 * c4:2 * c4 + 2, :],
                xT.rearrange("(c p) t -> p c t", p=128)[:, 2 * c4:2 * c4 + 2, :])
        dma("sync", gcol[:], gamma.rearrange("o (c p) -> p c", p=128), allow_slow_non_contiguous=True)
        dma("sync", bzc[:], bzd.rearrange("o (c p) -> p c", p=128), allow_slow_non_contiguous=True)
        dma("sync", bhc[:], bhd.rearrange("o (c p) -> p c", p=128), allow_slow_non_contiguous=True)
        dma("sync", bgc[:], bgd.rearrange("o (c p) -> p c", p=128), allow_slow_non_contiguous=True)
        V(lambda e: e.memset(onesb[:], 1.0), None)
        V(lambda e: e.memset(onef[:], 1.0), None)
        V(lambda e: e.memset(onecol[:], 1.0), None)
        V(lambda e: e.memset(zcol[:], 0.0), None)
        V(lambda e: e.memset(mgcol[:], MG), None)
        V(lambda e: e.memset(cells[:], 0.0), None)
        V(lambda e: e.memset(s32[:], 0.0), None)
        V(lambda e: e.memset(cells[0:1, C_EPS:C_EPS + 1], 1e-5), None)


        stg = AR(0, (4096,), F32)
        stgb = AR(16384, (4096,), BF16)

        def prep_cast(src, dst, ncols):
            for c in range(src.shape[0] // 128):
                b.wait("sync", "aa", b.val("aa"))
                t1 = dma("sync", stg[0:128, 0:ncols], src[c * 128:(c + 1) * 128, 0:ncols])
                b.wait("scalar", "dS", t1)
                b.wait("scalar", "dA", b.val("dA"))
                A(lambda e, n=ncols: e.copy(stgb[0:128, 0:n], stg[0:128, 0:n]))
                b.wait("scalar", "aa", b.val("aa"))
                dma("scalar", dst[c * 128:(c + 1) * 128, 0:ncols], stgb[0:128, 0:ncols])

        _DO = lambda p: p in phases
        if _DO('prep'):
            prep_cast(qkvaT, qkva_b, 3 * D)
            prep_cast(wzT, wz_b, D)
            prep_cast(whT, wh_b, D)
            prep_cast(wgn, wg_b, D)

        def prep_ternary(src, dst, ci, ncols):
            nch = src.shape[0] // 128
            V(lambda e: e.memset(cols[:, 0:1], 0.0))
            for c in range(nch):
                b.wait("sync", "vv", b.val("vv"))
                b.wait("sync", "aa", b.val("aa"))
                t1 = dma("sync", stg[0:128, 0:ncols], src[c * 128:(c + 1) * 128, 0:ncols])
                b.wait("vector", "dS", t1)
                V(lambda e, n=ncols: e.tensor_reduce(cols[:, 1:2], stg[0:128, 0:n], AX,
                                                     Alu.add, apply_absolute_value=True))
                VW()
                V(lambda e: e.tensor_tensor(cols[:, 0:1], cols[:, 0:1], cols[:, 1:2],
                                            Alu.add))
            fold128_to_cell(cols[:, 0:1], C_SUM + ci, Alu.add)
            Nw = float(nch * 128 * ncols)
            V(lambda e, Nw=Nw, ci=ci: e.tensor_scalar(
                cells[0:1, ci:ci + 1], cells[0:1, C_SUM + ci:C_SUM + ci + 1],
                1.0 / Nw, 1e-5, Alu.mult, Alu.max))
            VW()
            V(lambda e, ci=ci: e.reciprocal(cells[0:1, C_IMG0 + ci:C_IMG0 + ci + 1],
                                            cells[0:1, ci:ci + 1]))
            b.wait("gpsimd", "vv", b.val("vv"))
            t = cell_to_col(C_IMG0 + ci, ccol[:, ci:ci + 1])
            b.wait("vector", "dG", t)
            for c in range(nch):
                b.wait("sync", "vv", b.val("vv"))
                t1 = dma("sync", stg[0:128, 0:ncols], src[c * 128:(c + 1) * 128, 0:ncols])
                b.wait("vector", "dS", t1)
                b.wait("vector", "dA", b.val("dA"))
                V(lambda e, n=ncols, ci=ci: e.tensor_scalar(
                    stg[0:128, 0:n], stg[0:128, 0:n], ccol[:, ci:ci + 1], MG,
                    Alu.mult, Alu.add), None)
                V(lambda e, n=ncols: e.tensor_scalar(
                    stg[0:128, 0:n], stg[0:128, 0:n], MG + 1.0, -MG, Alu.min, Alu.add), None)
                V(lambda e, n=ncols: e.tensor_scalar(
                    stgb[0:128, 0:n], stg[0:128, 0:n], -1.0, None, Alu.max))
                b.wait("scalar", "vv", b.val("vv"))
                dma("scalar", dst[c * 128:(c + 1) * 128, 0:ncols], stgb[0:128, 0:ncols])

        if _DO('prep'):
            prep_ternary(l0g, l0g_q, C_MG0, 4096)
            prep_ternary(l0u, l0u_q, C_MU0, 4096)
            prep_ternary(l0d, l0d_q, C_MWD0, D)

        def prep_lpbq(src, dst, ncols):
            nch = src.shape[0] // 128
            nblk = ncols // 32
            smax = AR(24576, (nblk,), F32)
            sinv = AR(24576 + 4 * nblk, (nblk,), F32)
            for c in range(nch):
                b.wait("sync", "vv", b.val("vv"))
                b.wait("sync", "aa", b.val("aa"))
                t1 = dma("sync", stg[0:128, 0:ncols], src[c * 128:(c + 1) * 128, 0:ncols])
                b.wait("vector", "dS", t1)
                b.wait("vector", "dA", b.val("dA"))
                V(lambda e, n=ncols, nb=nblk: e.tensor_reduce(
                    smax[0:128, 0:nb],
                    stg[0:128, 0:n].rearrange("p (g c) -> p g c", c=32),
                    AX, Alu.max, apply_absolute_value=True))
                VW()
                V(lambda e, nb=nblk: e.tensor_scalar(
                    smax[0:128, 0:nb], smax[0:128, 0:nb], 1.0 / 7.0, 1e-12,
                    Alu.mult, Alu.max))
                VW()
                V(lambda e, nb=nblk: e.reciprocal(sinv[0:128, 0:nb], smax[0:128, 0:nb]))
                VW()
                V(lambda e, n=ncols, nb=nblk: e.tensor_tensor(
                    stg[0:128, 0:n].rearrange("p (g c) -> p g c", c=32),
                    stg[0:128, 0:n].rearrange("p (g c) -> p g c", c=32),
                    sinv[0:128, 0:nb].unsqueeze(2).broadcast_to([128, nb, 32]),
                    Alu.mult), None)
                V(lambda e, n=ncols: e.tensor_scalar(
                    stg[0:128, 0:n], stg[0:128, 0:n], MG, MG + 7.0, Alu.add, Alu.min), None)
                V(lambda e, n=ncols: e.tensor_scalar(
                    stg[0:128, 0:n], stg[0:128, 0:n], -MG, -8.0, Alu.add, Alu.max), None)
                V(lambda e, n=ncols, nb=nblk: e.tensor_tensor(
                    stgb[0:128, 0:n].rearrange("p (g c) -> p g c", c=32),
                    stg[0:128, 0:n].rearrange("p (g c) -> p g c", c=32),
                    smax[0:128, 0:nb].unsqueeze(2).broadcast_to([128, nb, 32]),
                    Alu.mult))
                b.wait("scalar", "vv", b.val("vv"))
                dma("scalar", dst[c * 128:(c + 1) * 128, 0:ncols], stgb[0:128, 0:ncols])

        if _DO('prep'):
            prep_lpbq(l1g, l1g_b, D)
            prep_lpbq(l1u, l1u_b, D)
            prep_lpbq(l1d, l1d_b, 3072)
        barrier()

        def ln_stats(tile, xb, xs, rms=False, maxmin=False):
            ts_ = slice(tile * T, (tile + 1) * T)
            b.wait("scalar", "mm", b.val("mm"))
            b.wait("scalar", "vv", b.val("vv"))
            for ch in range(FC):
                A(lambda e, ch=ch, t_=ts_, xb=xb: e.copy(xb[:, ch, :], xres[:, ch, t_]), None)
                A(lambda e, ch=ch, t_=ts_, xs=xs: e.activation(xs[:, ch, :],
                                                               xres[:, ch, t_], Act.Square))
            b.wait("tensor", "aa", b.val("aa"))
            for ch in range(FC):
                Tn(lambda e, ch=ch, xb=xb: e.matmul(PS[0:1, 6, :], onesb[:], xb[:, ch, :],
                                                    start=(ch == 0), stop=(ch == FC - 1)), None)
            for ch in range(FC):
                Tn(lambda e, ch=ch, xs=xs: e.matmul(PS[0:1, 7, :], onesb[:], xs[:, ch, :],
                                                    start=(ch == 0), stop=(ch == FC - 1)),
                   "mm" if ch == FC - 1 else None)
            b.wait("scalar", "mm", b.val("mm"))
            A(lambda e: e.copy(rw(R_S1), PS[0:1, 6, :]), None)
            A(lambda e: e.copy(rw(R_S2), PS[0:1, 7, :]))
            b.wait("vector", "aa", b.val("aa"))
            V(lambda e: e.tensor_scalar(rw(R_M), rw(R_S1), 1.0 / D, None, Alu.mult), None)
            V(lambda e: e.tensor_tensor(rw(R_T), rw(R_M), rw(R_M), Alu.mult), None)
            V(lambda e: e.scalar_tensor_tensor(rw(R_T), rw(R_S2), 1.0 / D, rw(R_T),
                                               Alu.mult, Alu.subtract))
            b.wait("scalar", "vv", b.val("vv"))
            A(lambda e: e.activation(rw(R_C), rw(R_T), Act.Sqrt, bias=cells[0:1, C_EPS:C_EPS + 1], scale=onef[:]))
            b.wait("vector", "aa", b.val("aa"))
            V(lambda e: e.reciprocal(rw(R_R), rw(R_C)))
            if rms:
                b.wait("scalar", "vv", b.val("vv"))
                A(lambda e: e.activation(rw(R_C), rw(R_T), Act.Sqrt, bias=0.0, scale=1.0))
                b.wait("vector", "aa", b.val("aa"))
                V(lambda e: e.scalar_tensor_tensor(rw(R_C), rw(R_C), 32.0, rw(R_R),
                                                   Alu.mult, Alu.mult), None)
                V(lambda e: e.tensor_scalar(rw(R_C), rw(R_C), 1e-12, None, Alu.max), None)
                V(lambda e: e.reciprocal(rw(R_T), rw(R_C)), None)
                V(lambda e: e.scalar_tensor_tensor(rw(R_R), rw(R_R), 32.0, rw(R_T),
                                                   Alu.mult, Alu.mult))
            if maxmin:
                V(lambda e, t_=ts_: e.tensor_reduce(
                    f0[:], xres[:, :, t_].transpose([0, 2, 1]), AX, Alu.max), None)
                V(lambda e, t_=ts_: e.tensor_reduce(
                    f1[:], xres[:, :, t_].transpose([0, 2, 1]), AX, Alu.min))
                b.wait("gpsimd", "vv", b.val("vv"))
                t = dma("gpsimd", f2[0:64, :], f0[64:128, :])
                b.wait("vector", "dG", t)
                V(lambda e: e.tensor_max(f0[0:64, :], f0[0:64, :], f2[0:64, :]))
                b.wait("gpsimd", "vv", b.val("vv"))
                t = dma("gpsimd", f2[0:64, :], f1[64:128, :])
                b.wait("vector", "dG", t)
                V(lambda e: e.tensor_tensor(f1[0:64, :], f1[0:64, :], f2[0:64, :], Alu.min))
                b.wait("gpsimd", "vv", b.val("vv"))
                t = dma("gpsimd", f2[0:32, :], f0[32:64, :])
                b.wait("vector", "dG", t)
                V(lambda e: e.tensor_max(f0[0:32, :], f0[0:32, :], f2[0:32, :]))
                b.wait("gpsimd", "vv", b.val("vv"))
                t = dma("gpsimd", f2[0:32, :], f1[32:64, :])
                b.wait("vector", "dG", t)
                V(lambda e: e.tensor_tensor(f1[0:32, :], f1[0:32, :], f2[0:32, :],
                                            Alu.min))
                VW()
                V(lambda e: e.transpose(f2[0:32, :], f0[0:32, :]))
                VW()
                V(lambda e: e.tensor_reduce(
                    s32[0:32, 0:NG], f2[0:32, :].rearrange("p (g c) -> p g c", c=32),
                    AX, Alu.max))
                VW()
                V(lambda e: e.transpose(s32[0:32, 32:64], s32[0:32, 0:32]))
                b.wait("gpsimd", "vv", b.val("vv"))
                tmx = dma("gpsimd", rw(R_MX).rearrange("o (g a) -> o g a", a=32),
                          s32[0:NG, 32:64])
                b.wait("vector", "dG", tmx)
                V(lambda e: e.transpose(f2[0:32, :], f1[0:32, :]))
                VW()
                V(lambda e: e.tensor_reduce(
                    s32[0:32, 0:NG], f2[0:32, :].rearrange("p (g c) -> p g c", c=32),
                    AX, Alu.min))
                VW()
                V(lambda e: e.transpose(s32[0:32, 32:64], s32[0:32, 0:32]))
                b.wait("gpsimd", "vv", b.val("vv"))
                tmn = dma("gpsimd", rw(R_MN).rearrange("o (g a) -> o g a", a=32),
                          s32[0:NG, 32:64])
                b.wait("vector", "dG", tmn)

        def l0_attn():
            wq = [AR(0, (FC, 512), BF16), AR(8192, (FC, 512), BF16)]
            hb = AR(16384, (FC, T), BF16)
            xbst = hb
            yy = AR(24576, (FC, T), F32)
            xsst = AR(40960, (FC, T), BF16)
            kvb = [AR(40960 + 2048 * i, (T,), F32) for i in range(4)]
            ab = [AR(49152 + 2048 * i, (T,), F32) for i in range(2)]
            V(lambda e: e.memset(yy[:, :, T - 1:T], 0.0))
            wst = {"n": 0, "rel": []}

            def load_ob(ob):
                buf = wst["n"] % 2
                q, sem = (("sync", "dS"), ("gpsimd", "dG"))[buf]
                if wst["n"] >= 2:
                    b.wait(q, "mm", wst["rel"][wst["n"] - 2])
                t = dma(q, wq[buf][:],
                        qkva_b.rearrange("(kc p) o -> p kc o", p=128)[:, :, ob * 512:(ob + 1) * 512])
                return buf, t, sem

            for t in range(NT):
                ts_ = slice(t * T, (t + 1) * T)
                ln_stats(t, xbst, xsst, rms=True)
                b.wait("gpsimd", "vv", b.val("vv"))
                bcast_row(rw(R_M), bc0[:])
                tb = bcast_row(rw(R_R), bc1[:])
                b.wait("vector", "dG", tb)
                if dbg and t == 1:
                    dma("gpsimd", dbgf[:, 0, :], bc0[:])
                    td = dma("gpsimd", dbgf[:, 1, :], bc1[:])
                    b.wait("vector", "dG", td)
                for ch in range(FC):
                    V(lambda e, ch=ch, t_=ts_: e.tensor_tensor(
                        f0[:], xres[:, ch, t_], bc0[:], Alu.subtract), None)
                    V(lambda e, ch=ch: e.scalar_tensor_tensor(
                        hb[:, ch, :], f0[:], gcol[:, ch:ch + 1], bc1[:],
                        Alu.mult, Alu.mult))
                vv_h = b.val("vv")
                if dbg and t == 1:
                    b.wait("gpsimd", "vv", vv_h)
                    dma("gpsimd", dbgf[:, 2, :], hb[:, 0, :])

                def mm_block(ob, wait_aa=None, wait_vv=None):
                    buf, tl, lsem = load_ob(ob)
                    b.wait("tensor", lsem, tl)
                    b.wait("tensor", "vv", max(vv_h, wait_vv or 0))
                    if wait_aa:
                        b.wait("tensor", "aa", wait_aa)
                    marks = []
                    for j in range(4):
                        for kc in range(FC):
                            Tn(lambda e, kc=kc, j=j, buf=buf: e.matmul(
                                PS[:, j, :], wq[buf][:, kc, j * 128:(j + 1) * 128],
                                hb[:, kc, :], start=(kc == 0), stop=(kc == FC - 1)),
                               "mm" if kc == FC - 1 else None)
                        marks.append(b.val("mm"))
                    wst["n"] += 1
                    wst["rel"].append(marks[-1])
                    return marks

                aa_prev, vv_prev = None, None
                for half in range(2):
                    mk_kv = mm_block(2 + half, wait_aa=aa_prev, wait_vv=vv_prev)
                    if vv_prev:
                        b.wait("scalar", "vv", vv_prev)
                    for j in range(4):
                        b.wait("scalar", "mm", mk_kv[j])
                        A(lambda e, j=j: e.copy(kvb[j][0:128, :], PS[:, j, :]))
                    aa_kv = b.val("aa")
                    mk_a = mm_block(4 + half, wait_aa=aa_kv)
                    scan_marks = []
                    for j in range(4):
                        ch = half * 4 + j
                        b.wait("scalar", "mm", mk_a[j])
                        if j >= 2:
                            b.wait("scalar", "vv", scan_marks[j - 2])
                        A(lambda e, j=j: e.activation(ab[j % 2][0:128, :], PS[:, j, :],
                                                      Act.Sigmoid))
                        b.wait("vector", "aa", b.val("aa"))
                        V(lambda e, j=j, ch=ch: e.tensor_tensor_scan(
                            yy[:, ch, :], ab[j % 2][0:128, :], kvb[j][0:128, :],
                            yy[:, ch, T - 1:T], Alu.mult, Alu.add))
                        scan_marks.append(b.val("vv"))
                        if dbg and t == 1 and ch == 0:
                            b.wait("gpsimd", "vv", b.val("vv"))
                            dma("gpsimd", dbgf[:, 3, :], kvb[j][0:128, :])
                            dma("gpsimd", dbgf[:, 4, :], ab[j % 2][0:128, :])
                            td = dma("gpsimd", dbgf[:, 5, :], yy[:, 0, :])
                            b.wait("scalar", "dG", td)
                    aa_prev, vv_prev = b.val("aa"), b.val("vv")
                for half in range(2):
                    mk_q = mm_block(half, wait_aa=aa_prev, wait_vv=vv_prev)
                    for j in range(4):
                        ch = half * 4 + j
                        b.wait("vector", "mm", mk_q[j])
                        V(lambda e, j=j, ch=ch: e.scalar_tensor_tensor(
                            f0[:], PS[:, j, :], 1.0, yy[:, ch, :], Alu.mult, Alu.mult), None)
                        V(lambda e, ch=ch, t_=ts_: e.tensor_tensor(
                            xres[:, ch, t_], xres[:, ch, t_], f0[:], Alu.add))
                        if dbg and t == 1 and ch == 0:
                            b.wait("gpsimd", "vv", b.val("vv"))
                            td = dma("gpsimd", dbgf[0:1, 6, :],
                                     rowsA[0:1, :, :].rearrange("o a b -> o (a b)")[0:1, 0:512])
                            b.wait("scalar", "dG", td)
                    aa_prev, vv_prev = b.val("aa"), b.val("vv")

        if _DO('l0a'):
            l0_attn()
        barrier()

        def l0_ffn(do_sweep2=True):
            xq = AR(0, (4, FC, T), BF16)
            xbst = AR(32768, (FC, T), BF16)
            xsst = AR(40960, (FC, T), BF16)
            wg_ = [AR(32768, (FC, 512), BF16)]
            wu_ = [AR(40960, (FC, 512), BF16)]
            vt = [AR(49152 + 1024 * i, (T,), BF16) for i in range(2)]
            vq = [AR(51200 + 1024 * i, (T,), BF16) for i in range(2)]
            wk = [AR(32768 + 2048 * i, (D,), BF16) for i in range(2)]

            for grp in range(2):
                tiles = list(range(grp * 4, grp * 4 + 4))
                b.wait("vector", "mm", b.val("mm"))
                for ti, t in enumerate(tiles):
                    ts_ = slice(t * T, (t + 1) * T)
                    ln_stats(t, xbst, xsst, maxmin=True)
                    V(lambda e: e.tensor_tensor(rw(R_MX), rw(R_MX), rw(R_M),
                                                Alu.subtract), None)
                    V(lambda e: e.tensor_tensor(rw(R_MN), rw(R_M), rw(R_MN),
                                                Alu.subtract), None)
                    V(lambda e: e.tensor_max(rw(R_MX), rw(R_MX), rw(R_MN)), None)
                    V(lambda e: e.tensor_tensor(rw(R_C), rw(R_R), rw(R_MX), Alu.mult), None)
                    V(lambda e: e.tensor_scalar(rw(R_C), rw(R_C), 1e-5, None, Alu.max), None)
                    V(lambda e: e.reciprocal(rw(R_T), rw(R_C)), None)
                    V(lambda e: e.tensor_scalar(rw(R_MN), rw(R_C),
                                                cells[0:1, C_MG0:C_MG0 + 1], 1.0 / 127.0,
                                                Alu.mult, Alu.mult), None)
                    V(lambda e: e.tensor_scalar(rw(R_MX), rw(R_C),
                                                cells[0:1, C_MU0:C_MU0 + 1], 1.0 / 127.0,
                                                Alu.mult, Alu.mult), None)
                    V(lambda e: e.scalar_tensor_tensor(rw(R_C), rw(R_T), 127.0, rw(R_R),
                                                       Alu.mult, Alu.mult))
                    b.wait("gpsimd", "vv", b.val("vv"))
                    row_pieces_to_tin(rw(R_MN), ti * 8)
                    row_pieces_to_tin(rw(R_MX), ti * 8 + 4)
                    bcast_row(rw(R_M), bc0[:])
                    tb = bcast_row(rw(R_C), bc1[:])
                    b.wait("vector", "dG", tb)
                    for ch in range(FC):
                        V(lambda e, ch=ch, t_=ts_: e.tensor_tensor(
                            f0[:], xres[:, ch, t_], bc0[:], Alu.subtract), None)
                        V(lambda e: e.tensor_tensor(f0[:], f0[:], bc1[:], Alu.mult), None)
                        V(lambda e: e.tensor_scalar(f0[:], f0[:], MG, MG + 127.0,
                                                    Alu.add, Alu.min), None)
                        V(lambda e, ti=ti, ch=ch: e.tensor_scalar(
                            xq[:, ti, ch, :], f0[:], -MG, -128.0, Alu.add, Alu.max))
                V(lambda e, g=grp: e.memset(cols2[:, g * 16:g * 16 + 16], 0.0), None)
                b.wait("vector", "dG", b.val("dG"))
                V(lambda e: e.transpose(cols[0:128, 0:32], tin[:]))
                if dbg and grp == 0:
                    b.wait("gpsimd", "vv", b.val("vv"))
                    dma("gpsimd", dbgf[:, 4, 0:32], cols[:, 0:32])
                    td = dma("gpsimd", dbgf[:, 5, 0:32], tin[:, 0:32])
                    b.wait("sync", "dG", td)
                vv_xq = b.val("vv")
                b.wait("sync", "vv", vv_xq)
                b.wait("scalar", "vv", vv_xq)
                wst = {"n": 0, "rel": []}
                ep = []
                it = 0
                for og in range(8):
                    buf = 0
                    if wst["n"] >= 1:
                        b.wait("sync", "mm", wst["rel"][-1])
                    dma("sync", wg_[buf][:],
                        l0g_q.rearrange("(kc p) o -> p kc o", p=128)[:, :, og * 512:(og + 1) * 512])
                    tu_ = dma("sync", wu_[buf][:],
                              l0u_q.rearrange("(kc p) o -> p kc o", p=128)[:, :, og * 512:(og + 1) * 512])
                    wst["n"] += 1
                    b.wait("tensor", "dS", tu_)
                    b.wait("tensor", "vv", vv_xq)
                    for ti, t in enumerate(tiles):
                        for tg in range(4):
                            par = it % 2
                            bk_g, bk_u = 2 * par, 2 * par + 1
                            if it >= 2:
                                b.wait("tensor", "aa", ep[it - 2][0])
                                b.wait("tensor", "vv", ep[it - 2][1])
                            for kc in range(FC):
                                Tn(lambda e, kc=kc, ti=ti, tg=tg, buf=buf, bk=bk_g: e.matmul(
                                    PS[:, bk, :], xq[:, ti, kc, tg * 128:(tg + 1) * 128],
                                    wg_[buf][:, kc, :], start=(kc == 0), stop=(kc == FC - 1)),
                                   "mm" if kc == FC - 1 else None)
                            for kc in range(FC):
                                Tn(lambda e, kc=kc, ti=ti, tg=tg, buf=buf, bk=bk_u: e.matmul(
                                    PS[:, bk, :], xq[:, ti, kc, tg * 128:(tg + 1) * 128],
                                    wu_[buf][:, kc, :], start=(kc == 0), stop=(kc == FC - 1)),
                                   "mm" if kc == FC - 1 else None)
                            mk = b.val("mm")
                            ci = ti * 8 + tg
                            b.wait("scalar", "mm", mk)
                            A(lambda e, bk=bk_g, ci=ci: e.activation(
                                f0[:], PS[:, bk, :], Act.Silu, bias=zcol[:],
                                scale=cols[:, ci:ci + 1]))
                            b.wait("vector", "aa", b.val("aa"))
                            V(lambda e, bk=bk_u, ci=ci: e.scalar_tensor_tensor(
                                f1[:], PS[:, bk, :], cols[:, ci + 4:ci + 5], f0[:],
                                Alu.mult, Alu.mult), None)
                            cx = grp * 16 + ti * 4 + tg
                            V(lambda e: e.tensor_reduce(f2[:, 0:1], f1[:], AX, Alu.max,
                                                        apply_absolute_value=True))
                            VW()
                            V(lambda e, cx=cx: e.tensor_max(
                                cols2[:, cx:cx + 1], cols2[:, cx:cx + 1], f2[:, 0:1]))
                            vv_ep = b.val("vv")
                            b.wait("scalar", "vv", vv_ep)
                            b.wait("scalar", "dA", b.val("dA"))
                            A(lambda e: e.copy(sp0[:], f1[:]))
                            aa_ep = b.val("aa")
                            b.wait("scalar", "aa", aa_ep)
                            dma("scalar",
                                Vbuf[t * T + tg * 128:t * T + (tg + 1) * 128,
                                     og * 512:(og + 1) * 512], sp0[:])
                            ep.append((aa_ep, vv_ep))
                            it += 1
                    wst["rel"].append(b.val("mm"))
            barrier()
            if not do_sweep2:
                return
            V(lambda e: e.transpose(tin[:], cols2[0:128, 0:32]))
            komark = []
            for t in range(NT):
                ts_ = slice(t * T, (t + 1) * T)
                b.wait("tensor", "vv", b.val("vv"))
                b.wait("gpsimd", "vv", b.val("vv"))
                tpz = tin_rowslices_from_tr(tin, t * 4, rw(R_T))
                b.wait("vector", "dG", tpz)
                V(lambda e: e.tensor_scalar(rw(R_T), rw(R_T), 1e-5, None, Alu.max), None)
                V(lambda e: e.reciprocal(rw(R_C), rw(R_T)), None)
                V(lambda e: e.tensor_scalar(rw(R_C), rw(R_C), 127.0, None, Alu.mult), None)
                V(lambda e: e.tensor_scalar(rw(R_T), rw(R_T),
                                            cells[0:1, C_MWD0:C_MWD0 + 1], 1.0 / 127.0,
                                            Alu.mult, Alu.mult))
                b.wait("gpsimd", "vv", b.val("vv"))
                bcast_row(rw(R_C), bc0[:])
                tb = bcast_row(rw(R_T), bc1[:])
                b.wait("vector", "dG", tb)
                if dbg and t == 0:
                    dma("gpsimd", dbgf[:, 6, :], bc0[:])
                    td = dma("gpsimd", dbgf[:, 7, :], bc1[:])
                    b.wait("sync", "dG", td)
                for ko in range(32):
                    r = ko % 2
                    if len(komark) >= 2:
                        b.wait("sync", "vv", komark[-2][1])
                        b.wait("sync", "mm", komark[-2][0])
                        b.wait("vector", "mm", komark[-2][0])
                    tkw = dma("sync", wk[r][0:128, :],
                              l0d_q[ko * 128:(ko + 1) * 128, :].rearrange(
                                  "(o p) d -> p (o d)", p=128))
                    tv = None
                    for tg in range(4):
                        tv = dma_tr("sync", vt[r][0:128, tg * 128:(tg + 1) * 128],
                                    Vbuf[t * T + tg * 128:t * T + (tg + 1) * 128,
                                         ko * 128:(ko + 1) * 128])
                    b.wait("vector", "dS", tv)
                    V(lambda e, r=r: e.tensor_tensor(f0[:], vt[r][0:128, :], bc0[:],
                                                     Alu.mult), None)
                    V(lambda e: e.tensor_scalar(f0[:], f0[:], MG, MG + 127.0,
                                                Alu.add, Alu.min), None)
                    V(lambda e, r=r: e.tensor_scalar(vq[r][0:128, :], f0[:], -MG, -128.0,
                                                     Alu.add, Alu.max))
                    vvq = b.val("vv")
                    b.wait("tensor", "vv", vvq)
                    b.wait("tensor", "dS", tkw)
                    for od in range(FC):
                        Tn(lambda e, ko=ko, od=od, r=r: e.matmul(
                            PS[:, od, :],
                            wk[r][0:128, :].rearrange("p (o d) -> p o d", o=8)[:, od, :]
                            if False else wk[r][0:128, od * 128:(od + 1) * 128],
                            vq[r][0:128, :], start=(ko == 0), stop=(ko == 31)),
                           "mm" if od == FC - 1 else None)
                    komark.append((b.val("mm"), vvq))
                mk = b.val("mm")
                b.wait("vector", "mm", mk)
                for od in range(FC):
                    V(lambda e, od=od: e.tensor_tensor(f0[:], PS[:, od, :], bc1[:],
                                                       Alu.mult), None)
                    V(lambda e, od=od, t_=ts_: e.tensor_tensor(
                        xres[:, od, t_], xres[:, od, t_], f0[:], Alu.add))

        if _DO('l0f') or _DO('l0f1'):
            l0_ffn(do_sweep2=_DO('l0f'))
        barrier()

        def l1_attn():
            wz_ = [AR(0, (FC, 128), BF16), AR(2048, (FC, 128), BF16)]
            wh_ = [AR(4096, (FC, 128), BF16), AR(6144, (FC, 128), BF16)]
            wg2 = [AR(8192, (FC, 128), BF16), AR(10240, (FC, 128), BF16)]
            ub = AR(12288, (FC, T), BF16)
            xbst = ub
            yy = AR(20480, (FC, T), F32)
            hbb = AR(36864, (FC, T), BF16)
            zb = [AR(45056 + 2048 * i, (T,), F32) for i in range(2)]
            xsst = AR(45056, (FC, T), BF16)
            V(lambda e: e.memset(yy[:, :, T - 1:T], 0.0))
            for t in range(NT):
                ts_ = slice(t * T, (t + 1) * T)
                ln_stats(t, xbst, xsst)
                b.wait("gpsimd", "vv", b.val("vv"))
                bcast_row(rw(R_M), bc0[:])
                tb = bcast_row(rw(R_R), bc1[:])
                b.wait("vector", "dG", tb)
                for ch in range(FC):
                    V(lambda e, ch=ch, t_=ts_: e.tensor_tensor(
                        f0[:], xres[:, ch, t_], bc0[:], Alu.subtract), None)
                    V(lambda e, ch=ch: e.tensor_tensor(ub[:, ch, :], f0[:], bc1[:],
                                                       Alu.mult))
                vv_u = b.val("vv")
                zmark = []
                for ch in range(FC):
                    buf = ch % 2
                    if ch >= 2:
                        b.wait("sync", "mm", zmark[ch - 2][2])
                        b.wait("gpsimd", "mm", zmark[ch - 2][2])
                    t1 = dma("sync", wz_[buf][:],
                             wz_b.rearrange("(kc p) o -> p kc o", p=128)[:, :, ch * 128:(ch + 1) * 128])
                    t2 = dma("gpsimd", wh_[buf][:],
                             wh_b.rearrange("(kc p) o -> p kc o", p=128)[:, :, ch * 128:(ch + 1) * 128])
                    b.wait("tensor", "dS", t1)
                    b.wait("tensor", "dG", t2)
                    b.wait("tensor", "vv", vv_u)
                    if ch >= 2:
                        b.wait("tensor", "aa", zmark[ch - 2][0])
                        b.wait("tensor", "vv", zmark[ch - 2][1])
                    bz_, bh_ = 2 * (ch % 2), 2 * (ch % 2) + 1
                    for kc in range(FC):
                        Tn(lambda e, kc=kc, buf=buf, bk=bz_: e.matmul(
                            PS[:, bk, :], wz_[buf][:, kc, :], ub[:, kc, :],
                            start=(kc == 0), stop=(kc == FC - 1)),
                           "mm" if kc == FC - 1 else None)
                    mz = b.val("mm")
                    for kc in range(FC):
                        Tn(lambda e, kc=kc, buf=buf, bk=bh_: e.matmul(
                            PS[:, bk, :], wh_[buf][:, kc, :], ub[:, kc, :],
                            start=(kc == 0), stop=(kc == FC - 1)),
                           "mm" if kc == FC - 1 else None)
                    mh = b.val("mm")
                    b.wait("scalar", "mm", mz)
                    A(lambda e, ch=ch, bk=bz_: e.activation(
                        zb[ch % 2][0:128, :], PS[:, bk, :], Act.Sigmoid,
                        bias=bzc[:, ch:ch + 1], scale=onecol[:]))
                    aa_z = b.val("aa")
                    b.wait("vector", "aa", aa_z)
                    b.wait("vector", "mm", mh)
                    V(lambda e, ch=ch, bk=bh_: e.scalar_tensor_tensor(
                        f0[:], PS[:, bk, :], bhc[:, ch:ch + 1], zb[ch % 2][0:128, :],
                        Alu.add, Alu.mult), None)
                    V(lambda e, ch=ch: e.tensor_scalar(
                        f1[:], zb[ch % 2][0:128, :], -1.0, 1.0, Alu.mult, Alu.add), None)
                    V(lambda e, ch=ch: e.tensor_tensor_scan(
                        yy[:, ch, :], f1[:], f0[:], yy[:, ch, T - 1:T],
                        Alu.mult, Alu.add))
                    vv_z = b.val("vv")
                    b.wait("scalar", "vv", vv_z)
                    A(lambda e, ch=ch: e.copy(hbb[:, ch, :], yy[:, ch, :]))
                    zmark.append((aa_z, vv_z, mh))
                    if dbg and t == 0 and ch == 0:
                        b.wait("gpsimd", "vv", vv_z)
                        dma("gpsimd", dbgf[:, 0, :], zb[0][0:128, :])
                        dma("gpsimd", dbgf[:, 1, :], f0[:])
                        dma("gpsimd", dbgf[:, 2, :], f1[:])
                        dma("gpsimd", dbgf[:, 3, :], yy[:, 0, :])
                        td = dma("gpsimd", dbgb[:, 0, :], ub[:, 0, :])
                        b.wait("vector", "dG", td)
                aa_h = b.val("aa")
                gmark = []
                for ch in range(FC):
                    buf = ch % 2
                    q, sem = (("sync", "dS"), ("gpsimd", "dG"))[buf]
                    if ch >= 2:
                        b.wait(q, "mm", gmark[ch - 2][1])
                    t1 = dma(q, wg2[buf][:],
                             wg_b.rearrange("(kc p) o -> p kc o", p=128)[:, :, ch * 128:(ch + 1) * 128])
                    b.wait("tensor", sem, t1)
                    b.wait("tensor", "aa", aa_h)
                    if ch >= 4:
                        b.wait("tensor", "aa", gmark[ch - 4][0])
                    bk = ch % 4
                    for kc in range(FC):
                        Tn(lambda e, kc=kc, buf=buf, bk=bk: e.matmul(
                            PS[:, bk, :], wg2[buf][:, kc, :], hbb[:, kc, :],
                            start=(kc == 0), stop=(kc == FC - 1)),
                           "mm" if kc == FC - 1 else None)
                    mg_ = b.val("mm")
                    b.wait("scalar", "mm", mg_)
                    A(lambda e, ch=ch, bk=bk: e.activation(
                        f2[:], PS[:, bk, :], Act.Sigmoid, bias=bgc[:, ch:ch + 1],
                        scale=onecol[:]))
                    aa_g = b.val("aa")
                    b.wait("vector", "aa", aa_g)
                    V(lambda e, ch=ch: e.tensor_tensor(f0[:], f2[:], yy[:, ch, :],
                                                       Alu.mult), None)
                    V(lambda e, ch=ch, t_=ts_: e.tensor_tensor(
                        xres[:, ch, t_], xres[:, ch, t_], f0[:], Alu.add))
                    gmark.append((aa_g, mg_))
                    if dbg and t == 0 and ch == 0:
                        b.wait("gpsimd", "vv", b.val("vv"))
                        td = dma("gpsimd", dbgf[:, 4, :], f2[:])
                        b.wait("scalar", "dG", td)

        if _DO('l1a'):
            l1_attn()
        barrier()
        if dbg:
            for c4 in range(4):
                dma("gpsimd",
                    x3T.rearrange("(c p) t -> p c t", p=128)[:, 2 * c4:2 * c4 + 2, :],
                    xres[:, 2 * c4:2 * c4 + 2, :])

        def allreduce_max(idx_lo, pair):
            b.wait("gpsimd", "vv", b.val("vv"))
            t = dma("gpsimd", cc_i[pair], cells[0:1, idx_lo:idx_lo + 2])
            b.wait("gpsimd", "dG", t)
            b.put("gpsimd", lambda e, pr=pair: e.collective_compute(
                "AllReduce", Alu.max, replica_groups=[list(range(n_cores))],
                ins=[cc_i[pr].opt()], outs=[cc_o[pr].opt()]), "gg")
            b.wait("gpsimd", "gg", b.val("gg"))
            t = dma("gpsimd", cells[0:1, idx_lo:idx_lo + 2], cc_o[pair])
            b.wait("vector", "dG", t)
            b.wait("scalar", "dG", t)

        def l1_ffn(upto="E"):
            xq = AR(0, (4, FC, T), BF16)
            xbst = AR(32768, (FC, T), BF16)
            xsst = AR(40960, (FC, T), BF16)
            wb_ = [AR(32768, (FC, 512), BF16), AR(40960, (FC, 512), BF16)]
            wdd = AR(0, (24, D), BF16)
            oib = [AR(49152 + 1024 * i, (T,), BF16) for i in range(2)]

            V(lambda e: e.memset(cols2[:, 32:33], 0.0))
            for t in range(NT):
                ts_ = slice(t * T, (t + 1) * T)
                ln_stats(t, xbst, xsst)
                b.wait("gpsimd", "vv", b.val("vv"))
                bcast_row(rw(R_M), bc0[:])
                tb = bcast_row(rw(R_R), bc1[:])
                b.wait("vector", "dG", tb)
                for ch in range(FC):
                    V(lambda e, ch=ch, t_=ts_: e.tensor_tensor(
                        f0[:], xres[:, ch, t_], bc0[:], Alu.subtract), None)
                    V(lambda e: e.tensor_tensor(f0[:], f0[:], bc1[:], Alu.mult), None)
                    V(lambda e: e.tensor_reduce(f2[:, 0:1], f0[:], AX, Alu.max,
                                                apply_absolute_value=True))
                    VW()
                    V(lambda e: e.tensor_max(cols2[:, 32:33], cols2[:, 32:33],
                                             f2[:, 0:1]))
            fold128_to_cell(cols2[:, 32:33], C_G1, Alu.max)
            V(lambda e: e.tensor_scalar(cells[0:1, C_G1 + 1:C_G1 + 2],
                                        cells[0:1, C_G1:C_G1 + 1], 1.0, None, Alu.mult))
            allreduce_max(C_G1, 0)
            V(lambda e: e.tensor_scalar(cells[0:1, C_S1:C_S1 + 1],
                                        cells[0:1, C_G1:C_G1 + 1], 1.0 / 32767.0, 1e-12,
                                        Alu.mult, Alu.max))
            VW()
            V(lambda e: e.reciprocal(cells[0:1, C_K1:C_K1 + 1],
                                     cells[0:1, C_S1:C_S1 + 1]))
            VW()
            if upto == "A":
                return

            V(lambda e: e.memset(cols2[:, 33:35], 0.0))
            for grp in range(2):
                tiles = list(range(grp * 4, grp * 4 + 4))
                b.wait("vector", "mm", b.val("mm"))
                for ti, t in enumerate(tiles):
                    ts_ = slice(t * T, (t + 1) * T)
                    ln_stats(t, xbst, xsst)
                    V(lambda e: e.tensor_scalar(rw(R_C), rw(R_R),
                                                cells[0:1, C_K1:C_K1 + 1], None, Alu.mult))
                    b.wait("gpsimd", "vv", b.val("vv"))
                    bcast_row(rw(R_M), bc0[:])
                    tb = bcast_row(rw(R_C), bc1[:])
                    b.wait("vector", "dG", tb)
                    for ch in range(FC):
                        V(lambda e, ch=ch, t_=ts_: e.tensor_tensor(
                            f0[:], xres[:, ch, t_], bc0[:], Alu.subtract), None)
                        V(lambda e: e.tensor_tensor(f0[:], f0[:], bc1[:], Alu.mult), None)
                        V(lambda e: e.tensor_scalar(f0[:], f0[:], MG, MG + 32767.0,
                                                    Alu.add, Alu.min), None)
                        V(lambda e, ti=ti, ch=ch: e.tensor_scalar(
                            xq[:, ti, ch, :], f0[:], -MG, -32768.0, Alu.add, Alu.max))
                vv_xq = b.val("vv")
                b.wait("sync", "vv", vv_xq)
                if upto == "B0":
                    continue
                nsrc = 1 if upto.startswith("B1") else 2
                ep = []
                it = 0
                wst = {"n": 0, "rel": []}
                for wsrc, spill, mcol in ((l1u_b, Ubuf, 33), (l1g_b, Gbuf, 34))[:nsrc]:
                    for og in range(6):
                        buf = wst["n"] % 2
                        if wst["n"] >= 2:
                            b.wait("sync", "mm", wst["rel"][wst["n"] - 2])
                            b.wait("scalar", "mm", wst["rel"][wst["n"] - 2])
                        tws = twa = None
                        for kc in range(FC):
                            for j in range(4):
                                tw_ = dma_tr(("sync", "scalar")[kc % 2],
                                             wb_[buf][:, kc, j * 128:(j + 1) * 128],
                                             wsrc[og * 512 + j * 128:og * 512 + (j + 1) * 128,
                                                  kc * 128:(kc + 1) * 128])
                                if kc % 2 == 0:
                                    tws = tw_
                                else:
                                    twa = tw_
                        wst["n"] += 1
                        if upto == "B1L":
                            wst["rel"].append(b.val("mm"))
                            continue
                        b.wait("tensor", "dS", tws)
                        b.wait("tensor", "dA", twa)
                        b.wait("tensor", "vv", vv_xq)
                        for ti, t in enumerate(tiles):
                            for j in range(4):
                                bk = it % 4
                                if it >= 4 and upto != "B1M":
                                    b.wait("tensor", "aa", ep[it - 4][0])
                                    b.wait("tensor", "vv", ep[it - 4][1])
                                for kc in range(FC):
                                    Tn(lambda e, kc=kc, ti=ti, j=j, buf=buf, bk=bk: e.matmul(
                                        PS[:, bk, :], wb_[buf][:, kc, j * 128:(j + 1) * 128],
                                        xq[:, ti, kc, :], start=(kc == 0),
                                        stop=(kc == FC - 1)),
                                       "mm" if kc == FC - 1 else None)
                                mk = b.val("mm")
                                if upto == "B1M":
                                    it += 1
                                    continue
                                b.wait("vector", "mm", mk)
                                V(lambda e, bk=bk: e.tensor_reduce(
                                    f2[:, 0:1], PS[:, bk, :], AX, Alu.max,
                                    apply_absolute_value=True))
                                VW()
                                V(lambda e, mc=mcol: e.tensor_max(
                                    cols2[:, mc:mc + 1], cols2[:, mc:mc + 1], f2[:, 0:1]))
                                vv_ep = b.val("vv")
                                b.wait("scalar", "mm", mk)
                                b.wait("scalar", "vv", vv_ep)
                                b.wait("scalar", "dA", b.val("dA"))
                                A(lambda e, bk=bk: e.copy(sp0[:], PS[:, bk, :]))
                                aa_ep = b.val("aa")
                                b.wait("scalar", "aa", aa_ep)
                                dma("scalar",
                                    spill[og * 512 + j * 128:og * 512 + (j + 1) * 128,
                                          t * T:(t + 1) * T], sp0[:])
                                ep.append((aa_ep, vv_ep))
                                it += 1
                        wst["rel"].append(b.val("mm"))
            if upto == "B0" or upto.startswith("B1"):
                return
            fold128_to_cell(cols2[:, 33:34], C_G2, Alu.max)
            fold128_to_cell(cols2[:, 34:35], C_G3, Alu.max)
            V(lambda e: e.tensor_tensor(cells[0:1, C_G2:C_G2 + 1],
                                        cells[0:1, C_G2:C_G2 + 1],
                                        cells[0:1, C_S1:C_S1 + 1], Alu.mult), None)
            V(lambda e: e.tensor_tensor(cells[0:1, C_G3:C_G3 + 1],
                                        cells[0:1, C_G3:C_G3 + 1],
                                        cells[0:1, C_S1:C_S1 + 1], Alu.mult))
            allreduce_max(C_G2, 1)
            for (gi, si, ki) in ((C_G2, C_S2, C_K2), (C_G3, C_S3, C_K3)):
                V(lambda e, gi=gi, si=si: e.tensor_scalar(
                    cells[0:1, si:si + 1], cells[0:1, gi:gi + 1], 1.0 / 32767.0, 1e-12,
                    Alu.mult, Alu.max))
                VW()
                V(lambda e, si=si, ki=ki: e.reciprocal(cells[0:1, ki:ki + 1],
                                                       cells[0:1, si:si + 1]))
                VW()
                V(lambda e, ki=ki: e.tensor_tensor(
                    cells[0:1, ki:ki + 1], cells[0:1, ki:ki + 1],
                    cells[0:1, C_S1:C_S1 + 1], Alu.mult))
                VW()
            V(lambda e: e.tensor_scalar(cells[0:1, C_NS3MG:C_NS3MG + 1],
                                        cells[0:1, C_S3:C_S3 + 1], -MG, None, Alu.mult))
            b.wait("gpsimd", "vv", b.val("vv"))
            cell_to_col(C_K2, ccol[:, 3:4])
            cell_to_col(C_K3, ccol[:, 4:5])
            cell_to_col(C_S3, ccol[:, 7:8])
            tb = cell_to_col(C_NS3MG, ccol[:, 6:7])
            b.wait("scalar", "dG", tb)
            b.wait("vector", "dG", tb)
            if upto == "B":
                return

            b.wait("sync", "dA", b.val("dA"))
            b.wait("gpsimd", "dA", b.val("dA"))
            V(lambda e: e.memset(cols2[:, 35:36], 0.0))
            for oc in range(24):
                for t in range(NT):
                    r = (oc * NT + t) % 2
                    sp = (sp0, sp1)[r]
                    q, sem = (("sync", "dS"), ("gpsimd", "dG"))[r]
                    b.wait(q, "aa", b.val("aa"))
                    tl = dma(q, sp[:], Gbuf[oc * 128:(oc + 1) * 128,
                                            t * T:(t + 1) * T])
                    b.wait("scalar", sem, tl)
                    b.wait("scalar", "vv", b.val("vv"))
                    A(lambda e, sp=sp: e.activation(f2[:], sp[:], Act.Identity,
                                                    bias=mgcol[:], scale=ccol[:, 4:5]), None)
                    A(lambda e: e.activation(bc0[:], f2[:], Act.Sigmoid,
                                             bias=ccol[:, 6:7], scale=ccol[:, 7:8]))
                    b.wait("vector", "aa", b.val("aa"))
                    V(lambda e: e.tensor_scalar(f2[:], f2[:], -MG, 32767.0,
                                                Alu.add, Alu.min), None)
                    V(lambda e: e.tensor_scalar(bc1[:], bc0[:], 65536.0, MG,
                                                Alu.mult, Alu.add), None)
                    V(lambda e: e.tensor_scalar(bc1[:], bc1[:], MG + 65535.0, -MG,
                                                Alu.min, Alu.add), None)
                    V(lambda e: e.tensor_tensor(f0[:], bc1[:], f2[:], Alu.mult), None)
                    V(lambda e: e.tensor_reduce(f2[:, 0:1], f0[:], AX, Alu.max,
                                                apply_absolute_value=True))
                    VW()
                    V(lambda e: e.tensor_max(cols2[:, 35:36], cols2[:, 35:36],
                                             f2[:, 0:1]))
                    b.wait("scalar", "vv", b.val("vv"))
                    b.wait("scalar", "dA", b.val("dA"))
                    A(lambda e, sp=sp: e.copy(sp[:], f0[:]))
                    b.wait("scalar", "aa", b.val("aa"))
                    dma("scalar", GSbuf[oc * 128:(oc + 1) * 128, t * T:(t + 1) * T], sp[:])
            fold128_to_cell(cols2[:, 35:36], C_G4, Alu.max)
            V(lambda e: e.tensor_scalar(cells[0:1, C_T0:C_T0 + 1],
                                        cells[0:1, C_S3:C_S3 + 1], 1.0 / 65536.0, None,
                                        Alu.mult))
            VW()
            V(lambda e: e.tensor_tensor(cells[0:1, C_G4:C_G4 + 1],
                                        cells[0:1, C_G4:C_G4 + 1],
                                        cells[0:1, C_T0:C_T0 + 1], Alu.mult))
            VW()
            V(lambda e: e.tensor_scalar(cells[0:1, C_G4 + 1:C_G4 + 2],
                                        cells[0:1, C_G4:C_G4 + 1], 1.0, None, Alu.mult))
            allreduce_max(C_G4, 2)
            V(lambda e: e.tensor_scalar(cells[0:1, C_S4:C_S4 + 1],
                                        cells[0:1, C_G4:C_G4 + 1], 1.0 / 32767.0, 1e-12,
                                        Alu.mult, Alu.max))
            VW()
            V(lambda e: e.reciprocal(cells[0:1, C_K4:C_K4 + 1],
                                     cells[0:1, C_S4:C_S4 + 1]))
            VW()
            V(lambda e: e.tensor_tensor(cells[0:1, C_K4:C_K4 + 1],
                                        cells[0:1, C_K4:C_K4 + 1],
                                        cells[0:1, C_T0:C_T0 + 1], Alu.mult))
            b.wait("gpsimd", "vv", b.val("vv"))
            tb = cell_to_col(C_K4, ccol[:, 5:6])
            b.wait("scalar", "dG", tb)
            if upto == "C":
                return

            b.wait("sync", "dA", b.val("dA"))
            b.wait("gpsimd", "dA", b.val("dA"))
            V(lambda e: e.memset(cols2[:, 35:36], 0.0))
            for oc in range(24):
                for t in range(NT):
                    r = (oc * NT + t) % 2
                    spg, spu = (sp0, sp1) if r == 0 else (sp1, sp0)
                    b.wait("sync", "aa", b.val("aa"))
                    b.wait("gpsimd", "aa", b.val("aa"))
                    t1 = dma("sync", spg[:], GSbuf[oc * 128:(oc + 1) * 128, t * T:(t + 1) * T])
                    t2 = dma("gpsimd", spu[:], Ubuf[oc * 128:(oc + 1) * 128,
                                                    t * T:(t + 1) * T])
                    b.wait("scalar", "dS", t1)
                    b.wait("scalar", "dG", t2)
                    b.wait("scalar", "vv", b.val("vv"))
                    A(lambda e, spg=spg: e.activation(f0[:], spg[:], Act.Identity,
                                                      bias=mgcol[:], scale=ccol[:, 5:6]), None)
                    A(lambda e, spu=spu: e.activation(f1[:], spu[:], Act.Identity,
                                                      bias=mgcol[:], scale=ccol[:, 3:4]))
                    b.wait("vector", "aa", b.val("aa"))
                    V(lambda e: e.tensor_scalar(f0[:], f0[:], -MG, 32767.0,
                                                Alu.add, Alu.min), None)
                    V(lambda e: e.tensor_scalar(f1[:], f1[:], -MG, 32767.0,
                                                Alu.add, Alu.min), None)
                    V(lambda e: e.tensor_tensor(f2[:], f0[:], f1[:], Alu.mult), None)
                    V(lambda e: e.tensor_reduce(f0[:, 0:1], f2[:], AX, Alu.max,
                                                apply_absolute_value=True))
                    VW()
                    V(lambda e: e.tensor_max(cols2[:, 35:36], cols2[:, 35:36],
                                             f0[:, 0:1]))
                    b.wait("scalar", "vv", b.val("vv"))
                    b.wait("scalar", "dA", b.val("dA"))
                    A(lambda e, spg=spg: e.copy(spg[:], f2[:]))
                    b.wait("scalar", "aa", b.val("aa"))
                    dma("scalar", Pbuf[oc * 128:(oc + 1) * 128, t * T:(t + 1) * T], spg[:])
            fold128_to_cell(cols2[:, 35:36], C_G5, Alu.max)
            V(lambda e: e.tensor_tensor(cells[0:1, C_T1:C_T1 + 1],
                                        cells[0:1, C_S4:C_S4 + 1],
                                        cells[0:1, C_S2:C_S2 + 1], Alu.mult))
            VW()
            V(lambda e: e.tensor_tensor(cells[0:1, C_G5:C_G5 + 1],
                                        cells[0:1, C_G5:C_G5 + 1],
                                        cells[0:1, C_T1:C_T1 + 1], Alu.mult))
            VW()
            V(lambda e: e.tensor_scalar(cells[0:1, C_G5 + 1:C_G5 + 2],
                                        cells[0:1, C_G5:C_G5 + 1], 1.0, None, Alu.mult))
            allreduce_max(C_G5, 3)
            V(lambda e: e.tensor_scalar(cells[0:1, C_S5:C_S5 + 1],
                                        cells[0:1, C_G5:C_G5 + 1], 1.0 / 32767.0, 1e-12,
                                        Alu.mult, Alu.max))
            VW()
            V(lambda e: e.reciprocal(cells[0:1, C_K5:C_K5 + 1],
                                     cells[0:1, C_S5:C_S5 + 1]))
            VW()
            V(lambda e: e.tensor_tensor(cells[0:1, C_K5:C_K5 + 1],
                                        cells[0:1, C_K5:C_K5 + 1],
                                        cells[0:1, C_T1:C_T1 + 1], Alu.mult))
            b.wait("gpsimd", "vv", b.val("vv"))
            cell_to_col(C_K5, ccol[:, 3:4])
            tb = cell_to_col(C_S5, ccol[:, 4:5])
            b.wait("scalar", "dG", tb)
            b.wait("vector", "dG", tb)
            if upto == "D":
                return

            b.wait("sync", "dA", b.val("dA"))
            b.wait("gpsimd", "dA", b.val("dA"))
            b.wait("sync", "mm", b.val("mm"))
            b.wait("scalar", "mm", b.val("mm"))
            tws = twa = None
            for ki in range(24):
                for od in range(FC):
                    tw_ = dma_tr(("sync", "scalar")[od % 2],
                                 wdd[:, ki, od * 128:(od + 1) * 128],
                                 l1d_b[od * 128:(od + 1) * 128, ki * 128:(ki + 1) * 128])
                    if od % 2 == 0:
                        tws = tw_
                    else:
                        twa = tw_
            b.wait("tensor", "dS", tws)
            b.wait("tensor", "dA", twa)
            kimark = []
            for t in range(NT):
                ts_ = slice(t * T, (t + 1) * T)
                for ki in range(24):
                    r = ki % 2
                    sp = (sp0, sp1)[r]
                    q, sem = (("sync", "dS"), ("gpsimd", "dG"))[r]
                    b.wait(q, "aa", b.val("aa"))
                    tl = dma(q, sp[:], Pbuf[ki * 128:(ki + 1) * 128,
                                            t * T:(t + 1) * T])
                    b.wait("scalar", sem, tl)
                    b.wait("scalar", "vv", b.val("vv"))
                    A(lambda e, sp=sp: e.activation(f0[:], sp[:], Act.Identity,
                                                    bias=mgcol[:], scale=ccol[:, 3:4]))
                    b.wait("vector", "aa", b.val("aa"))
                    if len(kimark) >= 2:
                        b.wait("vector", "mm", kimark[-2])
                    V(lambda e, r=r: e.tensor_scalar(oib[r][0:128, :], f0[:], -MG,
                                                     32767.0, Alu.add, Alu.min))
                    b.wait("tensor", "vv", b.val("vv"))
                    for od in range(FC):
                        Tn(lambda e, ki=ki, od=od, r=r: e.matmul(
                            PS[:, od, :], wdd[:, ki, od * 128:(od + 1) * 128],
                            oib[r][0:128, :], start=(ki == 0), stop=(ki == 23)),
                           "mm" if od == FC - 1 else None)
                    kimark.append(b.val("mm"))
                    if 'sere' in phases:
                        for e_ in ("tensor", "sync", "scalar", "vector", "gpsimd"):
                            b.wait(e_, "mm", b.val("mm"))
                            b.wait(e_, "vv", b.val("vv"))
                            b.wait(e_, "aa", b.val("aa"))
                mk = b.val("mm")
                b.wait("vector", "mm", mk)
                for od in range(FC):
                    V(lambda e, od=od, t_=ts_: e.scalar_tensor_tensor(
                        xres[:, od, t_], PS[:, od, :], ccol[:, 4:5], xres[:, od, t_],
                        Alu.mult, Alu.add))

        if _DO('l1f'):
            l1_ffn()
        else:
            for u in ("A", "B0", "B1L", "B1M", "B1", "B", "C", "D"):
                if _DO('l1f' + u):
                    l1_ffn(upto=u)
                    break
        barrier()

        if dbg:
            dma("gpsimd", dbgf[0:1, 3, 0:64], cells[0:1, 0:64])
            dma("gpsimd", dbgb[:, 0, :], Ubuf[0:128, 1536:2048])
            dma("gpsimd", dbgb[:, 1, :], Gbuf[0:128, 1536:2048])
            dma("gpsimd", dbgb[:, 2, :], GSbuf[0:128, 1536:2048])
            for oc5 in range(5):
                tst = dma("gpsimd", dbgb[:, 3 + oc5, :],
                          Pbuf[oc5 * 128:(oc5 + 1) * 128, 1536:2048])
        for c4 in range(4):
            tst = dma("gpsimd", yT.rearrange("(c p) t -> p c t", p=128)[:, 2 * c4:2 * c4 + 2, :],
                      xres[:, 2 * c4:2 * c4 + 2, :])
        b.wait("gpsimd", "dG", tst)

        blk = es.enter_context(nc.Block())
        for eng in ENGS:
            def _mk(eng):
                def _run(q):
                    for f in b.ops[eng]:
                        f(q)
                return _run
            getattr(blk, eng)(_mk(eng))

    return nc


_NC_CACHE = {}


def kernel(L0_attn_x, L0_attn_w_qkva, L0_attn_gamma, L0_ffn_w_gate, L0_ffn_w_up,
           L0_ffn_w_down, L1_attn_Wz, L1_attn_bz, L1_attn_Wh, L1_attn_bh,
           L1_attn_Wg, L1_attn_bg, L1_ffn_w_gate, L1_ffn_w_up, L1_ffn_w_down):
    n = 8
    if "nc" not in _NC_CACHE:
        _NC_CACHE["nc"] = build_program(n)
    nc = _NC_CACHE["nc"]
    asc = np.ascontiguousarray
    x = np.asarray(L0_attn_x, dtype=np.float32)
    common = dict(
        qkvaT=asc(np.asarray(L0_attn_w_qkva, np.float32).T),
        gamma=asc(np.asarray(L0_attn_gamma, np.float32).reshape(1, D)),
        l0g=asc(np.asarray(L0_ffn_w_gate, np.float32).T),
        l0u=asc(np.asarray(L0_ffn_w_up, np.float32).T),
        l0d=asc(np.asarray(L0_ffn_w_down, np.float32).T),
        wzT=asc(np.asarray(L1_attn_Wz, np.float32).transpose(1, 0, 2).reshape(D, D)),
        bz=asc(np.asarray(L1_attn_bz, np.float32).reshape(1, D)),
        whT=asc(np.asarray(L1_attn_Wh, np.float32).transpose(1, 0, 2).reshape(D, D)),
        bh=asc(np.asarray(L1_attn_bh, np.float32).reshape(1, D)),
        wgn=asc(np.asarray(L1_attn_Wg, np.float32)),
        bg=asc(np.asarray(L1_attn_bg, np.float32).reshape(1, D)),
        l1g=asc(np.asarray(L1_ffn_w_gate, np.float32)),
        l1u=asc(np.asarray(L1_ffn_w_up, np.float32)),
        l1d=asc(np.asarray(L1_ffn_w_down, np.float32)),
    )
    in_maps = []
    for c in range(n):
        m = dict(common)
        m["xT"] = asc(x[c].T)
        in_maps.append(m)
    res = run_bass_kernel_spmd(nc, in_maps, core_ids=list(range(n)))
    out = np.empty((n, TOK, D), dtype=np.float32)
    for c in range(n):
        out[c] = np.asarray(res.results[c]["yT"]).T
    return out


# version 05
# speedup vs baseline: 1.0455x; 1.0455x over previous
import contextlib
import numpy as np
import concourse.bass as bass
import concourse.mybir as mybir
from concourse.bass_utils import run_bass_kernel_spmd

F32 = mybir.dt.float32
BF16 = mybir.dt.bfloat16
U8 = mybir.dt.uint8
Alu = mybir.AluOpType
Act = mybir.ActivationFunctionType
AX = mybir.AxisListType.X

MG = 12582912.0
D = 1024
FC = 8
TOK = 4096
T = 512
NT = TOK // T
NG = T // 32
ENGS = ("sync", "scalar", "vector", "tensor", "gpsimd")

C_MG0, C_MU0, C_MWD0 = 0, 1, 2
C_IMG0, C_IMU0, C_IMWD0 = 3, 4, 5
C_G1, C_G2, C_G3, C_G4, C_G5 = 6, 7, 8, 9, 10
C_S1, C_S2, C_S3, C_S4, C_S5 = 11, 12, 13, 14, 15
C_K1, C_K2, C_K3, C_K4, C_K5 = 16, 17, 18, 19, 20
C_NS3MG = 21
C_T0, C_T1 = 24, 25
C_EPS = 26
C_SUM = 32


class B:
    def __init__(self, nc, es):
        self.nc, self.es = nc, es
        self.ops = {e: [] for e in ENGS}
        self.sem, self.cnt = {}, {}

    def mksem(self, name):
        self.sem[name] = self.es.enter_context(self.nc.semaphore(name))
        self.cnt[name] = 0

    def wait(self, eng, semname, val):
        s = self.sem[semname]
        self.ops[eng].append(lambda e, s=s, v=val: e.wait_ge(s, v))

    def put(self, eng, make, sem=None, n=1):
        if sem is None:
            self.ops[eng].append(make)
            return None
        self.cnt[sem] += n
        s = self.sem[sem]
        self.ops[eng].append(lambda e, make=make, s=s, n=n: make(e).then_inc(s, n))
        return self.cnt[sem]

    def val(self, s):
        return self.cnt[s]


def build_program(n_cores=8, phases=('prep', 'l0a', 'l0f', 'l1a', 'l1f'), dbg=False):
    nc = bass.Bass("TRN2", target_bir_lowering=False)

    din = lambda n, s: nc.dram_tensor(n, s, F32, kind="ExternalInput").ap()
    xT = din("xT", [D, TOK])
    yT = nc.dram_tensor("yT", [D, TOK], F32, kind="ExternalOutput").ap()
    qkvaT = din("qkvaT", [D, 3 * D])
    gamma = din("gamma", [1, D])
    l0g, l0u, l0d = din("l0g", [D, 4096]), din("l0u", [D, 4096]), din("l0d", [4096, D])
    wzT, bzd = din("wzT", [D, D]), din("bz", [1, D])
    whT, bhd = din("whT", [D, D]), din("bh", [1, D])
    wgn, bgd = din("wgn", [D, D]), din("bg", [1, D])
    l1g, l1u, l1d = din("l1g", [3072, D]), din("l1u", [3072, D]), din("l1d", [D, 3072])

    scr = lambda n, s, dt=BF16: nc.dram_tensor(n, s, dt).ap()
    qkva_b = scr("qkva_b", [D, 3 * D])
    wz_b, wh_b, wg_b = scr("wz_b", [D, D]), scr("wh_b", [D, D]), scr("wg_b", [D, D])
    l0g_q, l0u_q, l0d_q = scr("l0g_q", [D, 4096]), scr("l0u_q", [D, 4096]), scr("l0d_q", [4096, D])
    l1g_b, l1u_b, l1d_b = scr("l1g_b", [3072, D]), scr("l1u_b", [3072, D]), scr("l1d_b", [D, 3072])
    Vbuf = (nc.dram_tensor("Vbuf", [TOK, 4096], BF16, kind="ExternalOutput").ap()
            if dbg else scr("Vbuf", [TOK, 4096]))
    scrd = ((lambda n, s: nc.dram_tensor(n, s, BF16, kind="ExternalOutput").ap())
            if dbg else scr)
    Ubuf, Gbuf = scrd("Ubuf", [3072, TOK]), scrd("Gbuf", [3072, TOK])
    GSbuf, Pbuf = scrd("GSbuf", [3072, TOK]), scrd("Pbuf", [3072, TOK])
    dbgf = nc.dram_tensor("dbgf", [128, 8, 512], F32, kind="ExternalOutput").ap() if dbg else None
    dbgb = nc.dram_tensor("dbgb", [128, 8, 512], BF16, kind="ExternalOutput").ap() if dbg else None
    x3T = nc.dram_tensor("x3T", [D, TOK], F32, kind="ExternalOutput").ap() if dbg else None
    cc_i = [scr(f"cc_i{i}", [1, 2], F32) for i in range(4)]
    cc_o = [scr(f"cc_o{i}", [1, 2], F32) for i in range(4)]

    es = contextlib.ExitStack()
    with es:
        sbt = lambda name, shape, dt: es.enter_context(nc.sbuf_tensor(name, shape, dt))
        xres = sbt("xres", [128, FC, TOK], F32)
        arena = sbt("arena", [128, 54272], U8)
        f0 = sbt("f0", [128, T], F32)
        f1 = sbt("f1", [128, T], F32)
        f2 = sbt("f2", [128, T], F32)
        bc0 = sbt("bc0", [128, T], F32)
        bc1 = sbt("bc1", [128, T], F32)
        sp0 = sbt("sp0", [128, T], BF16)
        sp1 = sbt("sp1", [128, T], BF16)
        s32 = sbt("s32", [32, 64], F32)
        tin = sbt("tin", [128, 32], F32)
        carry = sbt("carry", [128, FC], F32)
        cols = sbt("cols", [128, 32], F32)
        cols2 = sbt("cols2", [128, 36], F32)
        cols3 = sbt("cols3", [128, 32], F32)
        cols4 = sbt("cols4", [128, 32], F32)
        cells = sbt("cells", [1, 64], F32)
        ccol = sbt("ccol", [128, 8], F32)
        gcol = sbt("gcol", [128, FC], F32)
        bzc = sbt("bzc", [128, FC], F32)
        bhc = sbt("bhc", [128, FC], F32)
        bgc = sbt("bgc", [128, FC], F32)
        onesb = sbt("onesb", [128, 1], BF16)
        onef = sbt("onef", [1, 1], F32)
        onecol = sbt("onecol", [128, 1], F32)
        eyef = sbt("eyef", [128, 128], F32)
        zcol = sbt("zcol", [128, 1], F32)
        mgcol = sbt("mgcol", [128, 1], F32)
        rowsA = sbt("rowsA", [1, 6, T], F32)
        PS = es.enter_context(nc.psum_tensor("psum", [128, 8, 512], F32))

        def AR(off, shape, dt):
            n = int(np.prod(shape)) * (2 if dt == BF16 else 4)
            v = arena[:, off:off + n].bitcast(dt)
            if len(shape) == 1:
                return v
            pat = " ".join(f"d{i}" for i in range(len(shape)))
            return v.rearrange(f"p ({pat}) -> p {pat}",
                               **{f"d{i}": s for i, s in enumerate(shape)})

        b = B(nc, es)
        for s in ("dS", "dA", "dG", "mm", "vv", "aa", "gg", "bar"):
            b.mksem(s)

        def dma(q, out, in_, **kw):
            sem = {"sync": "dS", "scalar": "dA", "gpsimd": "dG"}[q]
            return b.put(q, lambda e, o=out, i=in_, kw=kw: e.dma_start(out=o, in_=i, **kw),
                         sem, 16)

        def dma_tr(q, out, in_):
            sem = {"sync": "dS", "scalar": "dA", "gpsimd": "dG"}[q]
            return b.put(q, lambda e, o=out, i=in_: e.dma_start_transpose(o, i), sem, 16)

        V = lambda make, sem="vv": b.put("vector", make, sem)
        A = lambda make, sem="aa": b.put("scalar", make, sem)
        Tn = lambda make, sem=None: b.put("tensor", make, sem)
        VW = lambda: b.wait("vector", "vv", b.val("vv"))

        def barrier():
            tgt = b.val("bar") + 4
            b.put("vector", lambda e: e.memset(f2[0:1, 0:1], 0.0), "bar")
            b.put("scalar", lambda e: e.copy(f2[0:1, 1:2], f2[0:1, 1:2]), "bar")
            b.put("tensor", lambda e: e.matmul(PS[0:1, 7, 0:1], onesb[0:1, 0:1],
                                               onesb[0:1, 0:1], start=True, stop=True), "bar")
            b.put("gpsimd", lambda e: e.memset(f2[0:1, 2:3], 0.0), "bar")
            marks = {q: b.val(q) for q in ("dS", "dA", "dG")}
            for e in ENGS:
                b.wait(e, "bar", tgt)
                for q, v in marks.items():
                    b.wait(e, q, v)

        def fold128_to_cell(col, cell_idx, op):
            b.wait("gpsimd", "vv", b.val("vv"))
            t = dma("gpsimd", f2[0:64, 0:1], col[64:128, 0:1])
            b.wait("vector", "dG", t)
            V(lambda e: e.tensor_tensor(f2[0:64, 1:2], col[0:64, 0:1], f2[0:64, 0:1], op))
            b.wait("gpsimd", "vv", b.val("vv"))
            t = dma("gpsimd", s32[0:32, 0:1], f2[32:64, 1:2])
            b.wait("vector", "dG", t)
            V(lambda e: e.tensor_tensor(s32[0:32, 32:33], f2[0:32, 1:2], s32[0:32, 0:1], op))
            VW()
            V(lambda e: e.transpose(s32[0:32, 0:32], s32[0:32, 32:64]))
            VW()
            V(lambda e: e.tensor_reduce(cells[0:1, cell_idx:cell_idx + 1],
                                        s32[0:1, 0:32], AX, op))
            VW()

        def cell_to_col(cell_idx, col_out):
            return dma("gpsimd", col_out,
                       cells[0:1, cell_idx:cell_idx + 1].unsqueeze(1).broadcast_to([1, 128, 1]),
                       allow_slow_non_contiguous=True)

        def bcast_row(row, tile_out, n=T):
            return dma("gpsimd", tile_out, row.unsqueeze(1).broadcast_to([1, 128, n]))

        def row_pieces_to_tin(row, base):
            last = None
            for tg in range(4):
                for i in range(4):
                    p = 32 * i + base + tg
                    last = dma("gpsimd", tin[p:p + 1, 0:32],
                               row[0:1, tg * 128 + 32 * i:tg * 128 + 32 * i + 32])
            return last

        def tin_rowslices_from_tr(tr, col_base, row_out):
            last = None
            for tg in range(4):
                for i in range(4):
                    p = 32 * i + col_base + tg
                    last = dma("gpsimd", row_out[0:1, tg * 128 + 32 * i:tg * 128 + 32 * i + 32],
                               tr[p:p + 1, 0:32])
            return last

        R_S1, R_S2, R_M, R_R, R_C, R_T = range(6)
        R_MX, R_MN = R_S1, R_S2
        rw = lambda i: rowsA[0:1, i, :]

        for c4 in range(4):
            dma("sync", xres[:, 2 * c4:2 * c4 + 2, :],
                xT.rearrange("(c p) t -> p c t", p=128)[:, 2 * c4:2 * c4 + 2, :])
        dma("sync", gcol[:], gamma.rearrange("o (c p) -> p c", p=128), allow_slow_non_contiguous=True)
        dma("sync", bzc[:], bzd.rearrange("o (c p) -> p c", p=128), allow_slow_non_contiguous=True)
        dma("sync", bhc[:], bhd.rearrange("o (c p) -> p c", p=128), allow_slow_non_contiguous=True)
        dma("sync", bgc[:], bgd.rearrange("o (c p) -> p c", p=128), allow_slow_non_contiguous=True)
        V(lambda e: e.memset(onesb[:], 1.0), None)
        V(lambda e: e.memset(onef[:], 1.0), None)
        V(lambda e: e.memset(onecol[:], 1.0), None)
        V(lambda e: e.memset(zcol[:], 0.0), None)
        V(lambda e: e.memset(mgcol[:], MG), None)
        V(lambda e: e.memset(cells[:], 0.0), None)
        V(lambda e: e.memset(s32[:], 0.0), None)
        V(lambda e: e.memset(cells[0:1, C_EPS:C_EPS + 1], 1e-5), None)


        stg_ = [AR(0, (4096,), F32), AR(16384, (4096,), F32)]
        stgb_ = [AR(32768, (4096,), BF16), AR(40960, (4096,), BF16)]
        stg = stg_[0]
        stgb = stgb_[0]

        def prep_cast(src, dst, ncols):
            cmark, smark = [], []
            for c in range(src.shape[0] // 128):
                r = c % 2
                if c >= 2:
                    b.wait("sync", "aa", cmark[c - 2])
                else:
                    b.wait("sync", "aa", b.val("aa"))
                    b.wait("sync", "vv", b.val("vv"))
                t1 = dma("sync", stg_[r][0:128, 0:ncols], src[c * 128:(c + 1) * 128, 0:ncols])
                b.wait("scalar", "dS", t1)
                b.wait("scalar", "dA", smark[c - 2] if c >= 2 else b.val("dA"))
                A(lambda e, n=ncols, r=r: e.copy(stgb_[r][0:128, 0:n], stg_[r][0:128, 0:n]))
                cmark.append(b.val("aa"))
                b.wait("scalar", "aa", cmark[-1])
                smark.append(dma("scalar", dst[c * 128:(c + 1) * 128, 0:ncols],
                                 stgb_[r][0:128, 0:ncols]))

        _DO = lambda p: p in phases
        if _DO('prep'):
            prep_cast(qkvaT, qkva_b, 3 * D)
            prep_cast(wzT, wz_b, D)
            prep_cast(whT, wh_b, D)
            prep_cast(wgn, wg_b, D)

        def prep_ternary(src, dst, ci, ncols):
            nch = src.shape[0] // 128
            V(lambda e: e.memset(cols[:, 0:1], 0.0))
            rmark = []
            for c in range(nch):
                r = c % 2
                if c >= 2:
                    b.wait("sync", "vv", rmark[c - 2])
                else:
                    b.wait("sync", "vv", b.val("vv"))
                    b.wait("sync", "aa", b.val("aa"))
                t1 = dma("sync", stg_[r][0:128, 0:ncols], src[c * 128:(c + 1) * 128, 0:ncols])
                b.wait("vector", "dS", t1)
                V(lambda e, n=ncols, r=r: e.tensor_reduce(
                    cols[:, 1:2], stg_[r][0:128, 0:n], AX,
                    Alu.add, apply_absolute_value=True))
                rmark.append(b.val("vv"))
                VW()
                V(lambda e: e.tensor_tensor(cols[:, 0:1], cols[:, 0:1], cols[:, 1:2],
                                            Alu.add))
            fold128_to_cell(cols[:, 0:1], C_SUM + ci, Alu.add)
            Nw = float(nch * 128 * ncols)
            V(lambda e, Nw=Nw, ci=ci: e.tensor_scalar(
                cells[0:1, ci:ci + 1], cells[0:1, C_SUM + ci:C_SUM + ci + 1],
                1.0 / Nw, 1e-5, Alu.mult, Alu.max))
            VW()
            V(lambda e, ci=ci: e.reciprocal(cells[0:1, C_IMG0 + ci:C_IMG0 + ci + 1],
                                            cells[0:1, ci:ci + 1]))
            b.wait("gpsimd", "vv", b.val("vv"))
            t = cell_to_col(C_IMG0 + ci, ccol[:, ci:ci + 1])
            b.wait("vector", "dG", t)
            qmark, spmark = [], []
            for c in range(nch):
                r = c % 2
                b.wait("sync", "vv", qmark[c - 2] if c >= 2 else b.val("vv"))
                t1 = dma("sync", stg_[r][0:128, 0:ncols], src[c * 128:(c + 1) * 128, 0:ncols])
                b.wait("vector", "dS", t1)
                b.wait("vector", "dA", spmark[c - 2] if c >= 2 else b.val("dA"))
                V(lambda e, n=ncols, ci=ci, r=r: e.tensor_scalar(
                    stg_[r][0:128, 0:n], stg_[r][0:128, 0:n], ccol[:, ci:ci + 1], MG,
                    Alu.mult, Alu.add), None)
                V(lambda e, n=ncols, r=r: e.tensor_scalar(
                    stg_[r][0:128, 0:n], stg_[r][0:128, 0:n], MG + 1.0, -MG,
                    Alu.min, Alu.add), None)
                V(lambda e, n=ncols, r=r: e.tensor_scalar(
                    stgb_[r][0:128, 0:n], stg_[r][0:128, 0:n], -1.0, None, Alu.max))
                qmark.append(b.val("vv"))
                b.wait("scalar", "vv", qmark[-1])
                spmark.append(dma("scalar", dst[c * 128:(c + 1) * 128, 0:ncols],
                                  stgb_[r][0:128, 0:ncols]))

        if _DO('prep'):
            prep_ternary(l0g, l0g_q, C_MG0, 4096)
            prep_ternary(l0u, l0u_q, C_MU0, 4096)
            prep_ternary(l0d, l0d_q, C_MWD0, D)

        def prep_lpbq(src, dst, ncols):
            nch = src.shape[0] // 128
            nblk = ncols // 32
            smax = AR(49152, (nblk,), F32)
            sinv = AR(49152 + 4 * nblk, (nblk,), F32)
            qmark, spmark = [], []
            for c in range(nch):
                r = c % 2
                if c >= 2:
                    b.wait("sync", "vv", qmark[c - 2])
                else:
                    b.wait("sync", "vv", b.val("vv"))
                    b.wait("sync", "aa", b.val("aa"))
                t1 = dma("sync", stg_[r][0:128, 0:ncols], src[c * 128:(c + 1) * 128, 0:ncols])
                b.wait("vector", "dS", t1)
                b.wait("vector", "dA", spmark[c - 2] if c >= 2 else b.val("dA"))
                V(lambda e, n=ncols, nb=nblk, r=r: e.tensor_reduce(
                    smax[0:128, 0:nb],
                    stg_[r][0:128, 0:n].rearrange("p (g c) -> p g c", c=32),
                    AX, Alu.max, apply_absolute_value=True))
                VW()
                V(lambda e, nb=nblk: e.tensor_scalar(
                    smax[0:128, 0:nb], smax[0:128, 0:nb], 1.0 / 7.0, 1e-12,
                    Alu.mult, Alu.max))
                VW()
                V(lambda e, nb=nblk: e.reciprocal(sinv[0:128, 0:nb], smax[0:128, 0:nb]))
                VW()
                V(lambda e, n=ncols, nb=nblk, r=r: e.tensor_tensor(
                    stg_[r][0:128, 0:n].rearrange("p (g c) -> p g c", c=32),
                    stg_[r][0:128, 0:n].rearrange("p (g c) -> p g c", c=32),
                    sinv[0:128, 0:nb].unsqueeze(2).broadcast_to([128, nb, 32]),
                    Alu.mult), None)
                V(lambda e, n=ncols, r=r: e.tensor_scalar(
                    stg_[r][0:128, 0:n], stg_[r][0:128, 0:n], MG, MG + 7.0,
                    Alu.add, Alu.min), None)
                V(lambda e, n=ncols, r=r: e.tensor_scalar(
                    stg_[r][0:128, 0:n], stg_[r][0:128, 0:n], -MG, -8.0,
                    Alu.add, Alu.max), None)
                V(lambda e, n=ncols, nb=nblk, r=r: e.tensor_tensor(
                    stgb_[r][0:128, 0:n].rearrange("p (g c) -> p g c", c=32),
                    stg_[r][0:128, 0:n].rearrange("p (g c) -> p g c", c=32),
                    smax[0:128, 0:nb].unsqueeze(2).broadcast_to([128, nb, 32]),
                    Alu.mult))
                qmark.append(b.val("vv"))
                b.wait("scalar", "vv", qmark[-1])
                spmark.append(dma("scalar", dst[c * 128:(c + 1) * 128, 0:ncols],
                                  stgb_[r][0:128, 0:ncols]))

        if _DO('prep'):
            prep_lpbq(l1g, l1g_b, D)
            prep_lpbq(l1u, l1u_b, D)
            prep_lpbq(l1d, l1d_b, 3072)
        barrier()

        def ln_stats(tile, xb, xs, rms=False, maxmin=False):
            ts_ = slice(tile * T, (tile + 1) * T)
            b.wait("scalar", "mm", b.val("mm"))
            b.wait("scalar", "vv", b.val("vv"))
            for ch in range(FC):
                A(lambda e, ch=ch, t_=ts_, xb=xb: e.copy(xb[:, ch, :], xres[:, ch, t_]), None)
                A(lambda e, ch=ch, t_=ts_, xs=xs: e.activation(xs[:, ch, :],
                                                               xres[:, ch, t_], Act.Square))
            b.wait("tensor", "aa", b.val("aa"))
            for ch in range(FC):
                Tn(lambda e, ch=ch, xb=xb: e.matmul(PS[0:1, 6, :], onesb[:], xb[:, ch, :],
                                                    start=(ch == 0), stop=(ch == FC - 1)), None)
            for ch in range(FC):
                Tn(lambda e, ch=ch, xs=xs: e.matmul(PS[0:1, 7, :], onesb[:], xs[:, ch, :],
                                                    start=(ch == 0), stop=(ch == FC - 1)),
                   "mm" if ch == FC - 1 else None)
            b.wait("scalar", "mm", b.val("mm"))
            A(lambda e: e.copy(rw(R_S1), PS[0:1, 6, :]), None)
            A(lambda e: e.copy(rw(R_S2), PS[0:1, 7, :]))
            b.wait("vector", "aa", b.val("aa"))
            V(lambda e: e.tensor_scalar(rw(R_M), rw(R_S1), 1.0 / D, None, Alu.mult), None)
            V(lambda e: e.tensor_tensor(rw(R_T), rw(R_M), rw(R_M), Alu.mult), None)
            V(lambda e: e.scalar_tensor_tensor(rw(R_T), rw(R_S2), 1.0 / D, rw(R_T),
                                               Alu.mult, Alu.subtract))
            b.wait("scalar", "vv", b.val("vv"))
            A(lambda e: e.activation(rw(R_C), rw(R_T), Act.Sqrt, bias=cells[0:1, C_EPS:C_EPS + 1], scale=onef[:]))
            b.wait("vector", "aa", b.val("aa"))
            V(lambda e: e.reciprocal(rw(R_R), rw(R_C)))
            if rms:
                b.wait("scalar", "vv", b.val("vv"))
                A(lambda e: e.activation(rw(R_C), rw(R_T), Act.Sqrt, bias=0.0, scale=1.0))
                b.wait("vector", "aa", b.val("aa"))
                V(lambda e: e.scalar_tensor_tensor(rw(R_C), rw(R_C), 32.0, rw(R_R),
                                                   Alu.mult, Alu.mult), None)
                V(lambda e: e.tensor_scalar(rw(R_C), rw(R_C), 1e-12, None, Alu.max), None)
                V(lambda e: e.reciprocal(rw(R_T), rw(R_C)), None)
                V(lambda e: e.scalar_tensor_tensor(rw(R_R), rw(R_R), 32.0, rw(R_T),
                                                   Alu.mult, Alu.mult))
            if maxmin:
                V(lambda e, t_=ts_: e.tensor_reduce(
                    f0[:], xres[:, :, t_].transpose([0, 2, 1]), AX, Alu.max), None)
                V(lambda e, t_=ts_: e.tensor_reduce(
                    f1[:], xres[:, :, t_].transpose([0, 2, 1]), AX, Alu.min))
                b.wait("gpsimd", "vv", b.val("vv"))
                t = dma("gpsimd", f2[0:64, :], f0[64:128, :])
                b.wait("vector", "dG", t)
                V(lambda e: e.tensor_max(f0[0:64, :], f0[0:64, :], f2[0:64, :]))
                b.wait("gpsimd", "vv", b.val("vv"))
                t = dma("gpsimd", f2[0:64, :], f1[64:128, :])
                b.wait("vector", "dG", t)
                V(lambda e: e.tensor_tensor(f1[0:64, :], f1[0:64, :], f2[0:64, :], Alu.min))
                b.wait("gpsimd", "vv", b.val("vv"))
                t = dma("gpsimd", f2[0:32, :], f0[32:64, :])
                b.wait("vector", "dG", t)
                V(lambda e: e.tensor_max(f0[0:32, :], f0[0:32, :], f2[0:32, :]))
                b.wait("gpsimd", "vv", b.val("vv"))
                t = dma("gpsimd", f2[0:32, :], f1[32:64, :])
                b.wait("vector", "dG", t)
                V(lambda e: e.tensor_tensor(f1[0:32, :], f1[0:32, :], f2[0:32, :],
                                            Alu.min))
                VW()
                V(lambda e: e.transpose(f2[0:32, :], f0[0:32, :]))
                VW()
                V(lambda e: e.tensor_reduce(
                    s32[0:32, 0:NG], f2[0:32, :].rearrange("p (g c) -> p g c", c=32),
                    AX, Alu.max))
                VW()
                V(lambda e: e.transpose(s32[0:32, 32:64], s32[0:32, 0:32]))
                b.wait("gpsimd", "vv", b.val("vv"))
                tmx = dma("gpsimd", rw(R_MX).rearrange("o (g a) -> o g a", a=32),
                          s32[0:NG, 32:64])
                b.wait("vector", "dG", tmx)
                V(lambda e: e.transpose(f2[0:32, :], f1[0:32, :]))
                VW()
                V(lambda e: e.tensor_reduce(
                    s32[0:32, 0:NG], f2[0:32, :].rearrange("p (g c) -> p g c", c=32),
                    AX, Alu.min))
                VW()
                V(lambda e: e.transpose(s32[0:32, 32:64], s32[0:32, 0:32]))
                b.wait("gpsimd", "vv", b.val("vv"))
                tmn = dma("gpsimd", rw(R_MN).rearrange("o (g a) -> o g a", a=32),
                          s32[0:NG, 32:64])
                b.wait("vector", "dG", tmn)

        def l0_attn():
            wq = [AR(0, (FC, 512), BF16), AR(8192, (FC, 512), BF16)]
            hb = AR(16384, (FC, T), BF16)
            xbst = hb
            yy = AR(24576, (FC, T), F32)
            xsst = AR(40960, (FC, T), BF16)
            kvb = [AR(40960 + 2048 * i, (T,), F32) for i in range(4)]
            ab = [AR(49152 + 2048 * i, (T,), F32) for i in range(2)]
            V(lambda e: e.memset(yy[:, :, T - 1:T], 0.0))
            wst = {"n": 0, "rel": []}

            def load_ob(ob):
                buf = wst["n"] % 2
                q, sem = (("sync", "dS"), ("gpsimd", "dG"))[buf]
                if wst["n"] >= 2:
                    b.wait(q, "mm", wst["rel"][wst["n"] - 2])
                t = dma(q, wq[buf][:],
                        qkva_b.rearrange("(kc p) o -> p kc o", p=128)[:, :, ob * 512:(ob + 1) * 512])
                return buf, t, sem

            for t in range(NT):
                ts_ = slice(t * T, (t + 1) * T)
                ln_stats(t, xbst, xsst, rms=True)
                b.wait("gpsimd", "vv", b.val("vv"))
                bcast_row(rw(R_M), bc0[:])
                tb = bcast_row(rw(R_R), bc1[:])
                b.wait("vector", "dG", tb)
                if dbg and t == 1:
                    dma("gpsimd", dbgf[:, 0, :], bc0[:])
                    td = dma("gpsimd", dbgf[:, 1, :], bc1[:])
                    b.wait("vector", "dG", td)
                for ch in range(FC):
                    V(lambda e, ch=ch, t_=ts_: e.tensor_tensor(
                        f0[:], xres[:, ch, t_], bc0[:], Alu.subtract), None)
                    V(lambda e, ch=ch: e.scalar_tensor_tensor(
                        hb[:, ch, :], f0[:], gcol[:, ch:ch + 1], bc1[:],
                        Alu.mult, Alu.mult))
                vv_h = b.val("vv")
                if dbg and t == 1:
                    b.wait("gpsimd", "vv", vv_h)
                    dma("gpsimd", dbgf[:, 2, :], hb[:, 0, :])

                def mm_block(ob, wait_aa=None, wait_vv=None):
                    buf, tl, lsem = load_ob(ob)
                    b.wait("tensor", lsem, tl)
                    b.wait("tensor", "vv", max(vv_h, wait_vv or 0))
                    if wait_aa:
                        b.wait("tensor", "aa", wait_aa)
                    marks = []
                    for j in range(4):
                        for kc in range(FC):
                            Tn(lambda e, kc=kc, j=j, buf=buf: e.matmul(
                                PS[:, j, :], wq[buf][:, kc, j * 128:(j + 1) * 128],
                                hb[:, kc, :], start=(kc == 0), stop=(kc == FC - 1)),
                               "mm" if kc == FC - 1 else None)
                        marks.append(b.val("mm"))
                    wst["n"] += 1
                    wst["rel"].append(marks[-1])
                    return marks

                aa_prev, vv_prev = None, None
                for half in range(2):
                    mk_kv = mm_block(2 + half, wait_aa=aa_prev, wait_vv=vv_prev)
                    if vv_prev:
                        b.wait("scalar", "vv", vv_prev)
                    for j in range(4):
                        b.wait("scalar", "mm", mk_kv[j])
                        A(lambda e, j=j: e.copy(kvb[j][0:128, :], PS[:, j, :]))
                    aa_kv = b.val("aa")
                    mk_a = mm_block(4 + half, wait_aa=aa_kv)
                    scan_marks = []
                    for j in range(4):
                        ch = half * 4 + j
                        b.wait("scalar", "mm", mk_a[j])
                        if j >= 2:
                            b.wait("scalar", "vv", scan_marks[j - 2])
                        A(lambda e, j=j: e.activation(ab[j % 2][0:128, :], PS[:, j, :],
                                                      Act.Sigmoid))
                        b.wait("vector", "aa", b.val("aa"))
                        V(lambda e, j=j, ch=ch: e.tensor_tensor_scan(
                            yy[:, ch, :], ab[j % 2][0:128, :], kvb[j][0:128, :],
                            yy[:, ch, T - 1:T], Alu.mult, Alu.add))
                        scan_marks.append(b.val("vv"))
                        if dbg and t == 1 and ch == 0:
                            b.wait("gpsimd", "vv", b.val("vv"))
                            dma("gpsimd", dbgf[:, 3, :], kvb[j][0:128, :])
                            dma("gpsimd", dbgf[:, 4, :], ab[j % 2][0:128, :])
                            td = dma("gpsimd", dbgf[:, 5, :], yy[:, 0, :])
                            b.wait("scalar", "dG", td)
                    aa_prev, vv_prev = b.val("aa"), b.val("vv")
                for half in range(2):
                    mk_q = mm_block(half, wait_aa=aa_prev, wait_vv=vv_prev)
                    for j in range(4):
                        ch = half * 4 + j
                        b.wait("vector", "mm", mk_q[j])
                        V(lambda e, j=j, ch=ch: e.scalar_tensor_tensor(
                            f0[:], PS[:, j, :], 1.0, yy[:, ch, :], Alu.mult, Alu.mult), None)
                        V(lambda e, ch=ch, t_=ts_: e.tensor_tensor(
                            xres[:, ch, t_], xres[:, ch, t_], f0[:], Alu.add))
                        if dbg and t == 1 and ch == 0:
                            b.wait("gpsimd", "vv", b.val("vv"))
                            td = dma("gpsimd", dbgf[0:1, 6, :],
                                     rowsA[0:1, :, :].rearrange("o a b -> o (a b)")[0:1, 0:512])
                            b.wait("scalar", "dG", td)
                    aa_prev, vv_prev = b.val("aa"), b.val("vv")

        if _DO('l0a'):
            l0_attn()
        barrier()

        def l0_ffn(do_sweep2=True):
            xq = AR(0, (4, FC, T), BF16)
            xbst = AR(32768, (FC, T), BF16)
            xsst = AR(40960, (FC, T), BF16)
            wg_ = [AR(32768, (FC, 512), BF16)]
            wu_ = [AR(40960, (FC, 512), BF16)]
            vt = [AR(49152 + 1024 * i, (T,), BF16) for i in range(2)]
            vq = [AR(51200 + 1024 * i, (T,), BF16) for i in range(2)]
            wk = [AR(32768 + 2048 * i, (D,), BF16) for i in range(2)]

            for grp in range(2):
                tiles = list(range(grp * 4, grp * 4 + 4))
                b.wait("vector", "mm", b.val("mm"))
                for ti, t in enumerate(tiles):
                    ts_ = slice(t * T, (t + 1) * T)
                    ln_stats(t, xbst, xsst, maxmin=True)
                    V(lambda e: e.tensor_tensor(rw(R_MX), rw(R_MX), rw(R_M),
                                                Alu.subtract), None)
                    V(lambda e: e.tensor_tensor(rw(R_MN), rw(R_M), rw(R_MN),
                                                Alu.subtract), None)
                    V(lambda e: e.tensor_max(rw(R_MX), rw(R_MX), rw(R_MN)), None)
                    V(lambda e: e.tensor_tensor(rw(R_C), rw(R_R), rw(R_MX), Alu.mult), None)
                    V(lambda e: e.tensor_scalar(rw(R_C), rw(R_C), 1e-5, None, Alu.max), None)
                    V(lambda e: e.reciprocal(rw(R_T), rw(R_C)), None)
                    V(lambda e: e.tensor_scalar(rw(R_MN), rw(R_C),
                                                cells[0:1, C_MG0:C_MG0 + 1], 1.0 / 127.0,
                                                Alu.mult, Alu.mult), None)
                    V(lambda e: e.tensor_scalar(rw(R_MX), rw(R_C),
                                                cells[0:1, C_MU0:C_MU0 + 1], 1.0 / 127.0,
                                                Alu.mult, Alu.mult), None)
                    V(lambda e: e.scalar_tensor_tensor(rw(R_C), rw(R_T), 127.0, rw(R_R),
                                                       Alu.mult, Alu.mult))
                    b.wait("gpsimd", "vv", b.val("vv"))
                    row_pieces_to_tin(rw(R_MN), ti * 8)
                    row_pieces_to_tin(rw(R_MX), ti * 8 + 4)
                    bcast_row(rw(R_M), bc0[:])
                    tb = bcast_row(rw(R_C), bc1[:])
                    b.wait("vector", "dG", tb)
                    for ch in range(FC):
                        V(lambda e, ch=ch, t_=ts_: e.tensor_tensor(
                            f0[:], xres[:, ch, t_], bc0[:], Alu.subtract), None)
                        V(lambda e: e.tensor_tensor(f0[:], f0[:], bc1[:], Alu.mult), None)
                        V(lambda e: e.tensor_scalar(f0[:], f0[:], MG, MG + 127.0,
                                                    Alu.add, Alu.min), None)
                        V(lambda e, ti=ti, ch=ch: e.tensor_scalar(
                            xq[:, ti, ch, :], f0[:], -MG, -128.0, Alu.add, Alu.max))
                V(lambda e, g=grp: e.memset(cols2[:, g * 16:g * 16 + 16], 0.0), None)
                b.wait("vector", "dG", b.val("dG"))
                V(lambda e: e.transpose(cols[0:128, 0:32], tin[:]))
                if dbg and grp == 0:
                    b.wait("gpsimd", "vv", b.val("vv"))
                    dma("gpsimd", dbgf[:, 4, 0:32], cols[:, 0:32])
                    td = dma("gpsimd", dbgf[:, 5, 0:32], tin[:, 0:32])
                    b.wait("sync", "dG", td)
                vv_xq = b.val("vv")
                b.wait("sync", "vv", vv_xq)
                b.wait("scalar", "vv", vv_xq)
                wst = {"n": 0, "rel": []}
                ep = []
                it = 0
                for og in range(8):
                    buf = 0
                    if wst["n"] >= 1:
                        b.wait("sync", "mm", wst["rel"][-1])
                    dma("sync", wg_[buf][:],
                        l0g_q.rearrange("(kc p) o -> p kc o", p=128)[:, :, og * 512:(og + 1) * 512])
                    tu_ = dma("sync", wu_[buf][:],
                              l0u_q.rearrange("(kc p) o -> p kc o", p=128)[:, :, og * 512:(og + 1) * 512])
                    wst["n"] += 1
                    b.wait("tensor", "dS", tu_)
                    b.wait("tensor", "vv", vv_xq)
                    for ti, t in enumerate(tiles):
                        for tg in range(4):
                            par = it % 2
                            bk_g, bk_u = 2 * par, 2 * par + 1
                            if it >= 2:
                                b.wait("tensor", "aa", ep[it - 2][0])
                                b.wait("tensor", "vv", ep[it - 2][1])
                            for kc in range(FC):
                                Tn(lambda e, kc=kc, ti=ti, tg=tg, buf=buf, bk=bk_g: e.matmul(
                                    PS[:, bk, :], xq[:, ti, kc, tg * 128:(tg + 1) * 128],
                                    wg_[buf][:, kc, :], start=(kc == 0), stop=(kc == FC - 1)),
                                   "mm" if kc == FC - 1 else None)
                            for kc in range(FC):
                                Tn(lambda e, kc=kc, ti=ti, tg=tg, buf=buf, bk=bk_u: e.matmul(
                                    PS[:, bk, :], xq[:, ti, kc, tg * 128:(tg + 1) * 128],
                                    wu_[buf][:, kc, :], start=(kc == 0), stop=(kc == FC - 1)),
                                   "mm" if kc == FC - 1 else None)
                            mk = b.val("mm")
                            ci = ti * 8 + tg
                            b.wait("scalar", "mm", mk)
                            A(lambda e, bk=bk_g, ci=ci: e.activation(
                                f0[:], PS[:, bk, :], Act.Silu, bias=zcol[:],
                                scale=cols[:, ci:ci + 1]))
                            b.wait("vector", "aa", b.val("aa"))
                            V(lambda e, bk=bk_u, ci=ci: e.scalar_tensor_tensor(
                                f1[:], PS[:, bk, :], cols[:, ci + 4:ci + 5], f0[:],
                                Alu.mult, Alu.mult), None)
                            cx = grp * 16 + ti * 4 + tg
                            V(lambda e: e.tensor_reduce(f2[:, 0:1], f1[:], AX, Alu.max,
                                                        apply_absolute_value=True))
                            VW()
                            V(lambda e, cx=cx: e.tensor_max(
                                cols2[:, cx:cx + 1], cols2[:, cx:cx + 1], f2[:, 0:1]))
                            vv_ep = b.val("vv")
                            b.wait("scalar", "vv", vv_ep)
                            b.wait("scalar", "dA", b.val("dA"))
                            A(lambda e: e.copy(sp0[:], f1[:]))
                            aa_ep = b.val("aa")
                            b.wait("scalar", "aa", aa_ep)
                            dma("scalar",
                                Vbuf[t * T + tg * 128:t * T + (tg + 1) * 128,
                                     og * 512:(og + 1) * 512], sp0[:])
                            ep.append((aa_ep, vv_ep))
                            it += 1
                    wst["rel"].append(b.val("mm"))
            barrier()
            if not do_sweep2:
                return
            V(lambda e: e.transpose(tin[:], cols2[0:128, 0:32]))
            komark = []
            for t in range(NT):
                ts_ = slice(t * T, (t + 1) * T)
                b.wait("tensor", "vv", b.val("vv"))
                b.wait("gpsimd", "vv", b.val("vv"))
                tpz = tin_rowslices_from_tr(tin, t * 4, rw(R_T))
                b.wait("vector", "dG", tpz)
                V(lambda e: e.tensor_scalar(rw(R_T), rw(R_T), 1e-5, None, Alu.max), None)
                V(lambda e: e.reciprocal(rw(R_C), rw(R_T)), None)
                V(lambda e: e.tensor_scalar(rw(R_C), rw(R_C), 127.0, None, Alu.mult), None)
                V(lambda e: e.tensor_scalar(rw(R_T), rw(R_T),
                                            cells[0:1, C_MWD0:C_MWD0 + 1], 1.0 / 127.0,
                                            Alu.mult, Alu.mult))
                b.wait("gpsimd", "vv", b.val("vv"))
                bcast_row(rw(R_C), bc0[:])
                tb = bcast_row(rw(R_T), bc1[:])
                b.wait("vector", "dG", tb)
                if dbg and t == 0:
                    dma("gpsimd", dbgf[:, 6, :], bc0[:])
                    td = dma("gpsimd", dbgf[:, 7, :], bc1[:])
                    b.wait("sync", "dG", td)
                for ko in range(32):
                    r = ko % 2
                    if len(komark) >= 2:
                        b.wait("sync", "vv", komark[-2][1])
                        b.wait("sync", "mm", komark[-2][0])
                        b.wait("vector", "mm", komark[-2][0])
                    tkw = dma("sync", wk[r][0:128, :],
                              l0d_q[ko * 128:(ko + 1) * 128, :].rearrange(
                                  "(o p) d -> p (o d)", p=128))
                    tv = None
                    for tg in range(4):
                        tv = dma_tr("sync", vt[r][0:128, tg * 128:(tg + 1) * 128],
                                    Vbuf[t * T + tg * 128:t * T + (tg + 1) * 128,
                                         ko * 128:(ko + 1) * 128])
                    b.wait("vector", "dS", tv)
                    V(lambda e, r=r: e.tensor_tensor(f0[:], vt[r][0:128, :], bc0[:],
                                                     Alu.mult), None)
                    V(lambda e: e.tensor_scalar(f0[:], f0[:], MG, MG + 127.0,
                                                Alu.add, Alu.min), None)
                    V(lambda e, r=r: e.tensor_scalar(vq[r][0:128, :], f0[:], -MG, -128.0,
                                                     Alu.add, Alu.max))
                    vvq = b.val("vv")
                    b.wait("tensor", "vv", vvq)
                    b.wait("tensor", "dS", tkw)
                    for od in range(FC):
                        Tn(lambda e, ko=ko, od=od, r=r: e.matmul(
                            PS[:, od, :],
                            wk[r][0:128, :].rearrange("p (o d) -> p o d", o=8)[:, od, :]
                            if False else wk[r][0:128, od * 128:(od + 1) * 128],
                            vq[r][0:128, :], start=(ko == 0), stop=(ko == 31)),
                           "mm" if od == FC - 1 else None)
                    komark.append((b.val("mm"), vvq))
                mk = b.val("mm")
                b.wait("vector", "mm", mk)
                for od in range(FC):
                    V(lambda e, od=od: e.tensor_tensor(f0[:], PS[:, od, :], bc1[:],
                                                       Alu.mult), None)
                    V(lambda e, od=od, t_=ts_: e.tensor_tensor(
                        xres[:, od, t_], xres[:, od, t_], f0[:], Alu.add))

        if _DO('l0f') or _DO('l0f1'):
            l0_ffn(do_sweep2=_DO('l0f'))
        barrier()

        def l1_attn():
            wz_ = [AR(0, (FC, 128), BF16), AR(2048, (FC, 128), BF16)]
            wh_ = [AR(4096, (FC, 128), BF16), AR(6144, (FC, 128), BF16)]
            wg2 = [AR(8192, (FC, 128), BF16), AR(10240, (FC, 128), BF16)]
            ub = AR(12288, (FC, T), BF16)
            xbst = ub
            yy = AR(20480, (FC, T), F32)
            hbb = AR(36864, (FC, T), BF16)
            zb = [AR(45056 + 2048 * i, (T,), F32) for i in range(2)]
            xsst = AR(45056, (FC, T), BF16)
            V(lambda e: e.memset(yy[:, :, T - 1:T], 0.0))
            for t in range(NT):
                ts_ = slice(t * T, (t + 1) * T)
                ln_stats(t, xbst, xsst)
                b.wait("gpsimd", "vv", b.val("vv"))
                bcast_row(rw(R_M), bc0[:])
                tb = bcast_row(rw(R_R), bc1[:])
                b.wait("vector", "dG", tb)
                for ch in range(FC):
                    V(lambda e, ch=ch, t_=ts_: e.tensor_tensor(
                        f0[:], xres[:, ch, t_], bc0[:], Alu.subtract), None)
                    V(lambda e, ch=ch: e.tensor_tensor(ub[:, ch, :], f0[:], bc1[:],
                                                       Alu.mult))
                vv_u = b.val("vv")
                zmark = []
                for ch in range(FC):
                    buf = ch % 2
                    if ch >= 2:
                        b.wait("sync", "mm", zmark[ch - 2][2])
                        b.wait("gpsimd", "mm", zmark[ch - 2][2])
                    t1 = dma("sync", wz_[buf][:],
                             wz_b.rearrange("(kc p) o -> p kc o", p=128)[:, :, ch * 128:(ch + 1) * 128])
                    t2 = dma("gpsimd", wh_[buf][:],
                             wh_b.rearrange("(kc p) o -> p kc o", p=128)[:, :, ch * 128:(ch + 1) * 128])
                    b.wait("tensor", "dS", t1)
                    b.wait("tensor", "dG", t2)
                    b.wait("tensor", "vv", vv_u)
                    if ch >= 2:
                        b.wait("tensor", "aa", zmark[ch - 2][0])
                        b.wait("tensor", "vv", zmark[ch - 2][1])
                    bz_, bh_ = 2 * (ch % 2), 2 * (ch % 2) + 1
                    for kc in range(FC):
                        Tn(lambda e, kc=kc, buf=buf, bk=bz_: e.matmul(
                            PS[:, bk, :], wz_[buf][:, kc, :], ub[:, kc, :],
                            start=(kc == 0), stop=(kc == FC - 1)),
                           "mm" if kc == FC - 1 else None)
                    mz = b.val("mm")
                    for kc in range(FC):
                        Tn(lambda e, kc=kc, buf=buf, bk=bh_: e.matmul(
                            PS[:, bk, :], wh_[buf][:, kc, :], ub[:, kc, :],
                            start=(kc == 0), stop=(kc == FC - 1)),
                           "mm" if kc == FC - 1 else None)
                    mh = b.val("mm")
                    b.wait("scalar", "mm", mz)
                    A(lambda e, ch=ch, bk=bz_: e.activation(
                        zb[ch % 2][0:128, :], PS[:, bk, :], Act.Sigmoid,
                        bias=bzc[:, ch:ch + 1], scale=onecol[:]))
                    aa_z = b.val("aa")
                    b.wait("vector", "aa", aa_z)
                    b.wait("vector", "mm", mh)
                    V(lambda e, ch=ch, bk=bh_: e.scalar_tensor_tensor(
                        f0[:], PS[:, bk, :], bhc[:, ch:ch + 1], zb[ch % 2][0:128, :],
                        Alu.add, Alu.mult), None)
                    V(lambda e, ch=ch: e.tensor_scalar(
                        f1[:], zb[ch % 2][0:128, :], -1.0, 1.0, Alu.mult, Alu.add), None)
                    V(lambda e, ch=ch: e.tensor_tensor_scan(
                        yy[:, ch, :], f1[:], f0[:], yy[:, ch, T - 1:T],
                        Alu.mult, Alu.add))
                    vv_z = b.val("vv")
                    b.wait("scalar", "vv", vv_z)
                    A(lambda e, ch=ch: e.copy(hbb[:, ch, :], yy[:, ch, :]))
                    zmark.append((aa_z, vv_z, mh))
                    if dbg and t == 0 and ch == 0:
                        b.wait("gpsimd", "vv", vv_z)
                        dma("gpsimd", dbgf[:, 0, :], zb[0][0:128, :])
                        dma("gpsimd", dbgf[:, 1, :], f0[:])
                        dma("gpsimd", dbgf[:, 2, :], f1[:])
                        dma("gpsimd", dbgf[:, 3, :], yy[:, 0, :])
                        td = dma("gpsimd", dbgb[:, 0, :], ub[:, 0, :])
                        b.wait("vector", "dG", td)
                aa_h = b.val("aa")
                gmark = []
                for ch in range(FC):
                    buf = ch % 2
                    q, sem = (("sync", "dS"), ("gpsimd", "dG"))[buf]
                    if ch >= 2:
                        b.wait(q, "mm", gmark[ch - 2][1])
                    t1 = dma(q, wg2[buf][:],
                             wg_b.rearrange("(kc p) o -> p kc o", p=128)[:, :, ch * 128:(ch + 1) * 128])
                    b.wait("tensor", sem, t1)
                    b.wait("tensor", "aa", aa_h)
                    if ch >= 4:
                        b.wait("tensor", "aa", gmark[ch - 4][0])
                    bk = ch % 4
                    for kc in range(FC):
                        Tn(lambda e, kc=kc, buf=buf, bk=bk: e.matmul(
                            PS[:, bk, :], wg2[buf][:, kc, :], hbb[:, kc, :],
                            start=(kc == 0), stop=(kc == FC - 1)),
                           "mm" if kc == FC - 1 else None)
                    mg_ = b.val("mm")
                    b.wait("scalar", "mm", mg_)
                    A(lambda e, ch=ch, bk=bk: e.activation(
                        f2[:], PS[:, bk, :], Act.Sigmoid, bias=bgc[:, ch:ch + 1],
                        scale=onecol[:]))
                    aa_g = b.val("aa")
                    b.wait("vector", "aa", aa_g)
                    V(lambda e, ch=ch: e.tensor_tensor(f0[:], f2[:], yy[:, ch, :],
                                                       Alu.mult), None)
                    V(lambda e, ch=ch, t_=ts_: e.tensor_tensor(
                        xres[:, ch, t_], xres[:, ch, t_], f0[:], Alu.add))
                    gmark.append((aa_g, mg_))
                    if dbg and t == 0 and ch == 0:
                        b.wait("gpsimd", "vv", b.val("vv"))
                        td = dma("gpsimd", dbgf[:, 4, :], f2[:])
                        b.wait("scalar", "dG", td)

        if _DO('l1a'):
            l1_attn()
        barrier()
        if dbg:
            for c4 in range(4):
                dma("gpsimd",
                    x3T.rearrange("(c p) t -> p c t", p=128)[:, 2 * c4:2 * c4 + 2, :],
                    xres[:, 2 * c4:2 * c4 + 2, :])

        def allreduce_max(idx_lo, pair):
            b.wait("gpsimd", "vv", b.val("vv"))
            t = dma("gpsimd", cc_i[pair], cells[0:1, idx_lo:idx_lo + 2])
            b.wait("gpsimd", "dG", t)
            b.put("gpsimd", lambda e, pr=pair: e.collective_compute(
                "AllReduce", Alu.max, replica_groups=[list(range(n_cores))],
                ins=[cc_i[pr].opt()], outs=[cc_o[pr].opt()]), "gg")
            b.wait("gpsimd", "gg", b.val("gg"))
            t = dma("gpsimd", cells[0:1, idx_lo:idx_lo + 2], cc_o[pair])
            b.wait("vector", "dG", t)
            b.wait("scalar", "dG", t)

        def l1_ffn(upto="E"):
            xq = AR(0, (4, FC, T), BF16)
            xbst = AR(32768, (FC, T), BF16)
            xsst = AR(40960, (FC, T), BF16)
            wb_ = [AR(32768, (FC, 512), BF16), AR(40960, (FC, 512), BF16)]
            wdd = AR(0, (24, D), BF16)
            oib = [AR(49152 + 1024 * i, (T,), BF16) for i in range(2)]

            V(lambda e: e.memset(cols2[:, 32:33], 0.0))
            for t in range(NT):
                ts_ = slice(t * T, (t + 1) * T)
                ln_stats(t, xbst, xsst)
                b.wait("gpsimd", "vv", b.val("vv"))
                bcast_row(rw(R_M), bc0[:])
                tb = bcast_row(rw(R_R), bc1[:])
                b.wait("vector", "dG", tb)
                for ch in range(FC):
                    V(lambda e, ch=ch, t_=ts_: e.tensor_tensor(
                        f0[:], xres[:, ch, t_], bc0[:], Alu.subtract), None)
                    V(lambda e: e.tensor_tensor(f0[:], f0[:], bc1[:], Alu.mult), None)
                    V(lambda e: e.tensor_reduce(f2[:, 0:1], f0[:], AX, Alu.max,
                                                apply_absolute_value=True))
                    VW()
                    V(lambda e: e.tensor_max(cols2[:, 32:33], cols2[:, 32:33],
                                             f2[:, 0:1]))
            fold128_to_cell(cols2[:, 32:33], C_G1, Alu.max)
            V(lambda e: e.tensor_scalar(cells[0:1, C_G1 + 1:C_G1 + 2],
                                        cells[0:1, C_G1:C_G1 + 1], 1.0, None, Alu.mult))
            allreduce_max(C_G1, 0)
            V(lambda e: e.tensor_scalar(cells[0:1, C_S1:C_S1 + 1],
                                        cells[0:1, C_G1:C_G1 + 1], 1.0 / 32767.0, 1e-12,
                                        Alu.mult, Alu.max))
            VW()
            V(lambda e: e.reciprocal(cells[0:1, C_K1:C_K1 + 1],
                                     cells[0:1, C_S1:C_S1 + 1]))
            VW()
            if upto == "A":
                return

            V(lambda e: e.memset(cols2[:, 33:35], 0.0))
            for grp in range(2):
                tiles = list(range(grp * 4, grp * 4 + 4))
                b.wait("vector", "mm", b.val("mm"))
                for ti, t in enumerate(tiles):
                    ts_ = slice(t * T, (t + 1) * T)
                    ln_stats(t, xbst, xsst)
                    V(lambda e: e.tensor_scalar(rw(R_C), rw(R_R),
                                                cells[0:1, C_K1:C_K1 + 1], None, Alu.mult))
                    b.wait("gpsimd", "vv", b.val("vv"))
                    bcast_row(rw(R_M), bc0[:])
                    tb = bcast_row(rw(R_C), bc1[:])
                    b.wait("vector", "dG", tb)
                    for ch in range(FC):
                        V(lambda e, ch=ch, t_=ts_: e.tensor_tensor(
                            f0[:], xres[:, ch, t_], bc0[:], Alu.subtract), None)
                        V(lambda e: e.tensor_tensor(f0[:], f0[:], bc1[:], Alu.mult), None)
                        V(lambda e: e.tensor_scalar(f0[:], f0[:], MG, MG + 32767.0,
                                                    Alu.add, Alu.min), None)
                        V(lambda e, ti=ti, ch=ch: e.tensor_scalar(
                            xq[:, ti, ch, :], f0[:], -MG, -32768.0, Alu.add, Alu.max))
                vv_xq = b.val("vv")
                b.wait("sync", "vv", vv_xq)
                if upto == "B0":
                    continue
                nsrc = 1 if upto.startswith("B1") else 2
                ep = []
                it = 0
                wst = {"n": 0, "rel": []}
                for wsrc, spill, mcol in ((l1u_b, Ubuf, 33), (l1g_b, Gbuf, 34))[:nsrc]:
                    for og in range(6):
                        buf = wst["n"] % 2
                        if wst["n"] >= 2:
                            b.wait("sync", "mm", wst["rel"][wst["n"] - 2])
                            b.wait("scalar", "mm", wst["rel"][wst["n"] - 2])
                        tws = twa = None
                        for kc in range(FC):
                            for j in range(4):
                                tw_ = dma_tr(("sync", "scalar")[kc % 2],
                                             wb_[buf][:, kc, j * 128:(j + 1) * 128],
                                             wsrc[og * 512 + j * 128:og * 512 + (j + 1) * 128,
                                                  kc * 128:(kc + 1) * 128])
                                if kc % 2 == 0:
                                    tws = tw_
                                else:
                                    twa = tw_
                        wst["n"] += 1
                        if upto == "B1L":
                            wst["rel"].append(b.val("mm"))
                            continue
                        b.wait("tensor", "dS", tws)
                        b.wait("tensor", "dA", twa)
                        b.wait("tensor", "vv", vv_xq)
                        for ti, t in enumerate(tiles):
                            for j in range(4):
                                bk = it % 4
                                if it >= 4 and upto != "B1M":
                                    b.wait("tensor", "aa", ep[it - 4][0])
                                    b.wait("tensor", "vv", ep[it - 4][1])
                                for kc in range(FC):
                                    Tn(lambda e, kc=kc, ti=ti, j=j, buf=buf, bk=bk: e.matmul(
                                        PS[:, bk, :], wb_[buf][:, kc, j * 128:(j + 1) * 128],
                                        xq[:, ti, kc, :], start=(kc == 0),
                                        stop=(kc == FC - 1)),
                                       "mm" if kc == FC - 1 else None)
                                mk = b.val("mm")
                                if upto == "B1M":
                                    it += 1
                                    continue
                                b.wait("vector", "mm", mk)
                                V(lambda e, bk=bk: e.tensor_reduce(
                                    f2[:, 0:1], PS[:, bk, :], AX, Alu.max,
                                    apply_absolute_value=True))
                                VW()
                                V(lambda e, mc=mcol: e.tensor_max(
                                    cols2[:, mc:mc + 1], cols2[:, mc:mc + 1], f2[:, 0:1]))
                                vv_ep = b.val("vv")
                                b.wait("scalar", "mm", mk)
                                b.wait("scalar", "vv", vv_ep)
                                b.wait("scalar", "dA", b.val("dA"))
                                A(lambda e, bk=bk: e.copy(sp0[:], PS[:, bk, :]))
                                aa_ep = b.val("aa")
                                b.wait("scalar", "aa", aa_ep)
                                dma("scalar",
                                    spill[og * 512 + j * 128:og * 512 + (j + 1) * 128,
                                          t * T:(t + 1) * T], sp0[:])
                                ep.append((aa_ep, vv_ep))
                                it += 1
                        wst["rel"].append(b.val("mm"))
            if upto == "B0" or upto.startswith("B1"):
                return
            fold128_to_cell(cols2[:, 33:34], C_G2, Alu.max)
            fold128_to_cell(cols2[:, 34:35], C_G3, Alu.max)
            V(lambda e: e.tensor_tensor(cells[0:1, C_G2:C_G2 + 1],
                                        cells[0:1, C_G2:C_G2 + 1],
                                        cells[0:1, C_S1:C_S1 + 1], Alu.mult), None)
            V(lambda e: e.tensor_tensor(cells[0:1, C_G3:C_G3 + 1],
                                        cells[0:1, C_G3:C_G3 + 1],
                                        cells[0:1, C_S1:C_S1 + 1], Alu.mult))
            allreduce_max(C_G2, 1)
            for (gi, si, ki) in ((C_G2, C_S2, C_K2), (C_G3, C_S3, C_K3)):
                V(lambda e, gi=gi, si=si: e.tensor_scalar(
                    cells[0:1, si:si + 1], cells[0:1, gi:gi + 1], 1.0 / 32767.0, 1e-12,
                    Alu.mult, Alu.max))
                VW()
                V(lambda e, si=si, ki=ki: e.reciprocal(cells[0:1, ki:ki + 1],
                                                       cells[0:1, si:si + 1]))
                VW()
                V(lambda e, ki=ki: e.tensor_tensor(
                    cells[0:1, ki:ki + 1], cells[0:1, ki:ki + 1],
                    cells[0:1, C_S1:C_S1 + 1], Alu.mult))
                VW()
            V(lambda e: e.tensor_scalar(cells[0:1, C_NS3MG:C_NS3MG + 1],
                                        cells[0:1, C_S3:C_S3 + 1], -MG, None, Alu.mult))
            b.wait("gpsimd", "vv", b.val("vv"))
            cell_to_col(C_K2, ccol[:, 3:4])
            cell_to_col(C_K3, ccol[:, 4:5])
            cell_to_col(C_S3, ccol[:, 7:8])
            tb = cell_to_col(C_NS3MG, ccol[:, 6:7])
            b.wait("scalar", "dG", tb)
            b.wait("vector", "dG", tb)
            if upto == "B":
                return

            b.wait("sync", "dA", b.val("dA"))
            b.wait("gpsimd", "dA", b.val("dA"))
            V(lambda e: e.memset(cols2[:, 35:36], 0.0))
            for oc in range(24):
                for t in range(NT):
                    r = (oc * NT + t) % 2
                    sp = (sp0, sp1)[r]
                    q, sem = (("sync", "dS"), ("gpsimd", "dG"))[r]
                    b.wait(q, "aa", b.val("aa"))
                    tl = dma(q, sp[:], Gbuf[oc * 128:(oc + 1) * 128,
                                            t * T:(t + 1) * T])
                    b.wait("scalar", sem, tl)
                    b.wait("scalar", "vv", b.val("vv"))
                    A(lambda e, sp=sp: e.activation(f2[:], sp[:], Act.Identity,
                                                    bias=mgcol[:], scale=ccol[:, 4:5]), None)
                    A(lambda e: e.activation(bc0[:], f2[:], Act.Sigmoid,
                                             bias=ccol[:, 6:7], scale=ccol[:, 7:8]))
                    b.wait("vector", "aa", b.val("aa"))
                    V(lambda e: e.tensor_scalar(f2[:], f2[:], -MG, 32767.0,
                                                Alu.add, Alu.min), None)
                    V(lambda e: e.tensor_scalar(bc1[:], bc0[:], 65536.0, MG,
                                                Alu.mult, Alu.add), None)
                    V(lambda e: e.tensor_scalar(bc1[:], bc1[:], MG + 65535.0, -MG,
                                                Alu.min, Alu.add), None)
                    V(lambda e: e.tensor_tensor(f0[:], bc1[:], f2[:], Alu.mult), None)
                    V(lambda e: e.tensor_reduce(f2[:, 0:1], f0[:], AX, Alu.max,
                                                apply_absolute_value=True))
                    VW()
                    V(lambda e: e.tensor_max(cols2[:, 35:36], cols2[:, 35:36],
                                             f2[:, 0:1]))
                    b.wait("scalar", "vv", b.val("vv"))
                    b.wait("scalar", "dA", b.val("dA"))
                    A(lambda e, sp=sp: e.copy(sp[:], f0[:]))
                    b.wait("scalar", "aa", b.val("aa"))
                    dma("scalar", GSbuf[oc * 128:(oc + 1) * 128, t * T:(t + 1) * T], sp[:])
            fold128_to_cell(cols2[:, 35:36], C_G4, Alu.max)
            V(lambda e: e.tensor_scalar(cells[0:1, C_T0:C_T0 + 1],
                                        cells[0:1, C_S3:C_S3 + 1], 1.0 / 65536.0, None,
                                        Alu.mult))
            VW()
            V(lambda e: e.tensor_tensor(cells[0:1, C_G4:C_G4 + 1],
                                        cells[0:1, C_G4:C_G4 + 1],
                                        cells[0:1, C_T0:C_T0 + 1], Alu.mult))
            VW()
            V(lambda e: e.tensor_scalar(cells[0:1, C_G4 + 1:C_G4 + 2],
                                        cells[0:1, C_G4:C_G4 + 1], 1.0, None, Alu.mult))
            allreduce_max(C_G4, 2)
            V(lambda e: e.tensor_scalar(cells[0:1, C_S4:C_S4 + 1],
                                        cells[0:1, C_G4:C_G4 + 1], 1.0 / 32767.0, 1e-12,
                                        Alu.mult, Alu.max))
            VW()
            V(lambda e: e.reciprocal(cells[0:1, C_K4:C_K4 + 1],
                                     cells[0:1, C_S4:C_S4 + 1]))
            VW()
            V(lambda e: e.tensor_tensor(cells[0:1, C_K4:C_K4 + 1],
                                        cells[0:1, C_K4:C_K4 + 1],
                                        cells[0:1, C_T0:C_T0 + 1], Alu.mult))
            b.wait("gpsimd", "vv", b.val("vv"))
            tb = cell_to_col(C_K4, ccol[:, 5:6])
            b.wait("scalar", "dG", tb)
            if upto == "C":
                return

            b.wait("sync", "dA", b.val("dA"))
            b.wait("gpsimd", "dA", b.val("dA"))
            V(lambda e: e.memset(cols2[:, 35:36], 0.0))
            for oc in range(24):
                for t in range(NT):
                    r = (oc * NT + t) % 2
                    spg, spu = (sp0, sp1) if r == 0 else (sp1, sp0)
                    b.wait("sync", "aa", b.val("aa"))
                    b.wait("gpsimd", "aa", b.val("aa"))
                    t1 = dma("sync", spg[:], GSbuf[oc * 128:(oc + 1) * 128, t * T:(t + 1) * T])
                    t2 = dma("gpsimd", spu[:], Ubuf[oc * 128:(oc + 1) * 128,
                                                    t * T:(t + 1) * T])
                    b.wait("scalar", "dS", t1)
                    b.wait("scalar", "dG", t2)
                    b.wait("scalar", "vv", b.val("vv"))
                    A(lambda e, spg=spg: e.activation(f0[:], spg[:], Act.Identity,
                                                      bias=mgcol[:], scale=ccol[:, 5:6]), None)
                    A(lambda e, spu=spu: e.activation(f1[:], spu[:], Act.Identity,
                                                      bias=mgcol[:], scale=ccol[:, 3:4]))
                    b.wait("vector", "aa", b.val("aa"))
                    V(lambda e: e.tensor_scalar(f0[:], f0[:], -MG, 32767.0,
                                                Alu.add, Alu.min), None)
                    V(lambda e: e.tensor_scalar(f1[:], f1[:], -MG, 32767.0,
                                                Alu.add, Alu.min), None)
                    V(lambda e: e.tensor_tensor(f2[:], f0[:], f1[:], Alu.mult), None)
                    V(lambda e: e.tensor_reduce(f0[:, 0:1], f2[:], AX, Alu.max,
                                                apply_absolute_value=True))
                    VW()
                    V(lambda e: e.tensor_max(cols2[:, 35:36], cols2[:, 35:36],
                                             f0[:, 0:1]))
                    b.wait("scalar", "vv", b.val("vv"))
                    b.wait("scalar", "dA", b.val("dA"))
                    A(lambda e, spg=spg: e.copy(spg[:], f2[:]))
                    b.wait("scalar", "aa", b.val("aa"))
                    dma("scalar", Pbuf[oc * 128:(oc + 1) * 128, t * T:(t + 1) * T], spg[:])
            fold128_to_cell(cols2[:, 35:36], C_G5, Alu.max)
            V(lambda e: e.tensor_tensor(cells[0:1, C_T1:C_T1 + 1],
                                        cells[0:1, C_S4:C_S4 + 1],
                                        cells[0:1, C_S2:C_S2 + 1], Alu.mult))
            VW()
            V(lambda e: e.tensor_tensor(cells[0:1, C_G5:C_G5 + 1],
                                        cells[0:1, C_G5:C_G5 + 1],
                                        cells[0:1, C_T1:C_T1 + 1], Alu.mult))
            VW()
            V(lambda e: e.tensor_scalar(cells[0:1, C_G5 + 1:C_G5 + 2],
                                        cells[0:1, C_G5:C_G5 + 1], 1.0, None, Alu.mult))
            allreduce_max(C_G5, 3)
            V(lambda e: e.tensor_scalar(cells[0:1, C_S5:C_S5 + 1],
                                        cells[0:1, C_G5:C_G5 + 1], 1.0 / 32767.0, 1e-12,
                                        Alu.mult, Alu.max))
            VW()
            V(lambda e: e.reciprocal(cells[0:1, C_K5:C_K5 + 1],
                                     cells[0:1, C_S5:C_S5 + 1]))
            VW()
            V(lambda e: e.tensor_tensor(cells[0:1, C_K5:C_K5 + 1],
                                        cells[0:1, C_K5:C_K5 + 1],
                                        cells[0:1, C_T1:C_T1 + 1], Alu.mult))
            b.wait("gpsimd", "vv", b.val("vv"))
            cell_to_col(C_K5, ccol[:, 3:4])
            tb = cell_to_col(C_S5, ccol[:, 4:5])
            b.wait("scalar", "dG", tb)
            b.wait("vector", "dG", tb)
            if upto == "D":
                return

            b.wait("sync", "dA", b.val("dA"))
            b.wait("gpsimd", "dA", b.val("dA"))
            b.wait("sync", "mm", b.val("mm"))
            b.wait("scalar", "mm", b.val("mm"))
            tws = twa = None
            for ki in range(24):
                for od in range(FC):
                    tw_ = dma_tr(("sync", "scalar")[od % 2],
                                 wdd[:, ki, od * 128:(od + 1) * 128],
                                 l1d_b[od * 128:(od + 1) * 128, ki * 128:(ki + 1) * 128])
                    if od % 2 == 0:
                        tws = tw_
                    else:
                        twa = tw_
            b.wait("tensor", "dS", tws)
            b.wait("tensor", "dA", twa)
            kimark = []
            for t in range(NT):
                ts_ = slice(t * T, (t + 1) * T)
                for ki in range(24):
                    r = ki % 2
                    sp = (sp0, sp1)[r]
                    q, sem = (("sync", "dS"), ("gpsimd", "dG"))[r]
                    b.wait(q, "aa", b.val("aa"))
                    tl = dma(q, sp[:], Pbuf[ki * 128:(ki + 1) * 128,
                                            t * T:(t + 1) * T])
                    b.wait("scalar", sem, tl)
                    b.wait("scalar", "vv", b.val("vv"))
                    A(lambda e, sp=sp: e.activation(f0[:], sp[:], Act.Identity,
                                                    bias=mgcol[:], scale=ccol[:, 3:4]))
                    b.wait("vector", "aa", b.val("aa"))
                    if len(kimark) >= 2:
                        b.wait("vector", "mm", kimark[-2])
                    V(lambda e, r=r: e.tensor_scalar(oib[r][0:128, :], f0[:], -MG,
                                                     32767.0, Alu.add, Alu.min))
                    b.wait("tensor", "vv", b.val("vv"))
                    for od in range(FC):
                        Tn(lambda e, ki=ki, od=od, r=r: e.matmul(
                            PS[:, od, :], wdd[:, ki, od * 128:(od + 1) * 128],
                            oib[r][0:128, :], start=(ki == 0), stop=(ki == 23)),
                           "mm" if od == FC - 1 else None)
                    kimark.append(b.val("mm"))
                    if 'sere' in phases:
                        for e_ in ("tensor", "sync", "scalar", "vector", "gpsimd"):
                            b.wait(e_, "mm", b.val("mm"))
                            b.wait(e_, "vv", b.val("vv"))
                            b.wait(e_, "aa", b.val("aa"))
                mk = b.val("mm")
                b.wait("vector", "mm", mk)
                for od in range(FC):
                    V(lambda e, od=od, t_=ts_: e.scalar_tensor_tensor(
                        xres[:, od, t_], PS[:, od, :], ccol[:, 4:5], xres[:, od, t_],
                        Alu.mult, Alu.add))

        if _DO('l1f'):
            l1_ffn()
        else:
            for u in ("A", "B0", "B1L", "B1M", "B1", "B", "C", "D"):
                if _DO('l1f' + u):
                    l1_ffn(upto=u)
                    break
        barrier()

        if dbg:
            dma("gpsimd", dbgf[0:1, 3, 0:64], cells[0:1, 0:64])
            dma("gpsimd", dbgb[:, 0, :], Ubuf[0:128, 1536:2048])
            dma("gpsimd", dbgb[:, 1, :], Gbuf[0:128, 1536:2048])
            dma("gpsimd", dbgb[:, 2, :], GSbuf[0:128, 1536:2048])
            for oc5 in range(5):
                tst = dma("gpsimd", dbgb[:, 3 + oc5, :],
                          Pbuf[oc5 * 128:(oc5 + 1) * 128, 1536:2048])
        for c4 in range(4):
            tst = dma("gpsimd", yT.rearrange("(c p) t -> p c t", p=128)[:, 2 * c4:2 * c4 + 2, :],
                      xres[:, 2 * c4:2 * c4 + 2, :])
        b.wait("gpsimd", "dG", tst)

        blk = es.enter_context(nc.Block())
        for eng in ENGS:
            def _mk(eng):
                def _run(q):
                    for f in b.ops[eng]:
                        f(q)
                return _run
            getattr(blk, eng)(_mk(eng))

    return nc


_NC_CACHE = {}


def kernel(L0_attn_x, L0_attn_w_qkva, L0_attn_gamma, L0_ffn_w_gate, L0_ffn_w_up,
           L0_ffn_w_down, L1_attn_Wz, L1_attn_bz, L1_attn_Wh, L1_attn_bh,
           L1_attn_Wg, L1_attn_bg, L1_ffn_w_gate, L1_ffn_w_up, L1_ffn_w_down):
    n = 8
    if "nc" not in _NC_CACHE:
        _NC_CACHE["nc"] = build_program(n)
    nc = _NC_CACHE["nc"]
    asc = np.ascontiguousarray
    x = np.asarray(L0_attn_x, dtype=np.float32)
    common = dict(
        qkvaT=asc(np.asarray(L0_attn_w_qkva, np.float32).T),
        gamma=asc(np.asarray(L0_attn_gamma, np.float32).reshape(1, D)),
        l0g=asc(np.asarray(L0_ffn_w_gate, np.float32).T),
        l0u=asc(np.asarray(L0_ffn_w_up, np.float32).T),
        l0d=asc(np.asarray(L0_ffn_w_down, np.float32).T),
        wzT=asc(np.asarray(L1_attn_Wz, np.float32).transpose(1, 0, 2).reshape(D, D)),
        bz=asc(np.asarray(L1_attn_bz, np.float32).reshape(1, D)),
        whT=asc(np.asarray(L1_attn_Wh, np.float32).transpose(1, 0, 2).reshape(D, D)),
        bh=asc(np.asarray(L1_attn_bh, np.float32).reshape(1, D)),
        wgn=asc(np.asarray(L1_attn_Wg, np.float32)),
        bg=asc(np.asarray(L1_attn_bg, np.float32).reshape(1, D)),
        l1g=asc(np.asarray(L1_ffn_w_gate, np.float32)),
        l1u=asc(np.asarray(L1_ffn_w_up, np.float32)),
        l1d=asc(np.asarray(L1_ffn_w_down, np.float32)),
    )
    in_maps = []
    for c in range(n):
        m = dict(common)
        m["xT"] = asc(x[c].T)
        in_maps.append(m)
    res = run_bass_kernel_spmd(nc, in_maps, core_ids=list(range(n)))
    out = np.empty((n, TOK, D), dtype=np.float32)
    for c in range(n):
        out[c] = np.asarray(res.results[c]["yT"]).T
    return out


# version 06 (final)
# speedup vs baseline: 1.1026x; 1.0546x over previous
import contextlib
import numpy as np
import concourse.bass as bass
import concourse.mybir as mybir
from concourse.bass_utils import run_bass_kernel_spmd

F32 = mybir.dt.float32
BF16 = mybir.dt.bfloat16
U8 = mybir.dt.uint8
Alu = mybir.AluOpType
Act = mybir.ActivationFunctionType
AX = mybir.AxisListType.X

MG = 12582912.0
D = 1024
FC = 8
TOK = 4096
T = 512
NT = TOK // T
NG = T // 32
ENGS = ("sync", "scalar", "vector", "tensor", "gpsimd")

C_MG0, C_MU0, C_MWD0 = 0, 1, 2
C_IMG0, C_IMU0, C_IMWD0 = 3, 4, 5
C_G1, C_G2, C_G3, C_G4, C_G5 = 6, 7, 8, 9, 10
C_S1, C_S2, C_S3, C_S4, C_S5 = 11, 12, 13, 14, 15
C_K1, C_K2, C_K3, C_K4, C_K5 = 16, 17, 18, 19, 20
C_NS3MG = 21
C_T0, C_T1 = 24, 25
C_EPS = 26
C_SUM = 32


class B:
    def __init__(self, nc, es):
        self.nc, self.es = nc, es
        self.ops = {e: [] for e in ENGS}
        self.sem, self.cnt = {}, {}

    def mksem(self, name):
        self.sem[name] = self.es.enter_context(self.nc.semaphore(name))
        self.cnt[name] = 0

    def wait(self, eng, semname, val):
        s = self.sem[semname]
        self.ops[eng].append(lambda e, s=s, v=val: e.wait_ge(s, v))

    def put(self, eng, make, sem=None, n=1):
        if sem is None:
            self.ops[eng].append(make)
            return None
        self.cnt[sem] += n
        s = self.sem[sem]
        self.ops[eng].append(lambda e, make=make, s=s, n=n: make(e).then_inc(s, n))
        return self.cnt[sem]

    def val(self, s):
        return self.cnt[s]


def build_program(n_cores=8, phases=('prep', 'l0a', 'l0f', 'l1a', 'l1f'), dbg=False):
    nc = bass.Bass("TRN2", target_bir_lowering=False)

    din = lambda n, s: nc.dram_tensor(n, s, F32, kind="ExternalInput").ap()
    xT = din("xT", [D, TOK])
    yT = nc.dram_tensor("yT", [D, TOK], F32, kind="ExternalOutput").ap()
    qkvaT = din("qkvaT", [D, 3 * D])
    gamma = din("gamma", [1, D])
    l0g, l0u, l0d = din("l0g", [D, 4096]), din("l0u", [D, 4096]), din("l0d", [4096, D])
    wzT, bzd = din("wzT", [D, D]), din("bz", [1, D])
    whT, bhd = din("whT", [D, D]), din("bh", [1, D])
    wgn, bgd = din("wgn", [D, D]), din("bg", [1, D])
    l1g, l1u, l1d = din("l1g", [3072, D]), din("l1u", [3072, D]), din("l1d", [D, 3072])

    scr = lambda n, s, dt=BF16: nc.dram_tensor(n, s, dt).ap()
    qkva_b = scr("qkva_b", [D, 3 * D])
    wz_b, wh_b, wg_b = scr("wz_b", [D, D]), scr("wh_b", [D, D]), scr("wg_b", [D, D])
    l0g_q, l0u_q, l0d_q = scr("l0g_q", [D, 4096]), scr("l0u_q", [D, 4096]), scr("l0d_q", [4096, D])
    l1g_b, l1u_b, l1d_b = scr("l1g_b", [3072, D]), scr("l1u_b", [3072, D]), scr("l1d_b", [D, 3072])
    Vbuf = (nc.dram_tensor("Vbuf", [TOK, 4096], BF16, kind="ExternalOutput").ap()
            if dbg else scr("Vbuf", [TOK, 4096]))
    scrd = ((lambda n, s: nc.dram_tensor(n, s, BF16, kind="ExternalOutput").ap())
            if dbg else scr)
    Ubuf, Gbuf = scrd("Ubuf", [3072, TOK]), scrd("Gbuf", [3072, TOK])
    GSbuf, Pbuf = scrd("GSbuf", [3072, TOK]), scrd("Pbuf", [3072, TOK])
    dbgf = nc.dram_tensor("dbgf", [128, 8, 512], F32, kind="ExternalOutput").ap() if dbg else None
    dbgb = nc.dram_tensor("dbgb", [128, 8, 512], BF16, kind="ExternalOutput").ap() if dbg else None
    x3T = nc.dram_tensor("x3T", [D, TOK], F32, kind="ExternalOutput").ap() if dbg else None
    cc_i = [scr(f"cc_i{i}", [1, 2], F32) for i in range(4)]
    cc_o = [scr(f"cc_o{i}", [1, 2], F32) for i in range(4)]

    es = contextlib.ExitStack()
    with es:
        sbt = lambda name, shape, dt: es.enter_context(nc.sbuf_tensor(name, shape, dt))
        xres = sbt("xres", [128, FC, TOK], F32)
        arena = sbt("arena", [128, 54272], U8)
        f0 = sbt("f0", [128, T], F32)
        f1 = sbt("f1", [128, T], F32)
        f2 = sbt("f2", [128, T], F32)
        bc0 = sbt("bc0", [128, T], F32)
        bc1 = sbt("bc1", [128, T], F32)
        sp0 = sbt("sp0", [128, T], BF16)
        sp1 = sbt("sp1", [128, T], BF16)
        s32 = sbt("s32", [32, 64], F32)
        tin = sbt("tin", [128, 32], F32)
        carry = sbt("carry", [128, FC], F32)
        cols = sbt("cols", [128, 32], F32)
        cols2 = sbt("cols2", [128, 36], F32)
        cols3 = sbt("cols3", [128, 32], F32)
        cols4 = sbt("cols4", [128, 32], F32)
        cells = sbt("cells", [1, 64], F32)
        ccol = sbt("ccol", [128, 8], F32)
        gcol = sbt("gcol", [128, FC], F32)
        bzc = sbt("bzc", [128, FC], F32)
        bhc = sbt("bhc", [128, FC], F32)
        bgc = sbt("bgc", [128, FC], F32)
        onesb = sbt("onesb", [128, 1], BF16)
        onef = sbt("onef", [1, 1], F32)
        onecol = sbt("onecol", [128, 1], F32)
        eyef = sbt("eyef", [128, 128], F32)
        zcol = sbt("zcol", [128, 1], F32)
        mgcol = sbt("mgcol", [128, 1], F32)
        rowsA = sbt("rowsA", [1, 6, T], F32)
        PS = es.enter_context(nc.psum_tensor("psum", [128, 8, 512], F32))

        def AR(off, shape, dt):
            n = int(np.prod(shape)) * (2 if dt == BF16 else 4)
            v = arena[:, off:off + n].bitcast(dt)
            if len(shape) == 1:
                return v
            pat = " ".join(f"d{i}" for i in range(len(shape)))
            return v.rearrange(f"p ({pat}) -> p {pat}",
                               **{f"d{i}": s for i, s in enumerate(shape)})

        b = B(nc, es)
        for s in ("dS", "dA", "dG", "mm", "vv", "aa", "gg", "bar"):
            b.mksem(s)

        def dma(q, out, in_, **kw):
            sem = {"sync": "dS", "scalar": "dA", "gpsimd": "dG"}[q]
            return b.put(q, lambda e, o=out, i=in_, kw=kw: e.dma_start(out=o, in_=i, **kw),
                         sem, 16)

        def dma_tr(q, out, in_):
            sem = {"sync": "dS", "scalar": "dA", "gpsimd": "dG"}[q]
            return b.put(q, lambda e, o=out, i=in_: e.dma_start_transpose(o, i), sem, 16)

        V = lambda make, sem="vv": b.put("vector", make, sem)
        A = lambda make, sem="aa": b.put("scalar", make, sem)
        Tn = lambda make, sem=None: b.put("tensor", make, sem)
        VW = lambda: b.wait("vector", "vv", b.val("vv"))

        def barrier():
            tgt = b.val("bar") + 4
            b.put("vector", lambda e: e.memset(f2[0:1, 0:1], 0.0), "bar")
            b.put("scalar", lambda e: e.copy(f2[0:1, 1:2], f2[0:1, 1:2]), "bar")
            b.put("tensor", lambda e: e.matmul(PS[0:1, 7, 0:1], onesb[0:1, 0:1],
                                               onesb[0:1, 0:1], start=True, stop=True), "bar")
            b.put("gpsimd", lambda e: e.memset(f2[0:1, 2:3], 0.0), "bar")
            marks = {q: b.val(q) for q in ("dS", "dA", "dG")}
            for e in ENGS:
                b.wait(e, "bar", tgt)
                for q, v in marks.items():
                    b.wait(e, q, v)

        def fold128_to_cell(col, cell_idx, op):
            b.wait("gpsimd", "vv", b.val("vv"))
            t = dma("gpsimd", f2[0:64, 0:1], col[64:128, 0:1])
            b.wait("vector", "dG", t)
            V(lambda e: e.tensor_tensor(f2[0:64, 1:2], col[0:64, 0:1], f2[0:64, 0:1], op))
            b.wait("gpsimd", "vv", b.val("vv"))
            t = dma("gpsimd", s32[0:32, 0:1], f2[32:64, 1:2])
            b.wait("vector", "dG", t)
            V(lambda e: e.tensor_tensor(s32[0:32, 32:33], f2[0:32, 1:2], s32[0:32, 0:1], op))
            VW()
            V(lambda e: e.transpose(s32[0:32, 0:32], s32[0:32, 32:64]))
            VW()
            V(lambda e: e.tensor_reduce(cells[0:1, cell_idx:cell_idx + 1],
                                        s32[0:1, 0:32], AX, op))
            VW()

        def cell_to_col(cell_idx, col_out):
            return dma("gpsimd", col_out,
                       cells[0:1, cell_idx:cell_idx + 1].unsqueeze(1).broadcast_to([1, 128, 1]),
                       allow_slow_non_contiguous=True)

        def bcast_row(row, tile_out, n=T):
            return dma("gpsimd", tile_out, row.unsqueeze(1).broadcast_to([1, 128, n]))

        def row_pieces_to_tin(row, base):
            last = None
            for tg in range(4):
                for i in range(4):
                    p = 32 * i + base + tg
                    last = dma("gpsimd", tin[p:p + 1, 0:32],
                               row[0:1, tg * 128 + 32 * i:tg * 128 + 32 * i + 32])
            return last

        def tin_rowslices_from_tr(tr, col_base, row_out):
            last = None
            for tg in range(4):
                for i in range(4):
                    p = 32 * i + col_base + tg
                    last = dma("gpsimd", row_out[0:1, tg * 128 + 32 * i:tg * 128 + 32 * i + 32],
                               tr[p:p + 1, 0:32])
            return last

        R_S1, R_S2, R_M, R_R, R_C, R_T = range(6)
        R_MX, R_MN = R_S1, R_S2
        rw = lambda i: rowsA[0:1, i, :]

        for c4 in range(4):
            dma("sync", xres[:, 2 * c4:2 * c4 + 2, :],
                xT.rearrange("(c p) t -> p c t", p=128)[:, 2 * c4:2 * c4 + 2, :])
        dma("sync", gcol[:], gamma.rearrange("o (c p) -> p c", p=128), allow_slow_non_contiguous=True)
        dma("sync", bzc[:], bzd.rearrange("o (c p) -> p c", p=128), allow_slow_non_contiguous=True)
        dma("sync", bhc[:], bhd.rearrange("o (c p) -> p c", p=128), allow_slow_non_contiguous=True)
        dma("sync", bgc[:], bgd.rearrange("o (c p) -> p c", p=128), allow_slow_non_contiguous=True)
        V(lambda e: e.memset(onesb[:], 1.0), None)
        V(lambda e: e.memset(onef[:], 1.0), None)
        V(lambda e: e.memset(onecol[:], 1.0), None)
        V(lambda e: e.memset(zcol[:], 0.0), None)
        V(lambda e: e.memset(mgcol[:], MG), None)
        V(lambda e: e.memset(cells[:], 0.0), None)
        V(lambda e: e.memset(s32[:], 0.0), None)
        V(lambda e: e.memset(cells[0:1, C_EPS:C_EPS + 1], 1e-5), None)


        stg_ = [AR(0, (4096,), F32), AR(16384, (4096,), F32)]
        stgb_ = [AR(32768, (4096,), BF16), AR(40960, (4096,), BF16)]
        stg = stg_[0]
        stgb = stgb_[0]

        def prep_cast(src, dst, ncols):
            cmark, smark = [], []
            for c in range(src.shape[0] // 128):
                r = c % 2
                if c >= 2:
                    b.wait("sync", "aa", cmark[c - 2])
                else:
                    b.wait("sync", "aa", b.val("aa"))
                    b.wait("sync", "vv", b.val("vv"))
                t1 = dma("sync", stg_[r][0:128, 0:ncols], src[c * 128:(c + 1) * 128, 0:ncols])
                b.wait("scalar", "dS", t1)
                b.wait("scalar", "dA", smark[c - 2] if c >= 2 else b.val("dA"))
                A(lambda e, n=ncols, r=r: e.copy(stgb_[r][0:128, 0:n], stg_[r][0:128, 0:n]))
                cmark.append(b.val("aa"))
                b.wait("scalar", "aa", cmark[-1])
                smark.append(dma("scalar", dst[c * 128:(c + 1) * 128, 0:ncols],
                                 stgb_[r][0:128, 0:ncols]))

        _DO = lambda p: p in phases
        if _DO('prep'):
            prep_cast(qkvaT, qkva_b, 3 * D)
            prep_cast(wzT, wz_b, D)
            prep_cast(whT, wh_b, D)
            prep_cast(wgn, wg_b, D)

        def prep_ternary(src, dst, ci, ncols):
            nch = src.shape[0] // 128
            V(lambda e: e.memset(cols[:, 0:1], 0.0))
            rmark = []
            for c in range(nch):
                r = c % 2
                if c >= 2:
                    b.wait("sync", "vv", rmark[c - 2])
                else:
                    b.wait("sync", "vv", b.val("vv"))
                    b.wait("sync", "aa", b.val("aa"))
                t1 = dma("sync", stg_[r][0:128, 0:ncols], src[c * 128:(c + 1) * 128, 0:ncols])
                b.wait("vector", "dS", t1)
                V(lambda e, n=ncols, r=r: e.tensor_reduce(
                    cols[:, 1:2], stg_[r][0:128, 0:n], AX,
                    Alu.add, apply_absolute_value=True))
                rmark.append(b.val("vv"))
                VW()
                V(lambda e: e.tensor_tensor(cols[:, 0:1], cols[:, 0:1], cols[:, 1:2],
                                            Alu.add))
            fold128_to_cell(cols[:, 0:1], C_SUM + ci, Alu.add)
            Nw = float(nch * 128 * ncols)
            V(lambda e, Nw=Nw, ci=ci: e.tensor_scalar(
                cells[0:1, ci:ci + 1], cells[0:1, C_SUM + ci:C_SUM + ci + 1],
                1.0 / Nw, 1e-5, Alu.mult, Alu.max))
            VW()
            V(lambda e, ci=ci: e.reciprocal(cells[0:1, C_IMG0 + ci:C_IMG0 + ci + 1],
                                            cells[0:1, ci:ci + 1]))
            b.wait("gpsimd", "vv", b.val("vv"))
            t = cell_to_col(C_IMG0 + ci, ccol[:, ci:ci + 1])
            b.wait("vector", "dG", t)
            qmark, spmark = [], []
            for c in range(nch):
                r = c % 2
                b.wait("sync", "vv", qmark[c - 2] if c >= 2 else b.val("vv"))
                t1 = dma("sync", stg_[r][0:128, 0:ncols], src[c * 128:(c + 1) * 128, 0:ncols])
                b.wait("vector", "dS", t1)
                b.wait("vector", "dA", spmark[c - 2] if c >= 2 else b.val("dA"))
                V(lambda e, n=ncols, ci=ci, r=r: e.tensor_scalar(
                    stg_[r][0:128, 0:n], stg_[r][0:128, 0:n], ccol[:, ci:ci + 1], MG,
                    Alu.mult, Alu.add), None)
                V(lambda e, n=ncols, r=r: e.tensor_scalar(
                    stg_[r][0:128, 0:n], stg_[r][0:128, 0:n], MG + 1.0, -MG,
                    Alu.min, Alu.add), None)
                V(lambda e, n=ncols, r=r: e.tensor_scalar(
                    stgb_[r][0:128, 0:n], stg_[r][0:128, 0:n], -1.0, None, Alu.max))
                qmark.append(b.val("vv"))
                b.wait("scalar", "vv", qmark[-1])
                spmark.append(dma("scalar", dst[c * 128:(c + 1) * 128, 0:ncols],
                                  stgb_[r][0:128, 0:ncols]))

        if _DO('prep'):
            prep_ternary(l0g, l0g_q, C_MG0, 4096)
            prep_ternary(l0u, l0u_q, C_MU0, 4096)
            prep_ternary(l0d, l0d_q, C_MWD0, D)

        def prep_lpbq(src, dst, ncols):
            nch = src.shape[0] // 128
            nblk = ncols // 32
            smax = AR(49152, (nblk,), F32)
            sinv = AR(49152 + 4 * nblk, (nblk,), F32)
            qmark, spmark = [], []
            for c in range(nch):
                r = c % 2
                if c >= 2:
                    b.wait("sync", "vv", qmark[c - 2])
                else:
                    b.wait("sync", "vv", b.val("vv"))
                    b.wait("sync", "aa", b.val("aa"))
                t1 = dma("sync", stg_[r][0:128, 0:ncols], src[c * 128:(c + 1) * 128, 0:ncols])
                b.wait("vector", "dS", t1)
                b.wait("vector", "dA", spmark[c - 2] if c >= 2 else b.val("dA"))
                V(lambda e, n=ncols, nb=nblk, r=r: e.tensor_reduce(
                    smax[0:128, 0:nb],
                    stg_[r][0:128, 0:n].rearrange("p (g c) -> p g c", c=32),
                    AX, Alu.max, apply_absolute_value=True))
                VW()
                V(lambda e, nb=nblk: e.tensor_scalar(
                    smax[0:128, 0:nb], smax[0:128, 0:nb], 1.0 / 7.0, 1e-12,
                    Alu.mult, Alu.max))
                VW()
                V(lambda e, nb=nblk: e.reciprocal(sinv[0:128, 0:nb], smax[0:128, 0:nb]))
                VW()
                V(lambda e, n=ncols, nb=nblk, r=r: e.tensor_tensor(
                    stg_[r][0:128, 0:n].rearrange("p (g c) -> p g c", c=32),
                    stg_[r][0:128, 0:n].rearrange("p (g c) -> p g c", c=32),
                    sinv[0:128, 0:nb].unsqueeze(2).broadcast_to([128, nb, 32]),
                    Alu.mult), None)
                V(lambda e, n=ncols, r=r: e.tensor_scalar(
                    stg_[r][0:128, 0:n], stg_[r][0:128, 0:n], MG, MG + 7.0,
                    Alu.add, Alu.min), None)
                V(lambda e, n=ncols, r=r: e.tensor_scalar(
                    stg_[r][0:128, 0:n], stg_[r][0:128, 0:n], -MG, -8.0,
                    Alu.add, Alu.max), None)
                V(lambda e, n=ncols, nb=nblk, r=r: e.tensor_tensor(
                    stgb_[r][0:128, 0:n].rearrange("p (g c) -> p g c", c=32),
                    stg_[r][0:128, 0:n].rearrange("p (g c) -> p g c", c=32),
                    smax[0:128, 0:nb].unsqueeze(2).broadcast_to([128, nb, 32]),
                    Alu.mult))
                qmark.append(b.val("vv"))
                b.wait("scalar", "vv", qmark[-1])
                spmark.append(dma("scalar", dst[c * 128:(c + 1) * 128, 0:ncols],
                                  stgb_[r][0:128, 0:ncols]))

        if _DO('prep'):
            prep_lpbq(l1g, l1g_b, D)
            prep_lpbq(l1u, l1u_b, D)
            prep_lpbq(l1d, l1d_b, 3072)
        barrier()

        def ln_stats(tile, xb, xs, rms=False, maxmin=False):
            ts_ = slice(tile * T, (tile + 1) * T)
            b.wait("scalar", "mm", b.val("mm"))
            b.wait("scalar", "vv", b.val("vv"))
            for ch in range(FC):
                A(lambda e, ch=ch, t_=ts_, xb=xb: e.copy(xb[:, ch, :], xres[:, ch, t_]), None)
                A(lambda e, ch=ch, t_=ts_, xs=xs: e.activation(xs[:, ch, :],
                                                               xres[:, ch, t_], Act.Square))
            b.wait("tensor", "aa", b.val("aa"))
            for ch in range(FC):
                Tn(lambda e, ch=ch, xb=xb: e.matmul(PS[0:1, 6, :], onesb[:], xb[:, ch, :],
                                                    start=(ch == 0), stop=(ch == FC - 1)), None)
            for ch in range(FC):
                Tn(lambda e, ch=ch, xs=xs: e.matmul(PS[0:1, 7, :], onesb[:], xs[:, ch, :],
                                                    start=(ch == 0), stop=(ch == FC - 1)),
                   "mm" if ch == FC - 1 else None)
            b.wait("scalar", "mm", b.val("mm"))
            A(lambda e: e.copy(rw(R_S1), PS[0:1, 6, :]), None)
            A(lambda e: e.copy(rw(R_S2), PS[0:1, 7, :]))
            b.wait("vector", "aa", b.val("aa"))
            V(lambda e: e.tensor_scalar(rw(R_M), rw(R_S1), 1.0 / D, None, Alu.mult), None)
            V(lambda e: e.tensor_tensor(rw(R_T), rw(R_M), rw(R_M), Alu.mult), None)
            V(lambda e: e.scalar_tensor_tensor(rw(R_T), rw(R_S2), 1.0 / D, rw(R_T),
                                               Alu.mult, Alu.subtract))
            b.wait("scalar", "vv", b.val("vv"))
            A(lambda e: e.activation(rw(R_C), rw(R_T), Act.Sqrt, bias=cells[0:1, C_EPS:C_EPS + 1], scale=onef[:]))
            b.wait("vector", "aa", b.val("aa"))
            V(lambda e: e.reciprocal(rw(R_R), rw(R_C)))
            if rms:
                b.wait("scalar", "vv", b.val("vv"))
                A(lambda e: e.activation(rw(R_C), rw(R_T), Act.Sqrt, bias=0.0, scale=1.0))
                b.wait("vector", "aa", b.val("aa"))
                V(lambda e: e.scalar_tensor_tensor(rw(R_C), rw(R_C), 32.0, rw(R_R),
                                                   Alu.mult, Alu.mult), None)
                V(lambda e: e.tensor_scalar(rw(R_C), rw(R_C), 1e-12, None, Alu.max), None)
                V(lambda e: e.reciprocal(rw(R_T), rw(R_C)), None)
                V(lambda e: e.scalar_tensor_tensor(rw(R_R), rw(R_R), 32.0, rw(R_T),
                                                   Alu.mult, Alu.mult))
            if maxmin:
                V(lambda e, t_=ts_: e.tensor_reduce(
                    f0[:], xres[:, :, t_].transpose([0, 2, 1]), AX, Alu.max), None)
                V(lambda e, t_=ts_: e.tensor_reduce(
                    f1[:], xres[:, :, t_].transpose([0, 2, 1]), AX, Alu.min))
                b.wait("gpsimd", "vv", b.val("vv"))
                t = dma("gpsimd", f2[0:64, :], f0[64:128, :])
                b.wait("vector", "dG", t)
                V(lambda e: e.tensor_max(f0[0:64, :], f0[0:64, :], f2[0:64, :]))
                b.wait("gpsimd", "vv", b.val("vv"))
                t = dma("gpsimd", f2[0:64, :], f1[64:128, :])
                b.wait("vector", "dG", t)
                V(lambda e: e.tensor_tensor(f1[0:64, :], f1[0:64, :], f2[0:64, :], Alu.min))
                b.wait("gpsimd", "vv", b.val("vv"))
                t = dma("gpsimd", f2[0:32, :], f0[32:64, :])
                b.wait("vector", "dG", t)
                V(lambda e: e.tensor_max(f0[0:32, :], f0[0:32, :], f2[0:32, :]))
                b.wait("gpsimd", "vv", b.val("vv"))
                t = dma("gpsimd", f2[0:32, :], f1[32:64, :])
                b.wait("vector", "dG", t)
                V(lambda e: e.tensor_tensor(f1[0:32, :], f1[0:32, :], f2[0:32, :],
                                            Alu.min))
                VW()
                V(lambda e: e.transpose(f2[0:32, :], f0[0:32, :]))
                VW()
                V(lambda e: e.tensor_reduce(
                    s32[0:32, 0:NG], f2[0:32, :].rearrange("p (g c) -> p g c", c=32),
                    AX, Alu.max))
                VW()
                V(lambda e: e.transpose(s32[0:32, 32:64], s32[0:32, 0:32]))
                b.wait("gpsimd", "vv", b.val("vv"))
                tmx = dma("gpsimd", rw(R_MX).rearrange("o (g a) -> o g a", a=32),
                          s32[0:NG, 32:64])
                b.wait("vector", "dG", tmx)
                V(lambda e: e.transpose(f2[0:32, :], f1[0:32, :]))
                VW()
                V(lambda e: e.tensor_reduce(
                    s32[0:32, 0:NG], f2[0:32, :].rearrange("p (g c) -> p g c", c=32),
                    AX, Alu.min))
                VW()
                V(lambda e: e.transpose(s32[0:32, 32:64], s32[0:32, 0:32]))
                b.wait("gpsimd", "vv", b.val("vv"))
                tmn = dma("gpsimd", rw(R_MN).rearrange("o (g a) -> o g a", a=32),
                          s32[0:NG, 32:64])
                b.wait("vector", "dG", tmn)

        def l0_attn():
            wq = [AR(0, (FC, 512), BF16), AR(8192, (FC, 512), BF16)]
            hb = AR(16384, (FC, T), BF16)
            xbst = hb
            yy = AR(24576, (FC, T), F32)
            xsst = AR(40960, (FC, T), BF16)
            kvb = [AR(40960 + 2048 * i, (T,), F32) for i in range(4)]
            ab = [AR(49152 + 2048 * i, (T,), F32) for i in range(2)]
            V(lambda e: e.memset(yy[:, :, T - 1:T], 0.0))
            wst = {"n": 0, "rel": []}

            def load_ob(ob):
                buf = wst["n"] % 2
                q, sem = (("sync", "dS"), ("gpsimd", "dG"))[buf]
                if wst["n"] >= 2:
                    b.wait(q, "mm", wst["rel"][wst["n"] - 2])
                t = dma(q, wq[buf][:],
                        qkva_b.rearrange("(kc p) o -> p kc o", p=128)[:, :, ob * 512:(ob + 1) * 512])
                return buf, t, sem

            for t in range(NT):
                ts_ = slice(t * T, (t + 1) * T)
                ln_stats(t, xbst, xsst, rms=True)
                b.wait("gpsimd", "vv", b.val("vv"))
                bcast_row(rw(R_M), bc0[:])
                tb = bcast_row(rw(R_R), bc1[:])
                b.wait("vector", "dG", tb)
                if dbg and t == 1:
                    dma("gpsimd", dbgf[:, 0, :], bc0[:])
                    td = dma("gpsimd", dbgf[:, 1, :], bc1[:])
                    b.wait("vector", "dG", td)
                for ch in range(FC):
                    V(lambda e, ch=ch, t_=ts_: e.tensor_tensor(
                        f0[:], xres[:, ch, t_], bc0[:], Alu.subtract), None)
                    V(lambda e, ch=ch: e.scalar_tensor_tensor(
                        hb[:, ch, :], f0[:], gcol[:, ch:ch + 1], bc1[:],
                        Alu.mult, Alu.mult))
                vv_h = b.val("vv")
                if dbg and t == 1:
                    b.wait("gpsimd", "vv", vv_h)
                    dma("gpsimd", dbgf[:, 2, :], hb[:, 0, :])

                def mm_block(ob, wait_aa=None, wait_vv=None):
                    buf, tl, lsem = load_ob(ob)
                    b.wait("tensor", lsem, tl)
                    b.wait("tensor", "vv", max(vv_h, wait_vv or 0))
                    if wait_aa:
                        b.wait("tensor", "aa", wait_aa)
                    marks = []
                    for j in range(4):
                        for kc in range(FC):
                            Tn(lambda e, kc=kc, j=j, buf=buf: e.matmul(
                                PS[:, j, :], wq[buf][:, kc, j * 128:(j + 1) * 128],
                                hb[:, kc, :], start=(kc == 0), stop=(kc == FC - 1)),
                               "mm" if kc == FC - 1 else None)
                        marks.append(b.val("mm"))
                    wst["n"] += 1
                    wst["rel"].append(marks[-1])
                    return marks

                aa_prev, vv_prev = None, None
                for half in range(2):
                    mk_kv = mm_block(2 + half, wait_aa=aa_prev, wait_vv=vv_prev)
                    if vv_prev:
                        b.wait("scalar", "vv", vv_prev)
                    for j in range(4):
                        b.wait("scalar", "mm", mk_kv[j])
                        A(lambda e, j=j: e.copy(kvb[j][0:128, :], PS[:, j, :]))
                    aa_kv = b.val("aa")
                    mk_a = mm_block(4 + half, wait_aa=aa_kv)
                    scan_marks = []
                    for j in range(4):
                        ch = half * 4 + j
                        b.wait("scalar", "mm", mk_a[j])
                        if j >= 2:
                            b.wait("scalar", "vv", scan_marks[j - 2])
                        A(lambda e, j=j: e.activation(ab[j % 2][0:128, :], PS[:, j, :],
                                                      Act.Sigmoid))
                        b.wait("vector", "aa", b.val("aa"))
                        V(lambda e, j=j, ch=ch: e.tensor_tensor_scan(
                            yy[:, ch, :], ab[j % 2][0:128, :], kvb[j][0:128, :],
                            yy[:, ch, T - 1:T], Alu.mult, Alu.add))
                        scan_marks.append(b.val("vv"))
                        if dbg and t == 1 and ch == 0:
                            b.wait("gpsimd", "vv", b.val("vv"))
                            dma("gpsimd", dbgf[:, 3, :], kvb[j][0:128, :])
                            dma("gpsimd", dbgf[:, 4, :], ab[j % 2][0:128, :])
                            td = dma("gpsimd", dbgf[:, 5, :], yy[:, 0, :])
                            b.wait("scalar", "dG", td)
                    aa_prev, vv_prev = b.val("aa"), b.val("vv")
                for half in range(2):
                    mk_q = mm_block(half, wait_aa=aa_prev, wait_vv=vv_prev)
                    for j in range(4):
                        ch = half * 4 + j
                        b.wait("vector", "mm", mk_q[j])
                        V(lambda e, j=j, ch=ch: e.scalar_tensor_tensor(
                            f0[:], PS[:, j, :], 1.0, yy[:, ch, :], Alu.mult, Alu.mult), None)
                        V(lambda e, ch=ch, t_=ts_: e.tensor_tensor(
                            xres[:, ch, t_], xres[:, ch, t_], f0[:], Alu.add))
                        if dbg and t == 1 and ch == 0:
                            b.wait("gpsimd", "vv", b.val("vv"))
                            td = dma("gpsimd", dbgf[0:1, 6, :],
                                     rowsA[0:1, :, :].rearrange("o a b -> o (a b)")[0:1, 0:512])
                            b.wait("scalar", "dG", td)
                    aa_prev, vv_prev = b.val("aa"), b.val("vv")

        if _DO('l0a'):
            l0_attn()
        barrier()

        def l0_ffn(do_sweep2=True):
            xq = AR(0, (4, FC, T), BF16)
            xbst = AR(32768, (FC, T), BF16)
            xsst = AR(40960, (FC, T), BF16)
            wg_ = [AR(32768, (FC, 512), BF16)]
            wu_ = [AR(40960, (FC, 512), BF16)]
            vt = [AR(49152 + 1024 * i, (T,), BF16) for i in range(2)]
            vq = [AR(51200 + 1024 * i, (T,), BF16) for i in range(2)]
            wk = [AR(32768 + 2048 * i, (D,), BF16) for i in range(2)]

            for grp in range(2):
                tiles = list(range(grp * 4, grp * 4 + 4))
                b.wait("vector", "mm", b.val("mm"))
                for ti, t in enumerate(tiles):
                    ts_ = slice(t * T, (t + 1) * T)
                    ln_stats(t, xbst, xsst, maxmin=True)
                    V(lambda e: e.tensor_tensor(rw(R_MX), rw(R_MX), rw(R_M),
                                                Alu.subtract), None)
                    V(lambda e: e.tensor_tensor(rw(R_MN), rw(R_M), rw(R_MN),
                                                Alu.subtract), None)
                    V(lambda e: e.tensor_max(rw(R_MX), rw(R_MX), rw(R_MN)), None)
                    V(lambda e: e.tensor_tensor(rw(R_C), rw(R_R), rw(R_MX), Alu.mult), None)
                    V(lambda e: e.tensor_scalar(rw(R_C), rw(R_C), 1e-5, None, Alu.max), None)
                    V(lambda e: e.reciprocal(rw(R_T), rw(R_C)), None)
                    V(lambda e: e.tensor_scalar(rw(R_MN), rw(R_C),
                                                cells[0:1, C_MG0:C_MG0 + 1], 1.0 / 127.0,
                                                Alu.mult, Alu.mult), None)
                    V(lambda e: e.tensor_scalar(rw(R_MX), rw(R_C),
                                                cells[0:1, C_MU0:C_MU0 + 1], 1.0 / 127.0,
                                                Alu.mult, Alu.mult), None)
                    V(lambda e: e.scalar_tensor_tensor(rw(R_C), rw(R_T), 127.0, rw(R_R),
                                                       Alu.mult, Alu.mult))
                    b.wait("gpsimd", "vv", b.val("vv"))
                    row_pieces_to_tin(rw(R_MN), ti * 8)
                    row_pieces_to_tin(rw(R_MX), ti * 8 + 4)
                    bcast_row(rw(R_M), bc0[:])
                    tb = bcast_row(rw(R_C), bc1[:])
                    b.wait("vector", "dG", tb)
                    for ch in range(FC):
                        V(lambda e, ch=ch, t_=ts_: e.tensor_tensor(
                            f0[:], xres[:, ch, t_], bc0[:], Alu.subtract), None)
                        V(lambda e: e.tensor_tensor(f0[:], f0[:], bc1[:], Alu.mult), None)
                        V(lambda e: e.tensor_scalar(f0[:], f0[:], MG, MG + 127.0,
                                                    Alu.add, Alu.min), None)
                        V(lambda e, ti=ti, ch=ch: e.tensor_scalar(
                            xq[:, ti, ch, :], f0[:], -MG, -128.0, Alu.add, Alu.max))
                V(lambda e, g=grp: e.memset(cols2[:, g * 16:g * 16 + 16], 0.0), None)
                b.wait("vector", "dG", b.val("dG"))
                V(lambda e: e.transpose(cols[0:128, 0:32], tin[:]))
                if dbg and grp == 0:
                    b.wait("gpsimd", "vv", b.val("vv"))
                    dma("gpsimd", dbgf[:, 4, 0:32], cols[:, 0:32])
                    td = dma("gpsimd", dbgf[:, 5, 0:32], tin[:, 0:32])
                    b.wait("sync", "dG", td)
                vv_xq = b.val("vv")
                b.wait("sync", "vv", vv_xq)
                b.wait("scalar", "vv", vv_xq)
                wst = {"n": 0, "rel": []}
                ep = []
                it = 0
                for og in range(8):
                    buf = 0
                    if wst["n"] >= 1:
                        b.wait("sync", "mm", wst["rel"][-1])
                    dma("sync", wg_[buf][:],
                        l0g_q.rearrange("(kc p) o -> p kc o", p=128)[:, :, og * 512:(og + 1) * 512])
                    tu_ = dma("sync", wu_[buf][:],
                              l0u_q.rearrange("(kc p) o -> p kc o", p=128)[:, :, og * 512:(og + 1) * 512])
                    wst["n"] += 1
                    b.wait("tensor", "dS", tu_)
                    b.wait("tensor", "vv", vv_xq)
                    for ti, t in enumerate(tiles):
                        for tg in range(4):
                            par = it % 2
                            bk_g, bk_u = 2 * par, 2 * par + 1
                            if it >= 2:
                                b.wait("tensor", "aa", ep[it - 2][0])
                                b.wait("tensor", "vv", ep[it - 2][1])
                            for kc in range(FC):
                                Tn(lambda e, kc=kc, ti=ti, tg=tg, buf=buf, bk=bk_g: e.matmul(
                                    PS[:, bk, :], xq[:, ti, kc, tg * 128:(tg + 1) * 128],
                                    wg_[buf][:, kc, :], start=(kc == 0), stop=(kc == FC - 1)),
                                   "mm" if kc == FC - 1 else None)
                            for kc in range(FC):
                                Tn(lambda e, kc=kc, ti=ti, tg=tg, buf=buf, bk=bk_u: e.matmul(
                                    PS[:, bk, :], xq[:, ti, kc, tg * 128:(tg + 1) * 128],
                                    wu_[buf][:, kc, :], start=(kc == 0), stop=(kc == FC - 1)),
                                   "mm" if kc == FC - 1 else None)
                            mk = b.val("mm")
                            ci = ti * 8 + tg
                            b.wait("scalar", "mm", mk)
                            A(lambda e, bk=bk_g, ci=ci: e.activation(
                                f0[:], PS[:, bk, :], Act.Silu, bias=zcol[:],
                                scale=cols[:, ci:ci + 1]))
                            b.wait("vector", "aa", b.val("aa"))
                            V(lambda e, bk=bk_u, ci=ci: e.scalar_tensor_tensor(
                                f1[:], PS[:, bk, :], cols[:, ci + 4:ci + 5], f0[:],
                                Alu.mult, Alu.mult), None)
                            cx = grp * 16 + ti * 4 + tg
                            V(lambda e: e.tensor_reduce(f2[:, 0:1], f1[:], AX, Alu.max,
                                                        apply_absolute_value=True))
                            VW()
                            V(lambda e, cx=cx: e.tensor_max(
                                cols2[:, cx:cx + 1], cols2[:, cx:cx + 1], f2[:, 0:1]))
                            vv_ep = b.val("vv")
                            b.wait("scalar", "vv", vv_ep)
                            b.wait("scalar", "dA", b.val("dA"))
                            A(lambda e: e.copy(sp0[:], f1[:]))
                            aa_ep = b.val("aa")
                            b.wait("scalar", "aa", aa_ep)
                            dma("scalar",
                                Vbuf[t * T + tg * 128:t * T + (tg + 1) * 128,
                                     og * 512:(og + 1) * 512], sp0[:])
                            ep.append((aa_ep, vv_ep))
                            it += 1
                    wst["rel"].append(b.val("mm"))
            barrier()
            if not do_sweep2:
                return
            V(lambda e: e.transpose(tin[:], cols2[0:128, 0:32]))
            komark = []
            for t in range(NT):
                ts_ = slice(t * T, (t + 1) * T)
                b.wait("tensor", "vv", b.val("vv"))
                b.wait("gpsimd", "vv", b.val("vv"))
                tpz = tin_rowslices_from_tr(tin, t * 4, rw(R_T))
                b.wait("vector", "dG", tpz)
                V(lambda e: e.tensor_scalar(rw(R_T), rw(R_T), 1e-5, None, Alu.max), None)
                V(lambda e: e.reciprocal(rw(R_C), rw(R_T)), None)
                V(lambda e: e.tensor_scalar(rw(R_C), rw(R_C), 127.0, None, Alu.mult), None)
                V(lambda e: e.tensor_scalar(rw(R_T), rw(R_T),
                                            cells[0:1, C_MWD0:C_MWD0 + 1], 1.0 / 127.0,
                                            Alu.mult, Alu.mult))
                b.wait("gpsimd", "vv", b.val("vv"))
                bcast_row(rw(R_C), bc0[:])
                tb = bcast_row(rw(R_T), bc1[:])
                b.wait("vector", "dG", tb)
                if dbg and t == 0:
                    dma("gpsimd", dbgf[:, 6, :], bc0[:])
                    td = dma("gpsimd", dbgf[:, 7, :], bc1[:])
                    b.wait("sync", "dG", td)
                for ko in range(32):
                    r = ko % 2
                    if len(komark) >= 2:
                        b.wait("sync", "vv", komark[-2][1])
                        b.wait("sync", "mm", komark[-2][0])
                        b.wait("vector", "mm", komark[-2][0])
                    tkw = dma("sync", wk[r][0:128, :],
                              l0d_q[ko * 128:(ko + 1) * 128, :].rearrange(
                                  "(o p) d -> p (o d)", p=128))
                    tv = None
                    for tg in range(4):
                        tv = dma_tr("sync", vt[r][0:128, tg * 128:(tg + 1) * 128],
                                    Vbuf[t * T + tg * 128:t * T + (tg + 1) * 128,
                                         ko * 128:(ko + 1) * 128])
                    b.wait("vector", "dS", tv)
                    V(lambda e, r=r: e.tensor_tensor(f0[:], vt[r][0:128, :], bc0[:],
                                                     Alu.mult), None)
                    V(lambda e: e.tensor_scalar(f0[:], f0[:], MG, MG + 127.0,
                                                Alu.add, Alu.min), None)
                    V(lambda e, r=r: e.tensor_scalar(vq[r][0:128, :], f0[:], -MG, -128.0,
                                                     Alu.add, Alu.max))
                    vvq = b.val("vv")
                    b.wait("tensor", "vv", vvq)
                    b.wait("tensor", "dS", tkw)
                    for od in range(FC):
                        Tn(lambda e, ko=ko, od=od, r=r: e.matmul(
                            PS[:, od, :],
                            wk[r][0:128, :].rearrange("p (o d) -> p o d", o=8)[:, od, :]
                            if False else wk[r][0:128, od * 128:(od + 1) * 128],
                            vq[r][0:128, :], start=(ko == 0), stop=(ko == 31)),
                           "mm" if od == FC - 1 else None)
                    komark.append((b.val("mm"), vvq))
                mk = b.val("mm")
                b.wait("vector", "mm", mk)
                for od in range(FC):
                    V(lambda e, od=od: e.tensor_tensor(f0[:], PS[:, od, :], bc1[:],
                                                       Alu.mult), None)
                    V(lambda e, od=od, t_=ts_: e.tensor_tensor(
                        xres[:, od, t_], xres[:, od, t_], f0[:], Alu.add))

        if _DO('l0f') or _DO('l0f1'):
            l0_ffn(do_sweep2=_DO('l0f'))
        barrier()

        def l1_attn():
            wz_ = [AR(0, (FC, 128), BF16), AR(2048, (FC, 128), BF16)]
            wh_ = [AR(4096, (FC, 128), BF16), AR(6144, (FC, 128), BF16)]
            wg2 = [AR(8192, (FC, 128), BF16), AR(10240, (FC, 128), BF16)]
            ub = AR(12288, (FC, T), BF16)
            xbst = ub
            yy = AR(20480, (FC, T), F32)
            hbb = AR(36864, (FC, T), BF16)
            zb = [AR(45056 + 2048 * i, (T,), F32) for i in range(2)]
            xsst = AR(45056, (FC, T), BF16)
            V(lambda e: e.memset(yy[:, :, T - 1:T], 0.0))
            for t in range(NT):
                ts_ = slice(t * T, (t + 1) * T)
                ln_stats(t, xbst, xsst)
                b.wait("gpsimd", "vv", b.val("vv"))
                bcast_row(rw(R_M), bc0[:])
                tb = bcast_row(rw(R_R), bc1[:])
                b.wait("vector", "dG", tb)
                for ch in range(FC):
                    V(lambda e, ch=ch, t_=ts_: e.tensor_tensor(
                        f0[:], xres[:, ch, t_], bc0[:], Alu.subtract), None)
                    V(lambda e, ch=ch: e.tensor_tensor(ub[:, ch, :], f0[:], bc1[:],
                                                       Alu.mult))
                vv_u = b.val("vv")
                zmark = []
                for ch in range(FC):
                    buf = ch % 2
                    if ch >= 2:
                        b.wait("sync", "mm", zmark[ch - 2][2])
                    dma("sync", wz_[buf][:],
                        wz_b.rearrange("(kc p) o -> p kc o", p=128)[:, :, ch * 128:(ch + 1) * 128])
                    t2 = dma("sync", wh_[buf][:],
                             wh_b.rearrange("(kc p) o -> p kc o", p=128)[:, :, ch * 128:(ch + 1) * 128])
                    b.wait("tensor", "dS", t2)
                    b.wait("tensor", "vv", vv_u)
                    if ch >= 2:
                        b.wait("tensor", "aa", zmark[ch - 2][0])
                        b.wait("tensor", "vv", zmark[ch - 2][1])
                    bz_, bh_ = 2 * (ch % 2), 2 * (ch % 2) + 1
                    for kc in range(FC):
                        Tn(lambda e, kc=kc, buf=buf, bk=bz_: e.matmul(
                            PS[:, bk, :], wz_[buf][:, kc, :], ub[:, kc, :],
                            start=(kc == 0), stop=(kc == FC - 1)),
                           "mm" if kc == FC - 1 else None)
                    mz = b.val("mm")
                    for kc in range(FC):
                        Tn(lambda e, kc=kc, buf=buf, bk=bh_: e.matmul(
                            PS[:, bk, :], wh_[buf][:, kc, :], ub[:, kc, :],
                            start=(kc == 0), stop=(kc == FC - 1)),
                           "mm" if kc == FC - 1 else None)
                    mh = b.val("mm")
                    b.wait("scalar", "mm", mz)
                    A(lambda e, ch=ch, bk=bz_: e.activation(
                        zb[ch % 2][0:128, :], PS[:, bk, :], Act.Sigmoid,
                        bias=bzc[:, ch:ch + 1], scale=onecol[:]))
                    aa_z = b.val("aa")
                    b.wait("vector", "aa", aa_z)
                    b.wait("vector", "mm", mh)
                    V(lambda e, ch=ch, bk=bh_: e.scalar_tensor_tensor(
                        f0[:], PS[:, bk, :], bhc[:, ch:ch + 1], zb[ch % 2][0:128, :],
                        Alu.add, Alu.mult), None)
                    V(lambda e, ch=ch: e.tensor_scalar(
                        f1[:], zb[ch % 2][0:128, :], -1.0, 1.0, Alu.mult, Alu.add), None)
                    V(lambda e, ch=ch: e.tensor_tensor_scan(
                        yy[:, ch, :], f1[:], f0[:], yy[:, ch, T - 1:T],
                        Alu.mult, Alu.add))
                    vv_z = b.val("vv")
                    b.wait("scalar", "vv", vv_z)
                    A(lambda e, ch=ch: e.copy(hbb[:, ch, :], yy[:, ch, :]))
                    zmark.append((aa_z, vv_z, mh))
                    if dbg and t == 0 and ch == 0:
                        b.wait("gpsimd", "vv", vv_z)
                        dma("gpsimd", dbgf[:, 0, :], zb[0][0:128, :])
                        dma("gpsimd", dbgf[:, 1, :], f0[:])
                        dma("gpsimd", dbgf[:, 2, :], f1[:])
                        dma("gpsimd", dbgf[:, 3, :], yy[:, 0, :])
                        td = dma("gpsimd", dbgb[:, 0, :], ub[:, 0, :])
                        b.wait("vector", "dG", td)
                aa_h = b.val("aa")
                gmark = []
                for ch in range(FC):
                    buf = ch % 2
                    if ch >= 2:
                        b.wait("sync", "mm", gmark[ch - 2][1])
                    t1 = dma("sync", wg2[buf][:],
                             wg_b.rearrange("(kc p) o -> p kc o", p=128)[:, :, ch * 128:(ch + 1) * 128])
                    b.wait("tensor", "dS", t1)
                    b.wait("tensor", "aa", aa_h)
                    if ch >= 4:
                        b.wait("tensor", "aa", gmark[ch - 4][0])
                    bk = ch % 4
                    for kc in range(FC):
                        Tn(lambda e, kc=kc, buf=buf, bk=bk: e.matmul(
                            PS[:, bk, :], wg2[buf][:, kc, :], hbb[:, kc, :],
                            start=(kc == 0), stop=(kc == FC - 1)),
                           "mm" if kc == FC - 1 else None)
                    mg_ = b.val("mm")
                    b.wait("scalar", "mm", mg_)
                    A(lambda e, ch=ch, bk=bk: e.activation(
                        f2[:], PS[:, bk, :], Act.Sigmoid, bias=bgc[:, ch:ch + 1],
                        scale=onecol[:]))
                    aa_g = b.val("aa")
                    b.wait("vector", "aa", aa_g)
                    V(lambda e, ch=ch: e.tensor_tensor(f0[:], f2[:], yy[:, ch, :],
                                                       Alu.mult), None)
                    V(lambda e, ch=ch, t_=ts_: e.tensor_tensor(
                        xres[:, ch, t_], xres[:, ch, t_], f0[:], Alu.add))
                    gmark.append((aa_g, mg_))
                    if dbg and t == 0 and ch == 0:
                        b.wait("gpsimd", "vv", b.val("vv"))
                        td = dma("gpsimd", dbgf[:, 4, :], f2[:])
                        b.wait("scalar", "dG", td)

        if _DO('l1a'):
            l1_attn()
        barrier()
        if dbg:
            for c4 in range(4):
                dma("gpsimd",
                    x3T.rearrange("(c p) t -> p c t", p=128)[:, 2 * c4:2 * c4 + 2, :],
                    xres[:, 2 * c4:2 * c4 + 2, :])

        def allreduce_max(idx_lo, pair):
            b.wait("gpsimd", "vv", b.val("vv"))
            t = dma("gpsimd", cc_i[pair], cells[0:1, idx_lo:idx_lo + 2])
            b.wait("gpsimd", "dG", t)
            b.put("gpsimd", lambda e, pr=pair: e.collective_compute(
                "AllReduce", Alu.max, replica_groups=[list(range(n_cores))],
                ins=[cc_i[pr].opt()], outs=[cc_o[pr].opt()]), "gg")
            b.wait("gpsimd", "gg", b.val("gg"))
            t = dma("gpsimd", cells[0:1, idx_lo:idx_lo + 2], cc_o[pair])
            b.wait("vector", "dG", t)
            b.wait("scalar", "dG", t)

        def l1_ffn(upto="E"):
            xq = AR(0, (4, FC, T), BF16)
            xbst = AR(32768, (FC, T), BF16)
            xsst = AR(40960, (FC, T), BF16)
            wb_ = [AR(32768, (FC, 512), BF16), AR(40960, (FC, 512), BF16)]
            wdd = AR(0, (24, D), BF16)
            oib = [AR(49152 + 1024 * i, (T,), BF16) for i in range(2)]

            V(lambda e: e.memset(cols2[:, 32:33], 0.0))
            for t in range(NT):
                ts_ = slice(t * T, (t + 1) * T)
                ln_stats(t, xbst, xsst)
                b.wait("gpsimd", "vv", b.val("vv"))
                bcast_row(rw(R_M), bc0[:])
                tb = bcast_row(rw(R_R), bc1[:])
                b.wait("vector", "dG", tb)
                for ch in range(FC):
                    V(lambda e, ch=ch, t_=ts_: e.tensor_tensor(
                        f0[:], xres[:, ch, t_], bc0[:], Alu.subtract), None)
                    V(lambda e: e.tensor_tensor(f0[:], f0[:], bc1[:], Alu.mult), None)
                    V(lambda e: e.tensor_reduce(f2[:, 0:1], f0[:], AX, Alu.max,
                                                apply_absolute_value=True))
                    VW()
                    V(lambda e: e.tensor_max(cols2[:, 32:33], cols2[:, 32:33],
                                             f2[:, 0:1]))
            fold128_to_cell(cols2[:, 32:33], C_G1, Alu.max)
            V(lambda e: e.tensor_scalar(cells[0:1, C_G1 + 1:C_G1 + 2],
                                        cells[0:1, C_G1:C_G1 + 1], 1.0, None, Alu.mult))
            allreduce_max(C_G1, 0)
            V(lambda e: e.tensor_scalar(cells[0:1, C_S1:C_S1 + 1],
                                        cells[0:1, C_G1:C_G1 + 1], 1.0 / 32767.0, 1e-12,
                                        Alu.mult, Alu.max))
            VW()
            V(lambda e: e.reciprocal(cells[0:1, C_K1:C_K1 + 1],
                                     cells[0:1, C_S1:C_S1 + 1]))
            VW()
            if upto == "A":
                return

            V(lambda e: e.memset(cols2[:, 33:35], 0.0))
            for grp in range(2):
                tiles = list(range(grp * 4, grp * 4 + 4))
                b.wait("vector", "mm", b.val("mm"))
                for ti, t in enumerate(tiles):
                    ts_ = slice(t * T, (t + 1) * T)
                    ln_stats(t, xbst, xsst)
                    V(lambda e: e.tensor_scalar(rw(R_C), rw(R_R),
                                                cells[0:1, C_K1:C_K1 + 1], None, Alu.mult))
                    b.wait("gpsimd", "vv", b.val("vv"))
                    bcast_row(rw(R_M), bc0[:])
                    tb = bcast_row(rw(R_C), bc1[:])
                    b.wait("vector", "dG", tb)
                    for ch in range(FC):
                        V(lambda e, ch=ch, t_=ts_: e.tensor_tensor(
                            f0[:], xres[:, ch, t_], bc0[:], Alu.subtract), None)
                        V(lambda e: e.tensor_tensor(f0[:], f0[:], bc1[:], Alu.mult), None)
                        V(lambda e: e.tensor_scalar(f0[:], f0[:], MG, MG + 32767.0,
                                                    Alu.add, Alu.min), None)
                        V(lambda e, ti=ti, ch=ch: e.tensor_scalar(
                            xq[:, ti, ch, :], f0[:], -MG, -32768.0, Alu.add, Alu.max))
                vv_xq = b.val("vv")
                b.wait("sync", "vv", vv_xq)
                if upto == "B0":
                    continue
                nsrc = 1 if upto.startswith("B1") else 2
                ep = []
                it = 0
                wst = {"n": 0, "rel": []}
                for wsrc, spill, mcol in ((l1u_b, Ubuf, 33), (l1g_b, Gbuf, 34))[:nsrc]:
                    for og in range(6):
                        buf = wst["n"] % 2
                        if wst["n"] >= 2:
                            b.wait("sync", "mm", wst["rel"][wst["n"] - 2])
                        tw = None
                        for kc in range(FC):
                            for j in range(4):
                                tw = dma_tr("sync", wb_[buf][:, kc, j * 128:(j + 1) * 128],
                                            wsrc[og * 512 + j * 128:og * 512 + (j + 1) * 128,
                                                 kc * 128:(kc + 1) * 128])
                        wst["n"] += 1
                        if upto == "B1L":
                            wst["rel"].append(b.val("mm"))
                            continue
                        b.wait("tensor", "dS", tw)
                        b.wait("tensor", "vv", vv_xq)
                        for ti, t in enumerate(tiles):
                            for j in range(4):
                                bk = it % 4
                                if it >= 4 and upto != "B1M":
                                    b.wait("tensor", "aa", ep[it - 4][0])
                                    b.wait("tensor", "vv", ep[it - 4][1])
                                for kc in range(FC):
                                    Tn(lambda e, kc=kc, ti=ti, j=j, buf=buf, bk=bk: e.matmul(
                                        PS[:, bk, :], wb_[buf][:, kc, j * 128:(j + 1) * 128],
                                        xq[:, ti, kc, :], start=(kc == 0),
                                        stop=(kc == FC - 1)),
                                       "mm" if kc == FC - 1 else None)
                                mk = b.val("mm")
                                if upto == "B1M":
                                    it += 1
                                    continue
                                b.wait("vector", "mm", mk)
                                V(lambda e, bk=bk: e.tensor_reduce(
                                    f2[:, 0:1], PS[:, bk, :], AX, Alu.max,
                                    apply_absolute_value=True))
                                VW()
                                V(lambda e, mc=mcol: e.tensor_max(
                                    cols2[:, mc:mc + 1], cols2[:, mc:mc + 1], f2[:, 0:1]))
                                vv_ep = b.val("vv")
                                b.wait("scalar", "mm", mk)
                                b.wait("scalar", "vv", vv_ep)
                                b.wait("scalar", "dA", b.val("dA"))
                                A(lambda e, bk=bk: e.copy(sp0[:], PS[:, bk, :]))
                                aa_ep = b.val("aa")
                                b.wait("scalar", "aa", aa_ep)
                                dma("scalar",
                                    spill[og * 512 + j * 128:og * 512 + (j + 1) * 128,
                                          t * T:(t + 1) * T], sp0[:])
                                ep.append((aa_ep, vv_ep))
                                it += 1
                        wst["rel"].append(b.val("mm"))
            if upto == "B0" or upto.startswith("B1"):
                return
            fold128_to_cell(cols2[:, 33:34], C_G2, Alu.max)
            fold128_to_cell(cols2[:, 34:35], C_G3, Alu.max)
            V(lambda e: e.tensor_tensor(cells[0:1, C_G2:C_G2 + 1],
                                        cells[0:1, C_G2:C_G2 + 1],
                                        cells[0:1, C_S1:C_S1 + 1], Alu.mult), None)
            V(lambda e: e.tensor_tensor(cells[0:1, C_G3:C_G3 + 1],
                                        cells[0:1, C_G3:C_G3 + 1],
                                        cells[0:1, C_S1:C_S1 + 1], Alu.mult))
            allreduce_max(C_G2, 1)
            for (gi, si, ki) in ((C_G2, C_S2, C_K2), (C_G3, C_S3, C_K3)):
                V(lambda e, gi=gi, si=si: e.tensor_scalar(
                    cells[0:1, si:si + 1], cells[0:1, gi:gi + 1], 1.0 / 32767.0, 1e-12,
                    Alu.mult, Alu.max))
                VW()
                V(lambda e, si=si, ki=ki: e.reciprocal(cells[0:1, ki:ki + 1],
                                                       cells[0:1, si:si + 1]))
                VW()
                V(lambda e, ki=ki: e.tensor_tensor(
                    cells[0:1, ki:ki + 1], cells[0:1, ki:ki + 1],
                    cells[0:1, C_S1:C_S1 + 1], Alu.mult))
                VW()
            V(lambda e: e.tensor_scalar(cells[0:1, C_NS3MG:C_NS3MG + 1],
                                        cells[0:1, C_S3:C_S3 + 1], -MG, None, Alu.mult))
            b.wait("gpsimd", "vv", b.val("vv"))
            cell_to_col(C_K2, ccol[:, 3:4])
            cell_to_col(C_K3, ccol[:, 4:5])
            cell_to_col(C_S3, ccol[:, 7:8])
            tb = cell_to_col(C_NS3MG, ccol[:, 6:7])
            b.wait("scalar", "dG", tb)
            b.wait("vector", "dG", tb)
            if upto == "B":
                return

            b.wait("sync", "dA", b.val("dA"))
            b.wait("gpsimd", "dA", b.val("dA"))
            V(lambda e: e.memset(cols2[:, 35:36], 0.0))
            for oc in range(24):
                for t in range(NT):
                    r = (oc * NT + t) % 2
                    sp = (sp0, sp1)[r]
                    b.wait("sync", "aa", b.val("aa"))
                    tl = dma("sync", sp[:], Gbuf[oc * 128:(oc + 1) * 128,
                                                 t * T:(t + 1) * T])
                    b.wait("scalar", "dS", tl)
                    b.wait("scalar", "vv", b.val("vv"))
                    A(lambda e, sp=sp: e.activation(f2[:], sp[:], Act.Identity,
                                                    bias=mgcol[:], scale=ccol[:, 4:5]), None)
                    A(lambda e: e.activation(bc0[:], f2[:], Act.Sigmoid,
                                             bias=ccol[:, 6:7], scale=ccol[:, 7:8]))
                    b.wait("vector", "aa", b.val("aa"))
                    V(lambda e: e.tensor_scalar(f2[:], f2[:], -MG, 32767.0,
                                                Alu.add, Alu.min), None)
                    V(lambda e: e.tensor_scalar(bc1[:], bc0[:], 65536.0, MG,
                                                Alu.mult, Alu.add), None)
                    V(lambda e: e.tensor_scalar(bc1[:], bc1[:], MG + 65535.0, -MG,
                                                Alu.min, Alu.add), None)
                    V(lambda e: e.tensor_tensor(f0[:], bc1[:], f2[:], Alu.mult), None)
                    V(lambda e: e.tensor_reduce(f2[:, 0:1], f0[:], AX, Alu.max,
                                                apply_absolute_value=True))
                    VW()
                    V(lambda e: e.tensor_max(cols2[:, 35:36], cols2[:, 35:36],
                                             f2[:, 0:1]))
                    b.wait("scalar", "vv", b.val("vv"))
                    b.wait("scalar", "dA", b.val("dA"))
                    A(lambda e, sp=sp: e.copy(sp[:], f0[:]))
                    b.wait("scalar", "aa", b.val("aa"))
                    dma("scalar", GSbuf[oc * 128:(oc + 1) * 128, t * T:(t + 1) * T], sp[:])
            fold128_to_cell(cols2[:, 35:36], C_G4, Alu.max)
            V(lambda e: e.tensor_scalar(cells[0:1, C_T0:C_T0 + 1],
                                        cells[0:1, C_S3:C_S3 + 1], 1.0 / 65536.0, None,
                                        Alu.mult))
            VW()
            V(lambda e: e.tensor_tensor(cells[0:1, C_G4:C_G4 + 1],
                                        cells[0:1, C_G4:C_G4 + 1],
                                        cells[0:1, C_T0:C_T0 + 1], Alu.mult))
            VW()
            V(lambda e: e.tensor_scalar(cells[0:1, C_G4 + 1:C_G4 + 2],
                                        cells[0:1, C_G4:C_G4 + 1], 1.0, None, Alu.mult))
            allreduce_max(C_G4, 2)
            V(lambda e: e.tensor_scalar(cells[0:1, C_S4:C_S4 + 1],
                                        cells[0:1, C_G4:C_G4 + 1], 1.0 / 32767.0, 1e-12,
                                        Alu.mult, Alu.max))
            VW()
            V(lambda e: e.reciprocal(cells[0:1, C_K4:C_K4 + 1],
                                     cells[0:1, C_S4:C_S4 + 1]))
            VW()
            V(lambda e: e.tensor_tensor(cells[0:1, C_K4:C_K4 + 1],
                                        cells[0:1, C_K4:C_K4 + 1],
                                        cells[0:1, C_T0:C_T0 + 1], Alu.mult))
            b.wait("gpsimd", "vv", b.val("vv"))
            tb = cell_to_col(C_K4, ccol[:, 5:6])
            b.wait("scalar", "dG", tb)
            if upto == "C":
                return

            b.wait("sync", "dA", b.val("dA"))
            b.wait("gpsimd", "dA", b.val("dA"))
            V(lambda e: e.memset(cols2[:, 35:36], 0.0))
            for oc in range(24):
                for t in range(NT):
                    r = (oc * NT + t) % 2
                    spg, spu = (sp0, sp1) if r == 0 else (sp1, sp0)
                    b.wait("sync", "aa", b.val("aa"))
                    dma("sync", spg[:], GSbuf[oc * 128:(oc + 1) * 128, t * T:(t + 1) * T])
                    t2 = dma("sync", spu[:], Ubuf[oc * 128:(oc + 1) * 128,
                                                  t * T:(t + 1) * T])
                    b.wait("scalar", "dS", t2)
                    b.wait("scalar", "vv", b.val("vv"))
                    A(lambda e, spg=spg: e.activation(f0[:], spg[:], Act.Identity,
                                                      bias=mgcol[:], scale=ccol[:, 5:6]), None)
                    A(lambda e, spu=spu: e.activation(f1[:], spu[:], Act.Identity,
                                                      bias=mgcol[:], scale=ccol[:, 3:4]))
                    b.wait("vector", "aa", b.val("aa"))
                    V(lambda e: e.tensor_scalar(f0[:], f0[:], -MG, 32767.0,
                                                Alu.add, Alu.min), None)
                    V(lambda e: e.tensor_scalar(f1[:], f1[:], -MG, 32767.0,
                                                Alu.add, Alu.min), None)
                    V(lambda e: e.tensor_tensor(f2[:], f0[:], f1[:], Alu.mult), None)
                    V(lambda e: e.tensor_reduce(f0[:, 0:1], f2[:], AX, Alu.max,
                                                apply_absolute_value=True))
                    VW()
                    V(lambda e: e.tensor_max(cols2[:, 35:36], cols2[:, 35:36],
                                             f0[:, 0:1]))
                    b.wait("scalar", "vv", b.val("vv"))
                    b.wait("scalar", "dA", b.val("dA"))
                    A(lambda e, spg=spg: e.copy(spg[:], f2[:]))
                    b.wait("scalar", "aa", b.val("aa"))
                    dma("scalar", Pbuf[oc * 128:(oc + 1) * 128, t * T:(t + 1) * T], spg[:])
            fold128_to_cell(cols2[:, 35:36], C_G5, Alu.max)
            V(lambda e: e.tensor_tensor(cells[0:1, C_T1:C_T1 + 1],
                                        cells[0:1, C_S4:C_S4 + 1],
                                        cells[0:1, C_S2:C_S2 + 1], Alu.mult))
            VW()
            V(lambda e: e.tensor_tensor(cells[0:1, C_G5:C_G5 + 1],
                                        cells[0:1, C_G5:C_G5 + 1],
                                        cells[0:1, C_T1:C_T1 + 1], Alu.mult))
            VW()
            V(lambda e: e.tensor_scalar(cells[0:1, C_G5 + 1:C_G5 + 2],
                                        cells[0:1, C_G5:C_G5 + 1], 1.0, None, Alu.mult))
            allreduce_max(C_G5, 3)
            V(lambda e: e.tensor_scalar(cells[0:1, C_S5:C_S5 + 1],
                                        cells[0:1, C_G5:C_G5 + 1], 1.0 / 32767.0, 1e-12,
                                        Alu.mult, Alu.max))
            VW()
            V(lambda e: e.reciprocal(cells[0:1, C_K5:C_K5 + 1],
                                     cells[0:1, C_S5:C_S5 + 1]))
            VW()
            V(lambda e: e.tensor_tensor(cells[0:1, C_K5:C_K5 + 1],
                                        cells[0:1, C_K5:C_K5 + 1],
                                        cells[0:1, C_T1:C_T1 + 1], Alu.mult))
            b.wait("gpsimd", "vv", b.val("vv"))
            cell_to_col(C_K5, ccol[:, 3:4])
            tb = cell_to_col(C_S5, ccol[:, 4:5])
            b.wait("scalar", "dG", tb)
            b.wait("vector", "dG", tb)
            if upto == "D":
                return

            b.wait("sync", "dA", b.val("dA"))
            b.wait("gpsimd", "dA", b.val("dA"))
            b.wait("sync", "mm", b.val("mm"))
            tw = None
            for ki in range(24):
                for od in range(FC):
                    tw = dma_tr("sync", wdd[:, ki, od * 128:(od + 1) * 128],
                                l1d_b[od * 128:(od + 1) * 128, ki * 128:(ki + 1) * 128])
            b.wait("tensor", "dS", tw)
            kimark = []
            for t in range(NT):
                ts_ = slice(t * T, (t + 1) * T)
                for ki in range(24):
                    r = ki % 2
                    sp = (sp0, sp1)[r]
                    b.wait("sync", "aa", b.val("aa"))
                    tl = dma("sync", sp[:], Pbuf[ki * 128:(ki + 1) * 128,
                                                 t * T:(t + 1) * T])
                    b.wait("scalar", "dS", tl)
                    b.wait("scalar", "vv", b.val("vv"))
                    A(lambda e, sp=sp: e.activation(f0[:], sp[:], Act.Identity,
                                                    bias=mgcol[:], scale=ccol[:, 3:4]))
                    b.wait("vector", "aa", b.val("aa"))
                    if len(kimark) >= 2:
                        b.wait("vector", "mm", kimark[-2])
                    V(lambda e, r=r: e.tensor_scalar(oib[r][0:128, :], f0[:], -MG,
                                                     32767.0, Alu.add, Alu.min))
                    b.wait("tensor", "vv", b.val("vv"))
                    for od in range(FC):
                        Tn(lambda e, ki=ki, od=od, r=r: e.matmul(
                            PS[:, od, :], wdd[:, ki, od * 128:(od + 1) * 128],
                            oib[r][0:128, :], start=(ki == 0), stop=(ki == 23)),
                           "mm" if od == FC - 1 else None)
                    kimark.append(b.val("mm"))
                    if 'sere' in phases:
                        for e_ in ("tensor", "sync", "scalar", "vector", "gpsimd"):
                            b.wait(e_, "mm", b.val("mm"))
                            b.wait(e_, "vv", b.val("vv"))
                            b.wait(e_, "aa", b.val("aa"))
                mk = b.val("mm")
                b.wait("vector", "mm", mk)
                for od in range(FC):
                    V(lambda e, od=od, t_=ts_: e.scalar_tensor_tensor(
                        xres[:, od, t_], PS[:, od, :], ccol[:, 4:5], xres[:, od, t_],
                        Alu.mult, Alu.add))

        if _DO('l1f'):
            l1_ffn()
        else:
            for u in ("A", "B0", "B1L", "B1M", "B1", "B", "C", "D"):
                if _DO('l1f' + u):
                    l1_ffn(upto=u)
                    break
        barrier()

        if dbg:
            dma("gpsimd", dbgf[0:1, 3, 0:64], cells[0:1, 0:64])
            dma("gpsimd", dbgb[:, 0, :], Ubuf[0:128, 1536:2048])
            dma("gpsimd", dbgb[:, 1, :], Gbuf[0:128, 1536:2048])
            dma("gpsimd", dbgb[:, 2, :], GSbuf[0:128, 1536:2048])
            for oc5 in range(5):
                tst = dma("gpsimd", dbgb[:, 3 + oc5, :],
                          Pbuf[oc5 * 128:(oc5 + 1) * 128, 1536:2048])
        for c4 in range(4):
            tst = dma("gpsimd", yT.rearrange("(c p) t -> p c t", p=128)[:, 2 * c4:2 * c4 + 2, :],
                      xres[:, 2 * c4:2 * c4 + 2, :])
        b.wait("gpsimd", "dG", tst)

        blk = es.enter_context(nc.Block())
        for eng in ENGS:
            def _mk(eng):
                def _run(q):
                    for f in b.ops[eng]:
                        f(q)
                return _run
            getattr(blk, eng)(_mk(eng))

    return nc


_NC_CACHE = {}


def kernel(L0_attn_x, L0_attn_w_qkva, L0_attn_gamma, L0_ffn_w_gate, L0_ffn_w_up,
           L0_ffn_w_down, L1_attn_Wz, L1_attn_bz, L1_attn_Wh, L1_attn_bh,
           L1_attn_Wg, L1_attn_bg, L1_ffn_w_gate, L1_ffn_w_up, L1_ffn_w_down):
    n = 8
    if "nc" not in _NC_CACHE:
        _NC_CACHE["nc"] = build_program(n)
    nc = _NC_CACHE["nc"]
    asc = np.ascontiguousarray
    x = np.asarray(L0_attn_x, dtype=np.float32)
    common = dict(
        qkvaT=asc(np.asarray(L0_attn_w_qkva, np.float32).T),
        gamma=asc(np.asarray(L0_attn_gamma, np.float32).reshape(1, D)),
        l0g=asc(np.asarray(L0_ffn_w_gate, np.float32).T),
        l0u=asc(np.asarray(L0_ffn_w_up, np.float32).T),
        l0d=asc(np.asarray(L0_ffn_w_down, np.float32).T),
        wzT=asc(np.asarray(L1_attn_Wz, np.float32).transpose(1, 0, 2).reshape(D, D)),
        bz=asc(np.asarray(L1_attn_bz, np.float32).reshape(1, D)),
        whT=asc(np.asarray(L1_attn_Wh, np.float32).transpose(1, 0, 2).reshape(D, D)),
        bh=asc(np.asarray(L1_attn_bh, np.float32).reshape(1, D)),
        wgn=asc(np.asarray(L1_attn_Wg, np.float32)),
        bg=asc(np.asarray(L1_attn_bg, np.float32).reshape(1, D)),
        l1g=asc(np.asarray(L1_ffn_w_gate, np.float32)),
        l1u=asc(np.asarray(L1_ffn_w_up, np.float32)),
        l1d=asc(np.asarray(L1_ffn_w_down, np.float32)),
    )
    in_maps = []
    for c in range(n):
        m = dict(common)
        m["xT"] = asc(x[c].T)
        in_maps.append(m)
    res = run_bass_kernel_spmd(nc, in_maps, core_ids=list(range(n)))
    out = np.empty((n, TOK, D), dtype=np.float32)
    for c in range(n):
        out[c] = np.asarray(res.results[c]["yT"]).T
    return out
